# version 02
# baseline (speedup 1.0000x reference)
import numpy as np
import concourse.bass as bass
import concourse.mybir as mybir
from concourse.bass_utils import run_bass_kernel_spmd

F32 = mybir.dt.float32
BF16 = mybir.dt.bfloat16
FP8 = mybir.dt.float8e4
AF = mybir.ActivationFunctionType
OP = mybir.AluOpType
AX = mybir.AxisListType

D = 1024
G = 8
I_H = 4096
MI = 32
C8 = 2.0 ** -8
EPS = 1e-5


class Eng:
    def __init__(self, sem):
        self.ops = []
        self.sem = sem
        self.count = 0

    def emit(self, fn, inc=0):
        sem, n = self.sem, inc

        def op(e):
            ins = fn(e)
            if n:
                ins.then_inc(sem, n)
        self.ops.append(op)
        if inc:
            self.count += inc
        return self.count

    def wait(self, other, value=None):
        sem = other.sem
        v = other.count if value is None else value
        if v > 0:
            self.ops.append(lambda e, s=sem, vv=v: e.wait_ge(s, vv))


def build_program(t=2048, debug_out=None):
    tt = min(512, t)
    ntt = t // tt
    nlt = tt // 128
    ch = min(512, t)
    nch = t // ch
    hw_ = min(512, t)
    nh = t // hw_

    nc = bass.Bass("TRN2", target_bir_lowering=False, detect_race_conditions=False)

    x_in = nc.dram_tensor("x", [t, D], F32, kind="ExternalInput")
    y_out = nc.dram_tensor("y", [t, D], F32, kind="ExternalOutput")
    W = {}
    for nm in ("Wf1", "Wf2", "Wbx", "Wgx", "Wga", "Wb2"):
        W[nm] = nc.dram_tensor(nm, [D, D], F32, kind="ExternalInput")
    gate_w = nc.dram_tensor("gate_w", [I_H, D], F32, kind="ExternalInput")
    up_w = nc.dram_tensor("up_w", [I_H, D], F32, kind="ExternalInput")
    down_w = nc.dram_tensor("down_w", [D, I_H], F32, kind="ExternalInput")
    vecs8 = nc.dram_tensor("vecs8", [64, 128], F32, kind="ExternalInput")
    scal2 = nc.dram_tensor("scal2", [2, 1], F32, kind="ExternalInput")
    onehot = nc.dram_tensor("onehot", [128, 4], F32, kind="ExternalInput")

    qg_s = nc.dram_tensor("qg_s", [MI, G, 128, 128], BF16)
    qu_s = nc.dram_tensor("qu_s", [MI, G, 128, 128], BF16)
    qd_s = nc.dram_tensor("qd_s", [MI, G, 128, 128], BF16)
    si_s = nc.dram_tensor("si_s", [G, 128, t], F32)
    a_s = nc.dram_tensor("a_s", [G, 128, t], F32)
    cc_i = nc.dram_tensor("cc_i", [128, 24], F32)
    cc_o = nc.dram_tensor("cc_o", [4, 128, 24], F32)

    from contextlib import ExitStack
    stack = ExitStack()
    with stack:
        sbuf = lambda n_, s, d=F32: stack.enter_context(nc.sbuf_tensor(n_, s, d))

        BUFA = sbuf("BUFA", [128, t * G])
        X = sbuf("X", [128, t * G])
        BUFH = sbuf("BUFH", [128, 8192])
        WS = sbuf("WS", [128, 2048])
        WQ = sbuf("WQ", [128, 4096], BF16)
        QXK = sbuf("QXK", [128, G * 512], BF16)
        QHD = sbuf("QHD", [128, 1536], BF16)
        QTOK = sbuf("QTOK", [128, 1024], BF16)
        VQ8 = sbuf("VQ8", [128, 1024], FP8)
        AMB = sbuf("AMB", [128, 256])
        SCH = sbuf("SCH", [128, 512])
        AMH = sbuf("AMH", [128, 512])
        HSQ = sbuf("HSQ", [128, 512])
        V8T = sbuf("V8T", [128, 512], FP8)
        RRB = sbuf("RRB", [128, 512])
        S8B = sbuf("S8B", [128, 512])
        SP4 = sbuf("SP4", [128, 1024])
        IDF = sbuf("IDF", [128, 128])
        IDB = sbuf("IDB", [128, 128], BF16)
        ZRO = sbuf("ZRO", [128, 16])
        GAPB = sbuf("GAPB", [128, 8])
        STATS = sbuf("STATS", [128, 64])
        CONS = sbuf("CONS", [128, 64])
        SUM = sbuf("SUM", [128, 64])
        SCL = sbuf("SCL", [128, 32])
        EXC = sbuf("EXC", [128, 96])
        GWS = sbuf("GWS", [128, 16])

        PB = [stack.enter_context(nc.psum_tensor(f"PB{i}", [128, 512], F32))
              for i in range(8)]
        PTR = PB[6]
        PTB = PB[7].ap().bitcast(BF16)

        sems = {k: stack.enter_context(nc.semaphore("s_" + k))
                for k in ("sy", "gp", "gpc", "v", "s", "pe", "cc")}
        SY, GP, V, S, PE = (Eng(sems[k]) for k in ("sy", "gp", "v", "s", "pe"))
        GPC = Eng(sems["gpc"])
        GPC.ops = GP.ops
        CCSEM = sems["cc"]

        FQ0 = BUFH[:, 4096:5120]
        FQ1 = BUFH[:, 5120:6144]
        FQ2 = BUFH[:, 6144:7168]
        SQS = BUFH[:, 7168:8192]
        QXF = QXK.ap().bitcast(F32)
        HQ0, HQ1, HQ2 = SP4[:, 0:512], SP4[:, 512:1024], HSQ

        def dma(eng, out, in_):
            return eng.emit(lambda e, o=out, i=in_: e.dma_start(out=o, in_=i),
                            inc=16)

        _vw = []

        def _rngs(aps):
            out = []
            for ap in aps:
                if ap is None or isinstance(ap, (int, float)):
                    continue
                try:
                    nm = ap.tensor.name
                    off = int(ap.offset)
                    sh = list(ap.shape)
                    try:
                        stv = list(ap.stride)
                    except Exception:
                        stv = None
                    if stv and len(stv) == len(sh) and len(sh) > 1:
                        span = 1 + sum((int(d) - 1) * abs(int(s))
                                       for d, s in zip(sh[1:], stv[1:]))
                    else:
                        span = 1
                        for d in sh[1:]:
                            span *= int(d)
                    out.append((nm, off, off + max(span, 1)))
                except Exception:
                    out.append(("?", 0, 1 << 62))
            return out

        def _vsmall(aps):
            try:
                return all(int(ap.free_size) <= 128 for ap in aps)
            except Exception:
                return True

        def _vraw(outs, ins):
            rd = _rngs(ins)
            need = 0
            for age, (small, wr) in enumerate(_vw):
                if not small:
                    continue
                for nm, lo, hi in wr:
                    for nm2, lo2, hi2 in rd:
                        if ((nm == nm2 or nm == "?" or nm2 == "?")
                                and lo < hi2 and lo2 < hi):
                            need = max(need, 2 - age)
            for _ in range(need):
                V.emit(lambda e: e.tensor_copy(GAPB[:, 0:8], ZRO[:, 0:8]),
                       inc=0)
                _vw.insert(0, (False, []))
            _vw.insert(0, (_vsmall(outs), _rngs(outs)))
            del _vw[2:]

        def tsc(eng, out, in0, s1, s2=None, op0=OP.mult, op1=None):
            if eng is V:
                _vraw([out], [in0, s1, s2])
            if op1 is None:
                return eng.emit(lambda e, o=out, i=in0, a=s1, p0=op0:
                                e.tensor_scalar(o, i, a, None, p0), inc=1)
            return eng.emit(lambda e, o=out, i=in0, a=s1, b=s2, p0=op0, p1=op1:
                            e.tensor_scalar(o, i, a, b, p0, p1), inc=1)

        def tt_(eng, out, a, b, op):
            if eng is V:
                _vraw([out], [a, b])
            return eng.emit(lambda e, o=out, x=a, y=b, p=op:
                            e.tensor_tensor(o, x, y, p), inc=1)

        def stt(eng, out, in0, sc, in1, op0, op1):
            if eng is V:
                _vraw([out], [in0, sc, in1])
            return eng.emit(lambda e, o=out, i0=in0, s=sc, i1=in1, p0=op0, p1=op1:
                            e.scalar_tensor_tensor(o, i0, s, i1, p0, p1), inc=1)

        def act(eng, out, in_, fn, bias=0.0, scale=1.0, accum=None):
            return eng.emit(lambda e, o=out, i=in_, f=fn, b=bias, s=scale, a=accum:
                            e.activation(o, i, f, bias=b, scale=s, accum_out=a),
                            inc=1)

        def cp(eng, out, in_):
            if eng is V:
                _vraw([out], [in_])
            return eng.emit(lambda e, o=out, i=in_: e.tensor_copy(o, i), inc=1)

        def recip(eng, out, in_):
            if eng is V:
                _vraw([out], [in_])
                _vw[0] = (True, _vw[0][1])
            return eng.emit(lambda e, o=out, i=in_: e.reciprocal(o, i), inc=1)

        def red(eng, out, in_, op, absval=False):
            if eng is V:
                _vraw([out], [in_])
            return eng.emit(lambda e, o=out, i=in_, p=op, a=absval:
                            e.tensor_reduce(o, i, AX.X, p, apply_absolute_value=a),
                            inc=1)

        def mm(out, lhsT, rhs, start, stop, inc=0):
            return PE.emit(lambda e, o=out, l=lhsT, r=rhs, a=start, b=stop:
                           e.matmul(o, l, r, start=a, stop=b), inc=inc)

        def tr(out, in_, ident):
            return PE.emit(lambda e, o=out, i=in_, d=ident:
                           e.transpose(o, i, d), inc=1)

        def scan(eng, out, d0, d1, init):
            if eng is V:
                _vraw([out], [d0, d1, init])
            return eng.emit(lambda e, o=out, a=d0, b=d1, i=init:
                            e.tensor_tensor_scan(o, a, b, i, OP.mult, OP.add),
                            inc=1)

        dma(SY, SP4[:64, :128], vecs8.ap())
        dma(SY, SCL[:, 0:1], scal2[0:1, :].broadcast_to([128, 1]))
        dma(SY, SCL[:, 1:2], scal2[1:2, :].broadcast_to([128, 1]))
        dma(SY, SCL[:, 2:6], onehot.ap())
        GPC.emit(lambda e: e.iota(IDF[:, :], [[1, 128]], channel_multiplier=-1,
                                  allow_small_or_imprecise_dtypes=True), inc=1)
        GPC.emit(lambda e: e.memset(ZRO[:, :], 0.0), inc=1)
        GPC.emit(lambda e: e.memset(SP4[:, 512:640], 1.0), inc=1)
        V.wait(GPC)
        tsc(V, SCL[:, 31:32], ZRO[:, 0:1], 1.0, None, OP.add)
        ONECOL = SCL[:, 31:32]
        tsc(V, IDB[:, :], IDF[:, :], 0.0, None, OP.is_equal)
        tsc(V, IDF[:, :], IDF[:, :], 0.0, None, OP.is_equal)
        PE.wait(SY)
        PE.wait(V)
        tr(PTR[:, :64], SP4[:64, :128], IDF[:64, :64])
        S.wait(PE)
        act(S, CONS[:, :], PTR[:128, :64], AF.Identity)
        V.wait(S)
        AFWD = CONS[:, 0:8]
        S.wait(V)
        act(S, SUM[:, 0:8], CONS[:, 8:16], AF.Exp)
        V.wait(S)
        tsc(V, SUM[:, 0:8], SUM[:, 0:8], 1.0, None, OP.add)
        S.wait(V)
        act(S, SUM[:, 0:8], SUM[:, 0:8], AF.Ln)
        act(S, SUM[:, 16:24], CONS[:, 0:8], AF.Ln)
        V.wait(S)
        tsc(V, SUM[:, 8:16], SUM[:, 0:8], -8.0)
        tsc(V, SCL[:, 30:31], ZRO[:, 0:1], EPS, None, OP.add)
        M8SP = SUM[:, 8:16]
        LNAF = SUM[:, 16:24]
        cp(V, SUM[:, 24:32], CONS[:, 0:8])
        for _ in range(int(round(np.log2(t)))):
            tt_(V, SUM[:, 24:32], SUM[:, 24:32], SUM[:, 24:32], OP.mult)
        ACORE = SUM[:, 24:32]

        def gate_rows_ap(src, it):
            b0 = 2048 * (it // 16) + (it % 16)
            return bass.AP(src, b0 * D, [[16 * D, 128], [1, D]])

        def global_absmax(src_ap_fn, ntile, npiece, out_col):
            tt_(V, GWS[:, out_col:out_col + 1], ZRO[:, 0:1], ZRO[:, 0:1], OP.add)
            for it in range(ntile):
                for piece in range(npiece):
                    j = (it * npiece + piece) % 2
                    dst = WS[:, j * 1024:(j + 1) * 1024]
                    SY.wait(V)
                    dma(SY, dst, src_ap_fn(it, piece))
                    V.wait(SY)
                    red(V, GWS[:, out_col + 1:out_col + 2], dst, OP.max,
                        absval=True)
                    tt_(V, GWS[:, out_col:out_col + 1],
                        GWS[:, out_col:out_col + 1],
                        GWS[:, out_col + 1:out_col + 2], OP.max)
            PE.wait(V)
            PE.wait(S)
            tr(PTR[:1, 0:128], GWS[:, out_col:out_col + 1], IDF[:, :])
            S.wait(PE)
            act(S, AMB[:1, 128:256], PTR[:1, 0:128], AF.Identity)
            V.wait(S)
            red(V, GWS[:1, out_col:out_col + 1], AMB[:1, 128:256], OP.max)
            PE.wait(V)
            mm(PTR[:, 128:129], SP4[:1, 512:640], GWS[:1, out_col:out_col + 1],
               True, True, inc=1)
            V.wait(PE)
            cp(V, GWS[:, out_col:out_col + 1], PTR[:, 128:129])

        global_absmax(lambda it, p: gate_rows_ap(gate_w, it), MI, 1, 0)
        global_absmax(lambda it, p: gate_rows_ap(up_w, it), MI, 1, 4)
        global_absmax(lambda it, p: down_w.ap()[it * 128:(it + 1) * 128,
                                                p * 1024:(p + 1) * 1024],
                      G, 4, 8)
        for col in (0, 4, 8):
            recip(V, GWS[:, col + 2:col + 3], GWS[:, col:col + 1])
        cp(V, SP4[:, 0:128], IDF[:, :])
        cp(V, SP4[:, 128:256], IDF[:, :])
        for j, col in enumerate((0, 4, 8)):
            tsc(V, SCL[:, 6 + j:7 + j], GWS[:, col + 2:col + 3], 2688.0)
        tt_(V, SCL[:, 16:17], SCL[:, 0:1], SCL[:, 6:7], OP.mult)
        tt_(V, SCL[:, 17:18], SCL[:, 0:1], SCL[:, 7:8], OP.mult)
        tt_(V, SCL[:, 18:19], SCL[:, 1:2], SCL[:, 8:9], OP.mult)
        for j in range(3):
            recip(V, SCL[:, 19 + j:20 + j], SCL[:, 16 + j:17 + j])
        cp(V, SP4[:, 0:128], IDF[:, :])
        cp(V, SP4[:, 128:256], IDF[:, :])
        cp(V, SCL[:, 9:10], SCL[:, 19:20])
        cp(V, SCL[:, 10:11], SCL[:, 20:21])
        tsc(V, SCL[:, 11:12], SCL[:, 21:22], 0.5)
        tsc(V, SCL[:, 12:13], SCL[:, 1:2], 1.0 / 12.0)
        tsc(V, SCL[:, 13:14], SCL[:, 1:2], 0.5)
        tsc(V, SCL[:, 24:25], SCL[:, 6:7], 1.0 / 12.0)
        tsc(V, SCL[:, 26:27], SCL[:, 6:7], 0.5)
        tsc(V, SCL[:, 25:26], SCL[:, 7:8], 1.0 / 12.0)
        tsc(V, SCL[:, 27:28], SCL[:, 7:8], 0.5)
        tsc(V, SCL[:, 28:29], SCL[:, 8:9], 1.0 / 12.0)
        tsc(V, SCL[:, 29:30], SCL[:, 8:9], 0.5)
        GX = SCL[:, 0:1]
        INVG1, INVG2, HINVD = SCL[:, 9:10], SCL[:, 10:11], SCL[:, 11:12]
        GD12, GD2 = SCL[:, 12:13], SCL[:, 13:14]

        def fp4(src, width, rr, s8, t0, t1, t2, outq, blk16, vq):
            nb = width // 16
            if blk16:
                tt_(V, t0[:, :width].rearrange("p (b k) -> p b k", k=16),
                    src.rearrange("p (b k) -> p b k", k=16),
                    rr.unsqueeze(2).broadcast_to([128, nb, 16]), OP.mult)
            else:
                tt_(V, t0[:, :width], src, rr, OP.mult)
            tsc(V, t0[:, :width], t0[:, :width], 6.5, -6.5, OP.min, OP.max)
            tsc(V, t1[:, :width], t0[:, :width], 2.0, 0.0, OP.subtract, OP.max)
            tsc(V, t2[:, :width], t0[:, :width], -2.0, 0.0, OP.subtract, OP.min)
            tt_(V, t1[:, :width], t1[:, :width], t2[:, :width], OP.add)
            tsc(V, t2[:, :width], t0[:, :width], 4.0, 0.0, OP.subtract, OP.max)
            stt(V, t2[:, :width], t2[:, :width], -0.25, t0[:, :width],
                OP.mult, OP.add)
            tsc(V, t0[:, :width], t0[:, :width], -4.0, 0.0, OP.subtract, OP.min)
            stt(V, t0[:, :width], t0[:, :width], -0.25, t2[:, :width],
                OP.mult, OP.add)
            stt(V, t0[:, :width], t1[:, :width], -0.5, t0[:, :width],
                OP.mult, OP.add)
            tsc(V, vq[:, :width], t0[:, :width], C8)
            tsc(V, t1[:, :width], vq[:, :width], 2 * C8, 0.0, OP.subtract, OP.max)
            tsc(V, t2[:, :width], vq[:, :width], -2 * C8, 0.0, OP.subtract, OP.min)
            tt_(V, t1[:, :width], t1[:, :width], t2[:, :width], OP.add)
            tsc(V, t2[:, :width], vq[:, :width], 3 * C8, 0.0, OP.subtract, OP.max)
            stt(V, t2[:, :width], t2[:, :width], 2.0, t1[:, :width],
                OP.mult, OP.add)
            tsc(V, t1[:, :width], vq[:, :width], -3 * C8, 0.0, OP.subtract,
                OP.min)
            stt(V, t1[:, :width], t1[:, :width], 2.0, t2[:, :width],
                OP.mult, OP.add)
            tt_(V, t1[:, :width], vq[:, :width], t1[:, :width], OP.add)
            if blk16:
                tt_(V, outq.rearrange("p (b k) -> p b k", k=16),
                    t1[:, :width].rearrange("p (b k) -> p b k", k=16),
                    s8.unsqueeze(2).broadcast_to([128, nb, 16]), OP.mult)
            else:
                tt_(V, outq, t1[:, :width], s8, OP.mult)

        def sc_stage(amax_ap, n, mul_ap, half_ap, rr_out, s8_out, v8t):
            tt_(V, rr_out[:, :n], amax_ap, mul_ap.broadcast_to([128, n]), OP.mult)
            tsc(V, rr_out[:, :n], rr_out[:, :n], 1.0, 224.0, OP.mult, OP.min)
            cp(V, v8t[:, :n], rr_out[:, :n])
            recip(V, rr_out[:, :n], v8t[:, :n])
            tsc(V, s8_out[:, :n], v8t[:, :n], 512.0)
            cp(V, STATS[:, 32:64], SUM[:, 0:32])
            tsc(V, rr_out[:, :n], rr_out[:, :n], 1e30, None, OP.min)
            tt_(V, rr_out[:, :n], rr_out[:, :n],
                half_ap.broadcast_to([128, n]), OP.mult)

        def wprep_gateup(src, dst, which):
            mul_ap = SCL[:, 24 + which:25 + which]
            half_ap = SCL[:, 26 + which:27 + which]
            for pair in range(MI // 2):
                its = (2 * pair, 2 * pair + 1)
                for j, it in enumerate(its):
                    SY.wait(V)
                    SY.wait(PE)
                    dma(SY, WS[:, j * 1024:(j + 1) * 1024], gate_rows_ap(src, it))
                V.wait(SY)
                for j in range(2):
                    red(V, AMB[:, j * 64:(j + 1) * 64],
                        WS[:, j * 1024:(j + 1) * 1024]
                        .rearrange("p (b k) -> p b k", k=16), OP.max, absval=True)
                sc_stage(AMB[:, :128], 128, mul_ap, half_ap, RRB, S8B, V8T)
                for j, it in enumerate(its):
                    fp4(WS[:, j * 1024:(j + 1) * 1024], 1024,
                        RRB[:, j * 64:(j + 1) * 64], S8B[:, j * 64:(j + 1) * 64],
                        FQ0, FQ1, FQ2, QTOK[:, :1024], True, VQ8)
                    for g in range(G):
                        slot = g % 8
                        PE.wait(V)
                        PE.wait(S)
                        tr(PTB[:, slot * 128:slot * 128 + 128],
                           QTOK[:, g * 128:(g + 1) * 128], IDB[:, :])
                        S.wait(PE)
                        act(S, WQ[:, slot * 128:slot * 128 + 128],
                            PTB[:, slot * 128:slot * 128 + 128], AF.Identity)
                        GP.wait(S)
                        dma(GP, dst.ap()[it, g, :, :],
                            WQ[:, slot * 128:slot * 128 + 128])

        wprep_gateup(gate_w, qg_s, 0)
        wprep_gateup(up_w, qu_s, 1)

        def wprep_down():
            for it in range(G):
                for half in range(2):
                    SY.wait(V)
                    SY.wait(PE)
                    dma(SY, WS[:, :2048],
                        down_w.ap()[it * 128:(it + 1) * 128,
                                    half * 2048:(half + 1) * 2048])
                    V.wait(SY)
                    red(V, AMB[:, 0:128],
                        WS[:, :2048].rearrange("p (b k) -> p b k", k=16),
                        OP.max, absval=True)
                    sc_stage(AMB[:, :128], 128, SCL[:, 28:29], SCL[:, 29:30],
                             RRB, S8B, V8T)
                    for j in range(2):
                        fp4(WS[:, j * 1024:(j + 1) * 1024], 1024,
                            RRB[:, j * 64:(j + 1) * 64],
                            S8B[:, j * 64:(j + 1) * 64],
                            FQ0, FQ1, FQ2, QTOK[:, :1024], True, VQ8)
                        cp(V, WQ[:, (2 + j) * 1024:(3 + j) * 1024],
                           QTOK[:, :1024])
                    for kk in range(16):
                        tau = half * 16 + kk
                        slot = kk % 8
                        PE.wait(V)
                        PE.wait(S)
                        tr(PTB[:, slot * 128:slot * 128 + 128],
                           WQ[:, 2048 + kk:4096:16], IDB[:, :])
                        S.wait(PE)
                        act(S, WQ[:, slot * 128:slot * 128 + 128],
                            PTB[:, slot * 128:slot * 128 + 128], AF.Identity)
                        GP.wait(S)
                        dma(GP, qd_s.ap()[tau, it, :, :],
                            WQ[:, slot * 128:slot * 128 + 128])

        wprep_down()
        GP.ops.append(lambda e, s=GP.sem, v=GP.count: e.wait_ge(s, v))

        def ln_tile(xtok, c, gmul=None):
            red(V, STATS[:, c:c + 1], xtok, OP.add)
            S.wait(V)
            act(S, SQS, xtok, AF.Square, accum=STATS[:, 8 + c:9 + c])
            V.wait(S)
            tsc(V, STATS[:, 16 + c:17 + c], STATS[:, c:c + 1], 1.0 / D)
            tsc(V, STATS[:, 20 + c:21 + c], STATS[:, 8 + c:9 + c], 1.0 / D)
            tt_(V, STATS[:, 24 + c:25 + c], STATS[:, 16 + c:17 + c],
                STATS[:, 16 + c:17 + c], OP.mult)
            tt_(V, STATS[:, 20 + c:21 + c], STATS[:, 20 + c:21 + c],
                STATS[:, 24 + c:25 + c], OP.subtract)
            S.wait(V)
            act(S, STATS[:, 24 + c:25 + c], STATS[:, 20 + c:21 + c], AF.Sqrt,
                bias=SCL[:, 30:31])
            V.wait(S)
            recip(V, STATS[:, 28 + c:29 + c], STATS[:, 24 + c:25 + c])
            cp(V, STATS[:, 32:48], SUM[:, 0:16])
            cp(V, STATS[:, 48:64], SUM[:, 16:32])
            if gmul is not None:
                tt_(V, STATS[:, 28 + c:29 + c], STATS[:, 28 + c:29 + c],
                    gmul, OP.mult)
            tt_(V, STATS[:, 24 + c:25 + c], STATS[:, 16 + c:17 + c],
                STATS[:, 28 + c:29 + c], OP.mult)
            tsc(V, STATS[:, 24 + c:25 + c], STATS[:, 24 + c:25 + c], -1.0)
            return STATS[:, 28 + c:29 + c], STATS[:, 24 + c:25 + c]

        def transpose_fm_to_tok(tok0, dst):
            for g in range(G):
                sl = g % 4
                if sl == 0:
                    PE.wait(S)
                    PE.wait(V)
                tr(PTR[:, sl * 128:sl * 128 + 128],
                   X[:, g * t + tok0:g * t + tok0 + 128], IDF[:, :])
                if sl == 3:
                    S.wait(PE)
                    act(S, dst[:, (g - 3) * 128:(g + 1) * 128], PTR[:, :512],
                        AF.Identity)

        def build_x():
            for lt in range(t // 128):
                tok0 = lt * 128
                SY.wait(V)
                SY.wait(PE)
                dma(SY, SP4[:, :], x_in.ap()[tok0:tok0 + 128, :])
                PE.wait(SY)
                for g in range(G):
                    sl = g % 4
                    if sl == 0:
                        PE.wait(S)
                    tr(PTR[:, sl * 128:sl * 128 + 128],
                       SP4[:, g * 128:(g + 1) * 128], IDF[:, :])
                    if sl == 3:
                        S.wait(PE)
                        for gg in range(g - 3, g + 1):
                            act(S, X[:, gg * t + tok0:gg * t + tok0 + 128],
                                PTR[:, (gg % 4) * 128:(gg % 4) * 128 + 128],
                                AF.Identity)

        def ffn(sub):
            for it in range(ntt):
                for lt in range(nlt):
                    tok0 = it * tt + lt * 128
                    if sub == 1:
                        SY.wait(V)
                        SY.wait(S)
                        dma(SY, SP4[:, :], x_in.ap()[tok0:tok0 + 128, :])
                        V.wait(SY)
                        S.wait(SY)
                    else:
                        transpose_fm_to_tok(tok0, SP4)
                        V.wait(S)
                    scale, bias = ln_tile(SP4[:, :], lt, gmul=GX)
                    S.wait(V)
                    act(S, BUFH[:, lt * 1024:(lt + 1) * 1024], SP4[:, :],
                        AF.Identity, bias=bias, scale=scale)
                    V.wait(S)
                    red(V, AMB[:, lt * 64:(lt + 1) * 64],
                        BUFH[:, lt * 1024:(lt + 1) * 1024]
                        .rearrange("p (b k) -> p b k", k=16), OP.max, absval=True)
                nbt = nlt * 64
                tsc(V, SCH[:, :nbt], AMB[:, :nbt], 1.0 / 12.0, 224.0,
                    OP.mult, OP.min)
                cp(V, V8T[:, :nbt], SCH[:, :nbt])
                recip(V, SCH[:, :nbt], V8T[:, :nbt])
                tsc(V, S8B[:, :nbt], V8T[:, :nbt], 512.0)
                cp(V, STATS[:, 32:64], SUM[:, 0:32])
                tsc(V, RRB[:, :nbt], SCH[:, :nbt], 1e30, 0.5, OP.min, OP.mult)
                for lt in range(nlt):
                    fp4(BUFH[:, lt * 1024:(lt + 1) * 1024], 1024,
                        RRB[:, lt * 64:(lt + 1) * 64],
                        S8B[:, lt * 64:(lt + 1) * 64],
                        FQ0, FQ1, FQ2, QTOK[:, :1024], True, VQ8)
                    for g in range(G):
                        slot = g % 8
                        PE.wait(V)
                        PE.wait(S)
                        tr(PTB[:, slot * 128:slot * 128 + 128],
                           QTOK[:, g * 128:(g + 1) * 128], IDB[:, :])
                        S.wait(PE)
                        act(S, QXK[:, g * tt + lt * 128:g * tt + lt * 128 + 128],
                            PTB[:, slot * 128:slot * 128 + 128], AF.Identity)
                for hh in range(2):
                    for m16 in range(16):
                        m = hh * 16 + m16
                        bg, bu = PB[2 * (m % 2)], PB[2 * (m % 2) + 1]
                        wql = WQ[:, (m % 2) * 2048:(m % 2) * 2048 + 1024]
                        wqu = WQ[:, (m % 2) * 2048 + 1024:(m % 2) * 2048 + 2048]
                        GP.wait(PE)
                        dma(GP, wql.rearrange("p (g c) -> p g c", g=G),
                            qg_s.ap()[m, :, :, :].rearrange("g p c -> p g c"))
                        dma(GP, wqu.rearrange("p (g c) -> p g c", g=G),
                            qu_s.ap()[m, :, :, :].rearrange("g p c -> p g c"))
                        PE.wait(GP)
                        PE.wait(V)
                        PE.wait(S)
                        for k in range(G):
                            mm(bg[:, :tt], wql[:, k * 128:(k + 1) * 128],
                               QXK[:, k * tt:k * tt + tt], k == 0, k == G - 1,
                               inc=(1 if k == G - 1 else 0))
                        for k in range(G):
                            mm(bu[:, :tt], wqu[:, k * 128:(k + 1) * 128],
                               QXK[:, k * tt:k * tt + tt], k == 0, k == G - 1,
                               inc=(1 if k == G - 1 else 0))
                        S.wait(PE)
                        sgt = SP4[:, :512] if m % 2 == 0 else SP4[:, 512:1024]
                        act(S, sgt[:, :tt], bg[:, :tt], AF.Silu, scale=INVG1)
                        V.wait(S)
                        hdst = BUFH[:, m16 * 512:m16 * 512 + tt]
                        stt(V, hdst, bu[:, :tt], INVG2, sgt[:, :tt],
                            OP.mult, OP.mult)
                        S.wait(V)
                        act(S, HSQ[:, :tt], hdst, AF.Square)
                        V.wait(S)
                        if m16 == 0:
                            cp(V, AMH[:, :tt], HSQ[:, :tt])
                        else:
                            tt_(V, AMH[:, :tt], AMH[:, :tt], HSQ[:, :tt], OP.max)
                    S.wait(V)
                    act(S, AMH[:, :tt], AMH[:, :tt], AF.Sqrt)
                    V.wait(S)
                    sc_stage(AMH[:, :tt], tt, GD12, GD2, SCH, S8B, V8T)
                    for m16 in range(16):
                        k = hh * 16 + m16
                        qslot = QHD[:, (m16 % 3) * 512:(m16 % 3) * 512 + tt]
                        if m16 >= 2:
                            V.wait(PE)
                        fp4(BUFH[:, m16 * 512:m16 * 512 + tt], tt,
                            SCH[:, :tt], S8B[:, :tt],
                            HQ0, HQ1, HQ2, qslot, False, VQ8)
                        wqd = WQ[:, (m16 % 2) * 1024:(m16 % 2) * 1024 + 1024]
                        GP.wait(PE)
                        dma(GP, wqd.rearrange("p (g c) -> p g c", g=G),
                            qd_s.ap()[k, :, :, :].rearrange("g p c -> p g c"))
                        PE.wait(GP)
                        PE.wait(V)
                        for m in range(G):
                            mm(PB[m][:, :tt], wqd[:, m * 128:(m + 1) * 128],
                               qslot, m16 == 0, m16 == 15,
                               inc=(1 if m == G - 1 else 0))
                    V.wait(PE)
                    for m in range(G):
                        xsl = X[:, m * t + it * tt:m * t + it * tt + tt]
                        stt(V, xsl, PB[m][:, :tt], HINVD, xsl, OP.mult, OP.add)

        def attn():
            LGA = QXF[:, 0:512]
            PEX = QXF[:, 512:1024]
            WEX = QXF[:, 1024:1536]
            TM0 = QXF[:, 1536:2048]
            SIT = QTOK.ap().bitcast(F32)
            AT = AMH
            XG = BUFH[:, 0:G * 512]
            VC = BUFH[:, G * 512:2 * G * 512]
            U = BUFA

            GPC.wait(V)
            GPC.wait(S)
            GPC.emit(lambda e: e.iota(RRB[:, :], [[1, 512]],
                                      channel_multiplier=0,
                                      allow_small_or_imprecise_dtypes=True),
                     inc=1)
            S.wait(GPC)

            cp(V, SUM[:, 32:48], ZRO[:, 0:16])
            for c in range(nch):
                ct0 = c * ch
                for lt in range(ch // 128):
                    tok0 = ct0 + lt * 128
                    transpose_fm_to_tok(tok0, SP4)
                    V.wait(S)
                    scale, bias = ln_tile(SP4[:, :], lt % 4)
                    S.wait(V)
                    act(S, SQS, SP4[:, :], AF.Identity, bias=bias, scale=scale)
                    PE.wait(S)
                    for g in range(G):
                        sl = g % 4
                        if sl == 0:
                            PE.wait(S)
                            PE.wait(V)
                        tr(PTR[:, sl * 128:sl * 128 + 128],
                           SQS[:, g * 128:(g + 1) * 128], IDF[:, :])
                        if sl == 3:
                            q0 = (g // 4) * 512
                            S.wait(PE)
                            act(S, SP4[:, q0:q0 + 512], PTR[:, :512],
                                AF.Identity)
                            V.wait(S)
                            for gg in range(g - 3, g + 1):
                                pseg = SP4[:, q0 + (gg % 4) * 128:
                                            q0 + (gg % 4) * 128 + 128]
                                act(S, XG[:, gg * ch + lt * 128:
                                          gg * ch + lt * 128 + 128],
                                    pseg, AF.Gelu_apprx_tanh)
                                xsl = X[:, gg * t + tok0:gg * t + tok0 + 128]
                                tt_(V, xsl, xsl, pseg, OP.add)
                for wname, bcol, isU in (("Wf1", 2, True), ("Wbx", 4, False)):
                    for k in range(G):
                        SY.wait(PE)
                        dma(SY, WS[:, (k % 2) * 1024:(k % 2) * 1024 + 1024],
                            W[wname].ap()[k * 128:(k + 1) * 128, :])
                        PE.wait(SY)
                        if k == 0:
                            PE.wait(S)
                            PE.wait(V)
                        for m in range(G):
                            mm(PB[m][:, :ch],
                               WS[:, (k % 2) * 1024 + m * 128:
                                  (k % 2) * 1024 + m * 128 + 128],
                               XG[:, k * ch:(k + 1) * ch], k == 0, k == G - 1,
                               inc=(1 if m == G - 1 else 0))
                    S.wait(PE)
                    for m in range(G):
                        if isU:
                            dst = U[:, m * t + ct0:m * t + ct0 + ch]
                        else:
                            dst = VC[:, m * ch:(m + 1) * ch]
                        act(S, dst, PB[m][:, :ch], AF.Identity,
                            bias=CONS[:, bcol * 8 + m:bcol * 8 + m + 1])
                for g in range(G):
                    for wname, bcol, sdst in (("Wgx", 5, SP4[:, 0:512]),
                                              ("Wga", 6, SP4[:, 512:1024])):
                        for k in range(G):
                            SY.wait(PE)
                            dma(SY, WS[:, (k % 2) * 128:(k % 2) * 128 + 128],
                                W[wname].ap()[k * 128:(k + 1) * 128,
                                              g * 128:(g + 1) * 128])
                            PE.wait(SY)
                            if k == 0:
                                PE.wait(S)
                            mm(PB[4][:, :ch],
                               WS[:, (k % 2) * 128:(k % 2) * 128 + 128],
                               VC[:, k * ch:(k + 1) * ch], k == 0, k == G - 1,
                               inc=1)
                        S.wait(PE)
                        act(S, sdst[:, :ch], PB[4][:, :ch], AF.Sigmoid,
                            bias=CONS[:, bcol * 8 + g:bcol * 8 + g + 1])
                    V.wait(S)
                    tsc(V, LGA[:, :ch], SP4[:, 512:512 + ch], M8SP[:, g:g + 1])
                    S.wait(V)
                    act(S, AT[:, :ch], LGA[:, :ch], AF.Exp)
                    act(S, WEX[:, :ch], LGA[:, :ch], AF.Exp, scale=2.0)
                    GP.wait(S)
                    dma(GP, a_s.ap()[g, :, ct0:ct0 + ch], AT[:, :ch])
                    V.wait(S)
                    tsc(V, WEX[:, :ch], WEX[:, :ch], -1.0, 1.0, OP.mult, OP.add)
                    tsc(V, WEX[:, :ch], WEX[:, :ch], 0.0, None, OP.max)
                    S.wait(V)
                    act(S, TM0[:, :ch], WEX[:, :ch], AF.Sqrt)
                    V.wait(S)
                    tt_(V, SIT[:, :ch], SP4[:, 0:ch], VC[:, g * ch:g * ch + ch],
                        OP.mult)
                    tt_(V, SIT[:, :ch], SIT[:, :ch], TM0[:, :ch], OP.mult)
                    GP.wait(V)
                    dma(GP, si_s.ap()[g, :, ct0:ct0 + ch], SIT[:, :ch])
                    scan(V, PEX[:, :ch], ONECOL.broadcast_to([128, ch]),
                         LGA[:, :ch], 0.0)
                    tt_(V, PEX[:, :ch], PEX[:, :ch], LGA[:, :ch], OP.subtract)
                    S.wait(V)
                    act(S, WEX[:, :ch], PEX[:, :ch], AF.Exp,
                        bias=SUM[:, 32 + g:33 + g])
                    V.wait(S)
                    tt_(V, TM0[:, :ch], WEX[:, :ch], SIT[:, :ch], OP.mult)
                    red(V, STATS[:, 0:1], TM0[:, :ch], OP.add)
                    tt_(V, SUM[:, 40 + g:41 + g], SUM[:, 40 + g:41 + g],
                        STATS[:, 0:1], OP.add)
                    red(V, STATS[:, 1:2], LGA[:, :ch], OP.add)
                    tt_(V, SUM[:, 32 + g:33 + g], SUM[:, 32 + g:33 + g],
                        STATS[:, 1:2], OP.add)
            for g in range(G):
                for h2 in range(nh):
                    tsc(V, STATS[:, 2:3], LNAF[:, g:g + 1],
                        float(t - 1 - hw_ * h2))
                    tsc(V, STATS[:, 3:4], LNAF[:, g:g + 1], -1.0)
                    S.wait(V)
                    act(S, WEX[:, :hw_], RRB[:, :hw_], AF.Exp,
                        bias=STATS[:, 2:3], scale=STATS[:, 3:4])
                    V.wait(S)
                    tt_(V, TM0[:, :hw_],
                        U[:, g * t + h2 * hw_:g * t + h2 * hw_ + hw_],
                        WEX[:, :hw_], OP.mult)
                    red(V, STATS[:, 4 + h2:5 + h2], TM0[:, :hw_], OP.add)
                for h2 in range(1, nh):
                    tt_(V, STATS[:, 4:5], STATS[:, 4:5],
                        STATS[:, 4 + h2:5 + h2], OP.add)
                cp(V, SUM[:, 48 + g:49 + g], STATS[:, 4:5])
            S.wait(V)
            act(S, SUM[:, 56:64], SUM[:, 32:40], AF.Exp)
            V.wait(S)
            cp(V, EXC[:, 0:8], SUM[:, 48:56])
            cp(V, EXC[:, 8:16], SUM[:, 40:48])
            cp(V, EXC[:, 16:24], SUM[:, 56:64])
            GP.wait(V)
            dma(GP, cc_i.ap(), EXC[:, 0:24])
            GP.ops.append(lambda e, s=GP.sem, v=GP.count: e.wait_ge(s, v))

            def _cc(e):
                e.collective_compute(
                    "AllGather", OP.bypass,
                    replica_groups=[[0, 1, 2, 3], [4, 5, 6, 7]],
                    ins=[cc_i.ap().opt()], outs=[cc_o.ap().opt()]
                ).then_inc(CCSEM, 1)
            GP.ops.append(_cc)
            GP.ops.append(lambda e: e.wait_ge(CCSEM, 1))
            dma(GP, EXC[:, 0:96].rearrange("p (q c) -> p q c", q=4),
                cc_o.ap().rearrange("q p c -> p q c"))
            V.wait(GP)
            cp(V, STATS[:, 0:8], ZRO[:, 0:8])
            cp(V, STATS[:, 8:16], ZRO[:, 0:8])
            cp(V, STATS[:, 16:24], ZRO[:, 0:8])
            for qq in range(4):
                stt(V, STATS[:, 8:16], STATS[:, 0:8], SCL[:, 2 + qq:3 + qq],
                    STATS[:, 8:16], OP.mult, OP.add)
                tt_(V, STATS[:, 0:8], STATS[:, 0:8], ACORE, OP.mult)
                tt_(V, STATS[:, 0:8], STATS[:, 0:8],
                    EXC[:, qq * 24:qq * 24 + 8], OP.add)
            cp(V, STATS[:, 0:8], ZRO[:, 0:8])
            for qq in range(3, -1, -1):
                stt(V, STATS[:, 16:24], STATS[:, 0:8], SCL[:, 2 + qq:3 + qq],
                    STATS[:, 16:24], OP.mult, OP.add)
                tt_(V, STATS[:, 0:8], STATS[:, 0:8],
                    EXC[:, qq * 24 + 16:qq * 24 + 24], OP.mult)
                tt_(V, STATS[:, 0:8], STATS[:, 0:8],
                    EXC[:, qq * 24 + 8:qq * 24 + 16], OP.add)
            for g in range(G):
                cp(V, BUFH[:, 0:t], AFWD[:, g:g + 1].broadcast_to([128, t]))
                scan(V, U[:, g * t:(g + 1) * t], BUFH[:, 0:t],
                     U[:, g * t:(g + 1) * t], STATS[:, 8 + g:9 + g])
            for it in range(ntt):
                for k in range(G):
                    SY.wait(PE)
                    dma(SY, WS[:, (k % 2) * 1024:(k % 2) * 1024 + 1024],
                        W["Wf2"].ap()[k * 128:(k + 1) * 128, :])
                    PE.wait(SY)
                    if k == 0:
                        PE.wait(V)
                    for m in range(G):
                        mm(PB[m][:, :tt],
                           WS[:, (k % 2) * 1024 + m * 128:
                              (k % 2) * 1024 + m * 128 + 128],
                           U[:, k * t + it * tt:k * t + it * tt + tt],
                           k == 0, k == G - 1,
                           inc=(1 if m == G - 1 else 0))
                V.wait(PE)
                for m in range(G):
                    xsl = X[:, m * t + it * tt:m * t + it * tt + tt]
                    stt(V, xsl, PB[m][:, :tt], CONS[:, 3 * 8 + m:3 * 8 + m + 1],
                        xsl, OP.add, OP.add)
            SY.wait(GP)
            for g in range(G):
                SY.wait(V)
                dma(SY, BUFH[:, 0:t], si_s.ap()[g, :, :])
                dma(SY, BUFH[:, t:2 * t], a_s.ap()[g, :, :])
                V.wait(SY)
                scan(V, U[:, g * t:(g + 1) * t],
                     BUFH[:, 2 * t - 1:t - 1:-1] if t > 1 else BUFH[:, t:2 * t],
                     BUFH[:, t - 1::-1], STATS[:, 16 + g:17 + g])
            for it in range(ntt):
                for k in range(G):
                    SY.wait(PE)
                    dma(SY, WS[:, (k % 2) * 1024:(k % 2) * 1024 + 1024],
                        W["Wb2"].ap()[k * 128:(k + 1) * 128, :])
                    PE.wait(SY)
                    if k == 0:
                        PE.wait(V)
                    for m in range(G):
                        mm(PB[m][:, :tt],
                           WS[:, (k % 2) * 1024 + m * 128:
                              (k % 2) * 1024 + m * 128 + 128],
                           U[:, k * t + (t - (it + 1) * tt):
                              k * t + (t - it * tt)],
                           k == 0, k == G - 1,
                           inc=(1 if m == G - 1 else 0))
                V.wait(PE)
                for m in range(G):
                    xsl = X[:, m * t + it * tt:m * t + it * tt + tt]
                    stt(V, xsl, PB[m][:, tt - 1::-1],
                        CONS[:, 7 * 8 + m:7 * 8 + m + 1], xsl, OP.add, OP.add)

        def output():
            for lt in range(t // 128):
                tok0 = lt * 128
                transpose_fm_to_tok(tok0, SP4)
                V.wait(S)
                cp(V, BUFA[:, lt * 1024:(lt + 1) * 1024], SP4[:, :])
            SY.wait(V)
            dma(SY, y_out.ap().rearrange("(n p) c -> p n c", p=128),
                BUFA[:, 0:(t // 128) * 1024].rearrange("p (n c) -> p n c", c=D))
            SY.ops.append(lambda e, s=SY.sem, v=SY.count: e.wait_ge(s, v))

        if debug_out is not None:
            debug_out.update(locals())

        build_x()
        ffn(1)
        attn()
        ffn(3)
        output()

        with nc.Block() as block:
            block.sync(lambda e: [op(e) for op in SY.ops])
            block.gpsimd(lambda e: [op(e) for op in GP.ops])
            block.vector(lambda e: [op(e) for op in V.ops])
            block.scalar(lambda e: [op(e) for op in S.ops])
            block.tensor(lambda e: [op(e) for op in PE.ops])

    return nc


_PROG_CACHE = {}


def _get_prog(t):
    if t not in _PROG_CACHE:
        _PROG_CACHE[t] = build_program(t)
    return _PROG_CACHE[t]


def make_inmaps(x_full, wd, t=2048, ncore=8):
    vec_order = ("a_fwd", "a_param", "bf1", "bf2", "bbx", "bgx", "bga", "bb2")
    vecs8 = np.concatenate([np.asarray(wd[k], np.float32).reshape(8, 128)
                            for k in vec_order], axis=0)
    scal2 = np.stack([np.asarray(wd["s_in"], np.float32).reshape(1),
                      np.asarray(wd["s_in_down"], np.float32).reshape(1)])
    base = {
        "Wf1": np.asarray(wd["Wf1"], np.float32),
        "Wf2": np.asarray(wd["Wf2"], np.float32),
        "Wbx": np.asarray(wd["Wbx"], np.float32),
        "Wgx": np.asarray(wd["Wgx"], np.float32),
        "Wga": np.asarray(wd["Wga"], np.float32),
        "Wb2": np.asarray(wd["Wb2"], np.float32),
        "gate_w": np.asarray(wd["gate_w"], np.float32),
        "up_w": np.asarray(wd["up_w"], np.float32),
        "down_w": np.asarray(wd["down_w"], np.float32),
        "vecs8": vecs8, "scal2": scal2,
    }
    nq = ncore // x_full.shape[0]
    in_maps = []
    for c in range(ncore):
        b, q = divmod(c, nq)
        oh = np.zeros((128, 4), np.float32)
        oh[:, q % 4] = 1.0
        m = dict(base)
        m["x"] = np.ascontiguousarray(x_full[b, q * t:(q + 1) * t], dtype=np.float32)
        m["onehot"] = oh
        in_maps.append(m)
    return in_maps


def kernel(L0_attn_x, L0_attn_a_fwd, L0_attn_Wf1, L0_attn_bf1, L0_attn_Wf2,
           L0_attn_bf2, L0_attn_Wbx, L0_attn_bbx, L0_attn_Wgx, L0_attn_bgx,
           L0_attn_Wga, L0_attn_bga, L0_attn_a_param, L0_attn_Wb2, L0_attn_bb2,
           L0_ffn_gate_w, L0_ffn_up_w, L0_ffn_down_w, L0_ffn_s_in,
           L0_ffn_s_in_down):
    x = np.asarray(L0_attn_x, np.float32)
    wd = {
        "a_fwd": L0_attn_a_fwd, "a_param": L0_attn_a_param,
        "bf1": L0_attn_bf1, "bf2": L0_attn_bf2, "bbx": L0_attn_bbx,
        "bgx": L0_attn_bgx, "bga": L0_attn_bga, "bb2": L0_attn_bb2,
        "Wf1": L0_attn_Wf1, "Wf2": L0_attn_Wf2, "Wbx": L0_attn_Wbx,
        "Wgx": L0_attn_Wgx, "Wga": L0_attn_Wga, "Wb2": L0_attn_Wb2,
        "gate_w": L0_ffn_gate_w, "up_w": L0_ffn_up_w, "down_w": L0_ffn_down_w,
        "s_in": L0_ffn_s_in, "s_in_down": L0_ffn_s_in_down,
    }
    t = 2048
    nc = _get_prog(t)
    in_maps = make_inmaps(x, wd, t=t, ncore=8)
    res = run_bass_kernel_spmd(nc, in_maps, core_ids=list(range(8)))
    y = np.empty((2, 8192, 1024), np.float32)
    for c in range(8):
        b, q = divmod(c, 4)
        y[b, q * t:(q + 1) * t] = np.asarray(res.results[c]["y"], np.float32)
    return y


# version 03
# speedup vs baseline: 1.0441x; 1.0441x over previous
import numpy as np
import concourse.bass as bass
import concourse.mybir as mybir
from concourse.bass_utils import run_bass_kernel_spmd

F32 = mybir.dt.float32
BF16 = mybir.dt.bfloat16
FP8 = mybir.dt.float8e4
AF = mybir.ActivationFunctionType
OP = mybir.AluOpType
AX = mybir.AxisListType

D = 1024
G = 8
I_H = 4096
MI = 32
C8 = 2.0 ** -8
EPS = 1e-5


class Eng:
    def __init__(self, sem):
        self.ops = []
        self.sem = sem
        self.count = 0

    def emit(self, fn, inc=0):
        sem, n = self.sem, inc

        def op(e):
            ins = fn(e)
            if n:
                ins.then_inc(sem, n)
        self.ops.append(op)
        if inc:
            self.count += inc
        return self.count

    def wait(self, other, value=None):
        sem = other.sem
        v = other.count if value is None else value
        if v > 0:
            self.ops.append(lambda e, s=sem, vv=v: e.wait_ge(s, vv))


def build_program(t=2048, debug_out=None):
    tt = min(512, t)
    ntt = t // tt
    nlt = tt // 128
    ch = min(512, t)
    nch = t // ch
    hw_ = min(512, t)
    nh = t // hw_

    nc = bass.Bass("TRN2", target_bir_lowering=False, detect_race_conditions=False)

    x_in = nc.dram_tensor("x", [t, D], F32, kind="ExternalInput")
    y_out = nc.dram_tensor("y", [t, D], F32, kind="ExternalOutput")
    W = {}
    for nm in ("Wf1", "Wf2", "Wbx", "Wgx", "Wga", "Wb2"):
        W[nm] = nc.dram_tensor(nm, [D, D], F32, kind="ExternalInput")
    gate_w = nc.dram_tensor("gate_w", [I_H, D], F32, kind="ExternalInput")
    up_w = nc.dram_tensor("up_w", [I_H, D], F32, kind="ExternalInput")
    down_w = nc.dram_tensor("down_w", [D, I_H], F32, kind="ExternalInput")
    vecs8 = nc.dram_tensor("vecs8", [64, 128], F32, kind="ExternalInput")
    scal2 = nc.dram_tensor("scal2", [2, 1], F32, kind="ExternalInput")
    onehot = nc.dram_tensor("onehot", [128, 4], F32, kind="ExternalInput")

    qg_s = nc.dram_tensor("qg_s", [MI, G, 128, 128], BF16)
    qu_s = nc.dram_tensor("qu_s", [MI, G, 128, 128], BF16)
    qd_s = nc.dram_tensor("qd_s", [MI, G, 128, 128], BF16)
    si_s = nc.dram_tensor("si_s", [G, 128, t], F32)
    a_s = nc.dram_tensor("a_s", [G, 128, t], F32)
    cc_i = nc.dram_tensor("cc_i", [128, 24], F32)
    cc_o = nc.dram_tensor("cc_o", [4, 128, 24], F32)

    from contextlib import ExitStack
    stack = ExitStack()
    with stack:
        sbuf = lambda n_, s, d=F32: stack.enter_context(nc.sbuf_tensor(n_, s, d))

        BUFA = sbuf("BUFA", [128, t * G])
        X = sbuf("X", [128, t * G])
        BUFH = sbuf("BUFH", [128, 8192])
        WS = sbuf("WS", [128, 2048])
        WQ = sbuf("WQ", [128, 4096], BF16)
        QXK = sbuf("QXK", [128, G * 512], BF16)
        QHD = sbuf("QHD", [128, 1536], BF16)
        QTOK = sbuf("QTOK", [128, 1024], BF16)
        VQ8 = sbuf("VQ8", [128, 1024], FP8)
        AMB = sbuf("AMB", [128, 256])
        SCH = sbuf("SCH", [128, 512])
        AMH = sbuf("AMH", [128, 512])
        HSQ = sbuf("HSQ", [128, 512])
        V8T = sbuf("V8T", [128, 512], FP8)
        RRB = sbuf("RRB", [128, 512])
        S8B = sbuf("S8B", [128, 512])
        SP4 = sbuf("SP4", [128, 1024])
        IDF = sbuf("IDF", [128, 128])
        IDB = sbuf("IDB", [128, 128], BF16)
        ZRO = sbuf("ZRO", [128, 16])
        GAPB = sbuf("GAPB", [128, 8])
        STATS = sbuf("STATS", [128, 64])
        CONS = sbuf("CONS", [128, 64])
        SUM = sbuf("SUM", [128, 64])
        SCL = sbuf("SCL", [128, 32])
        EXC = sbuf("EXC", [128, 96])
        GWS = sbuf("GWS", [128, 16])

        PB = [stack.enter_context(nc.psum_tensor(f"PB{i}", [128, 512], F32))
              for i in range(8)]
        PTR = PB[6]
        PTB = PB[7].ap().bitcast(BF16)

        sems = {k: stack.enter_context(nc.semaphore("s_" + k))
                for k in ("sy", "gp", "gpc", "v", "s", "pe", "cc")}
        SY, GP, V, S, PE = (Eng(sems[k]) for k in ("sy", "gp", "v", "s", "pe"))
        GPC = Eng(sems["gpc"])
        GPC.ops = GP.ops
        CCSEM = sems["cc"]

        FQ0 = BUFH[:, 4096:5120]
        FQ1 = BUFH[:, 5120:6144]
        FQ2 = BUFH[:, 6144:7168]
        SQS = BUFH[:, 7168:8192]
        QXF = QXK.ap().bitcast(F32)
        HQ0, HQ1, HQ2 = SP4[:, 0:512], SP4[:, 512:1024], HSQ

        def dma(eng, out, in_):
            return eng.emit(lambda e, o=out, i=in_: e.dma_start(out=o, in_=i),
                            inc=16)

        _vw = []

        def _rngs(aps):
            out = []
            for ap in aps:
                if ap is None or isinstance(ap, (int, float)):
                    continue
                try:
                    nm = ap.tensor.name
                    off = int(ap.offset)
                    sh = list(ap.shape)
                    try:
                        stv = list(ap.stride)
                    except Exception:
                        stv = None
                    if stv and len(stv) == len(sh) and len(sh) > 1:
                        span = 1 + sum((int(d) - 1) * abs(int(s))
                                       for d, s in zip(sh[1:], stv[1:]))
                    else:
                        span = 1
                        for d in sh[1:]:
                            span *= int(d)
                    out.append((nm, off, off + max(span, 1)))
                except Exception:
                    out.append(("?", 0, 1 << 62))
            return out

        def _fsz(ap):
            try:
                return int(ap.free_size)
            except Exception:
                pass
            try:
                n = 1
                for d in list(ap.shape)[1:]:
                    n *= int(d)
                return n
            except Exception:
                return 1

        def _vsmall(aps):
            return all(_fsz(ap) <= 128 for ap in aps)

        def _vraw(outs, ins):
            rd = _rngs(ins)
            need = 0
            for age, (small, wr) in enumerate(_vw):
                if not small:
                    continue
                for nm, lo, hi in wr:
                    for nm2, lo2, hi2 in rd:
                        if ((nm == nm2 or nm == "?" or nm2 == "?")
                                and lo < hi2 and lo2 < hi):
                            need = max(need, 2 - age)
            for _ in range(need):
                V.emit(lambda e: e.tensor_copy(GAPB[:, 0:8], ZRO[:, 0:8]),
                       inc=0)
                _vw.insert(0, (False, []))
            _vw.insert(0, (_vsmall(outs), _rngs(outs)))
            del _vw[2:]

        def tsc(eng, out, in0, s1, s2=None, op0=OP.mult, op1=None):
            if eng is V:
                _vraw([out], [in0, s1, s2])
            if op1 is None:
                return eng.emit(lambda e, o=out, i=in0, a=s1, p0=op0:
                                e.tensor_scalar(o, i, a, None, p0), inc=1)
            return eng.emit(lambda e, o=out, i=in0, a=s1, b=s2, p0=op0, p1=op1:
                            e.tensor_scalar(o, i, a, b, p0, p1), inc=1)

        def tt_(eng, out, a, b, op):
            if eng is V:
                _vraw([out], [a, b])
            return eng.emit(lambda e, o=out, x=a, y=b, p=op:
                            e.tensor_tensor(o, x, y, p), inc=1)

        def stt(eng, out, in0, sc, in1, op0, op1):
            if eng is V:
                _vraw([out], [in0, sc, in1])
            return eng.emit(lambda e, o=out, i0=in0, s=sc, i1=in1, p0=op0, p1=op1:
                            e.scalar_tensor_tensor(o, i0, s, i1, p0, p1), inc=1)

        def act(eng, out, in_, fn, bias=0.0, scale=1.0, accum=None):
            return eng.emit(lambda e, o=out, i=in_, f=fn, b=bias, s=scale, a=accum:
                            e.activation(o, i, f, bias=b, scale=s, accum_out=a),
                            inc=1)

        def cp(eng, out, in_):
            if eng is V:
                _vraw([out], [in_])
            return eng.emit(lambda e, o=out, i=in_: e.tensor_copy(o, i), inc=1)

        def recip(eng, out, in_):
            if eng is V:
                _vraw([out], [in_])
                _vw[0] = (True, _vw[0][1])
            return eng.emit(lambda e, o=out, i=in_: e.reciprocal(o, i), inc=1)

        def red(eng, out, in_, op, absval=False):
            if eng is V:
                _vraw([out], [in_])
            return eng.emit(lambda e, o=out, i=in_, p=op, a=absval:
                            e.tensor_reduce(o, i, AX.X, p, apply_absolute_value=a),
                            inc=1)

        def mm(out, lhsT, rhs, start, stop, inc=0):
            return PE.emit(lambda e, o=out, l=lhsT, r=rhs, a=start, b=stop:
                           e.matmul(o, l, r, start=a, stop=b), inc=inc)

        def tr(out, in_, ident):
            return PE.emit(lambda e, o=out, i=in_, d=ident:
                           e.transpose(o, i, d), inc=1)

        def scan(eng, out, d0, d1, init):
            if eng is V:
                _vraw([out], [d0, d1, init])
            return eng.emit(lambda e, o=out, a=d0, b=d1, i=init:
                            e.tensor_tensor_scan(o, a, b, i, OP.mult, OP.add),
                            inc=1)

        dma(SY, SP4[:64, :128], vecs8.ap())
        dma(SY, SCL[:, 0:1], scal2[0:1, :].broadcast_to([128, 1]))
        dma(SY, SCL[:, 1:2], scal2[1:2, :].broadcast_to([128, 1]))
        dma(SY, SCL[:, 2:6], onehot.ap())
        GPC.emit(lambda e: e.iota(IDF[:, :], [[1, 128]], channel_multiplier=-1,
                                  allow_small_or_imprecise_dtypes=True), inc=1)
        GPC.emit(lambda e: e.memset(ZRO[:, :], 0.0), inc=1)
        GPC.emit(lambda e: e.memset(SP4[:, 512:640], 1.0), inc=1)
        V.wait(GPC)
        tsc(V, SCL[:, 31:32], ZRO[:, 0:1], 1.0, None, OP.add)
        ONECOL = SCL[:, 31:32]
        tsc(V, IDB[:, :], IDF[:, :], 0.0, None, OP.is_equal)
        tsc(V, IDF[:, :], IDF[:, :], 0.0, None, OP.is_equal)
        PE.wait(SY)
        PE.wait(V)
        tr(PTR[:, :64], SP4[:64, :128], IDF[:64, :64])
        S.wait(PE)
        act(S, CONS[:, :], PTR[:128, :64], AF.Identity)
        V.wait(S)
        AFWD = CONS[:, 0:8]
        S.wait(V)
        act(S, SUM[:, 0:8], CONS[:, 8:16], AF.Exp)
        V.wait(S)
        tsc(V, SUM[:, 0:8], SUM[:, 0:8], 1.0, None, OP.add)
        S.wait(V)
        act(S, SUM[:, 0:8], SUM[:, 0:8], AF.Ln)
        act(S, SUM[:, 16:24], CONS[:, 0:8], AF.Ln)
        V.wait(S)
        tsc(V, SUM[:, 8:16], SUM[:, 0:8], -8.0)
        tsc(V, SCL[:, 30:31], ZRO[:, 0:1], EPS, None, OP.add)
        M8SP = SUM[:, 8:16]
        LNAF = SUM[:, 16:24]
        cp(V, SUM[:, 24:32], CONS[:, 0:8])
        for _ in range(int(round(np.log2(t)))):
            tt_(V, SUM[:, 24:32], SUM[:, 24:32], SUM[:, 24:32], OP.mult)
        ACORE = SUM[:, 24:32]

        def gate_rows_ap(src, it):
            b0 = 2048 * (it // 16) + (it % 16)
            return bass.AP(src, b0 * D, [[16 * D, 128], [1, D]])

        def global_absmax(src_ap_fn, ntile, npiece, out_col):
            tt_(V, GWS[:, out_col:out_col + 1], ZRO[:, 0:1], ZRO[:, 0:1], OP.add)
            for it in range(ntile):
                for piece in range(npiece):
                    j = (it * npiece + piece) % 2
                    dst = WS[:, j * 1024:(j + 1) * 1024]
                    SY.wait(V)
                    dma(SY, dst, src_ap_fn(it, piece))
                    V.wait(SY)
                    red(V, GWS[:, out_col + 1:out_col + 2], dst, OP.max,
                        absval=True)
                    tt_(V, GWS[:, out_col:out_col + 1],
                        GWS[:, out_col:out_col + 1],
                        GWS[:, out_col + 1:out_col + 2], OP.max)
            PE.wait(V)
            PE.wait(S)
            tr(PTR[:1, 0:128], GWS[:, out_col:out_col + 1], IDF[:, :])
            S.wait(PE)
            act(S, AMB[:1, 128:256], PTR[:1, 0:128], AF.Identity)
            V.wait(S)
            red(V, GWS[:1, out_col:out_col + 1], AMB[:1, 128:256], OP.max)
            PE.wait(V)
            mm(PTR[:, 128:129], SP4[:1, 512:640], GWS[:1, out_col:out_col + 1],
               True, True, inc=1)
            V.wait(PE)
            cp(V, GWS[:, out_col:out_col + 1], PTR[:, 128:129])

        global_absmax(lambda it, p: gate_rows_ap(gate_w, it), MI, 1, 0)
        global_absmax(lambda it, p: gate_rows_ap(up_w, it), MI, 1, 4)
        global_absmax(lambda it, p: down_w.ap()[it * 128:(it + 1) * 128,
                                                p * 1024:(p + 1) * 1024],
                      G, 4, 8)
        for col in (0, 4, 8):
            recip(V, GWS[:, col + 2:col + 3], GWS[:, col:col + 1])
        cp(V, SP4[:, 0:128], IDF[:, :])
        cp(V, SP4[:, 128:256], IDF[:, :])
        for j, col in enumerate((0, 4, 8)):
            tsc(V, SCL[:, 6 + j:7 + j], GWS[:, col + 2:col + 3], 2688.0)
        tt_(V, SCL[:, 16:17], SCL[:, 0:1], SCL[:, 6:7], OP.mult)
        tt_(V, SCL[:, 17:18], SCL[:, 0:1], SCL[:, 7:8], OP.mult)
        tt_(V, SCL[:, 18:19], SCL[:, 1:2], SCL[:, 8:9], OP.mult)
        for j in range(3):
            recip(V, SCL[:, 19 + j:20 + j], SCL[:, 16 + j:17 + j])
        cp(V, SP4[:, 0:128], IDF[:, :])
        cp(V, SP4[:, 128:256], IDF[:, :])
        cp(V, SCL[:, 9:10], SCL[:, 19:20])
        cp(V, SCL[:, 10:11], SCL[:, 20:21])
        tsc(V, SCL[:, 11:12], SCL[:, 21:22], 0.5)
        tsc(V, SCL[:, 12:13], SCL[:, 1:2], 1.0 / 12.0)
        tsc(V, SCL[:, 13:14], SCL[:, 1:2], 0.5)
        tsc(V, SCL[:, 24:25], SCL[:, 6:7], 1.0 / 12.0)
        tsc(V, SCL[:, 26:27], SCL[:, 6:7], 0.5)
        tsc(V, SCL[:, 25:26], SCL[:, 7:8], 1.0 / 12.0)
        tsc(V, SCL[:, 27:28], SCL[:, 7:8], 0.5)
        tsc(V, SCL[:, 28:29], SCL[:, 8:9], 1.0 / 12.0)
        tsc(V, SCL[:, 29:30], SCL[:, 8:9], 0.5)
        GX = SCL[:, 0:1]
        INVG1, INVG2, HINVD = SCL[:, 9:10], SCL[:, 10:11], SCL[:, 11:12]
        GD12, GD2 = SCL[:, 12:13], SCL[:, 13:14]

        def fp4(src, width, rr, s8, t0, t1, t2, outq, blk16, vq):
            nb = width // 16
            if blk16:
                tt_(V, t0[:, :width].rearrange("p (b k) -> p b k", k=16),
                    src.rearrange("p (b k) -> p b k", k=16),
                    rr.unsqueeze(2).broadcast_to([128, nb, 16]), OP.mult)
            else:
                tt_(V, t0[:, :width], src, rr, OP.mult)
            tsc(V, t0[:, :width], t0[:, :width], 6.5, -6.5, OP.min, OP.max)
            tsc(V, t1[:, :width], t0[:, :width], 2.0, 0.0, OP.subtract, OP.max)
            tsc(V, t2[:, :width], t0[:, :width], -2.0, 0.0, OP.subtract, OP.min)
            tt_(V, t1[:, :width], t1[:, :width], t2[:, :width], OP.add)
            tsc(V, t2[:, :width], t0[:, :width], 4.0, 0.0, OP.subtract, OP.max)
            stt(V, t2[:, :width], t2[:, :width], -0.25, t0[:, :width],
                OP.mult, OP.add)
            tsc(V, t0[:, :width], t0[:, :width], -4.0, 0.0, OP.subtract, OP.min)
            stt(V, t0[:, :width], t0[:, :width], -0.25, t2[:, :width],
                OP.mult, OP.add)
            stt(V, t0[:, :width], t1[:, :width], -0.5, t0[:, :width],
                OP.mult, OP.add)
            tsc(V, vq[:, :width], t0[:, :width], C8)
            tsc(V, t1[:, :width], vq[:, :width], 2 * C8, 0.0, OP.subtract, OP.max)
            tsc(V, t2[:, :width], vq[:, :width], -2 * C8, 0.0, OP.subtract, OP.min)
            tt_(V, t1[:, :width], t1[:, :width], t2[:, :width], OP.add)
            tsc(V, t2[:, :width], vq[:, :width], 3 * C8, 0.0, OP.subtract, OP.max)
            stt(V, t2[:, :width], t2[:, :width], 2.0, t1[:, :width],
                OP.mult, OP.add)
            tsc(V, t1[:, :width], vq[:, :width], -3 * C8, 0.0, OP.subtract,
                OP.min)
            stt(V, t1[:, :width], t1[:, :width], 2.0, t2[:, :width],
                OP.mult, OP.add)
            tt_(V, t1[:, :width], vq[:, :width], t1[:, :width], OP.add)
            if blk16:
                tt_(V, outq.rearrange("p (b k) -> p b k", k=16),
                    t1[:, :width].rearrange("p (b k) -> p b k", k=16),
                    s8.unsqueeze(2).broadcast_to([128, nb, 16]), OP.mult)
            else:
                tt_(V, outq, t1[:, :width], s8, OP.mult)

        def sc_stage(amax_ap, n, mul_ap, half_ap, rr_out, s8_out, v8t):
            tt_(V, rr_out[:, :n], amax_ap, mul_ap.broadcast_to([128, n]), OP.mult)
            tsc(V, rr_out[:, :n], rr_out[:, :n], 1.0, 224.0, OP.mult, OP.min)
            cp(V, v8t[:, :n], rr_out[:, :n])
            recip(V, rr_out[:, :n], v8t[:, :n])
            tsc(V, s8_out[:, :n], v8t[:, :n], 512.0)
            cp(V, STATS[:, 32:64], SUM[:, 0:32])
            tsc(V, rr_out[:, :n], rr_out[:, :n], 1e30, None, OP.min)
            tt_(V, rr_out[:, :n], rr_out[:, :n],
                half_ap.broadcast_to([128, n]), OP.mult)

        def wprep_gateup(src, dst, which):
            mul_ap = SCL[:, 24 + which:25 + which]
            half_ap = SCL[:, 26 + which:27 + which]
            for pair in range(MI // 2):
                its = (2 * pair, 2 * pair + 1)
                for j, it in enumerate(its):
                    SY.wait(V)
                    SY.wait(PE)
                    dma(SY, WS[:, j * 1024:(j + 1) * 1024], gate_rows_ap(src, it))
                V.wait(SY)
                for j in range(2):
                    red(V, AMB[:, j * 64:(j + 1) * 64],
                        WS[:, j * 1024:(j + 1) * 1024]
                        .rearrange("p (b k) -> p b k", k=16), OP.max, absval=True)
                sc_stage(AMB[:, :128], 128, mul_ap, half_ap, RRB, S8B, V8T)
                for j, it in enumerate(its):
                    fp4(WS[:, j * 1024:(j + 1) * 1024], 1024,
                        RRB[:, j * 64:(j + 1) * 64], S8B[:, j * 64:(j + 1) * 64],
                        FQ0, FQ1, FQ2, QTOK[:, :1024], True, VQ8)
                    for g in range(G):
                        slot = g % 8
                        PE.wait(V)
                        PE.wait(S)
                        tr(PTB[:, slot * 128:slot * 128 + 128],
                           QTOK[:, g * 128:(g + 1) * 128], IDB[:, :])
                        S.wait(PE)
                        act(S, WQ[:, slot * 128:slot * 128 + 128],
                            PTB[:, slot * 128:slot * 128 + 128], AF.Identity)
                        GP.wait(S)
                        dma(GP, dst.ap()[it, g, :, :],
                            WQ[:, slot * 128:slot * 128 + 128])

        wprep_gateup(gate_w, qg_s, 0)
        wprep_gateup(up_w, qu_s, 1)

        def wprep_down():
            for it in range(G):
                for half in range(2):
                    SY.wait(V)
                    SY.wait(PE)
                    dma(SY, WS[:, :2048],
                        down_w.ap()[it * 128:(it + 1) * 128,
                                    half * 2048:(half + 1) * 2048])
                    V.wait(SY)
                    red(V, AMB[:, 0:128],
                        WS[:, :2048].rearrange("p (b k) -> p b k", k=16),
                        OP.max, absval=True)
                    sc_stage(AMB[:, :128], 128, SCL[:, 28:29], SCL[:, 29:30],
                             RRB, S8B, V8T)
                    for j in range(2):
                        fp4(WS[:, j * 1024:(j + 1) * 1024], 1024,
                            RRB[:, j * 64:(j + 1) * 64],
                            S8B[:, j * 64:(j + 1) * 64],
                            FQ0, FQ1, FQ2, QTOK[:, :1024], True, VQ8)
                        cp(V, WQ[:, (2 + j) * 1024:(3 + j) * 1024],
                           QTOK[:, :1024])
                    for kk in range(16):
                        tau = half * 16 + kk
                        slot = kk % 8
                        PE.wait(V)
                        PE.wait(S)
                        tr(PTB[:, slot * 128:slot * 128 + 128],
                           WQ[:, 2048 + kk:4096:16], IDB[:, :])
                        S.wait(PE)
                        act(S, WQ[:, slot * 128:slot * 128 + 128],
                            PTB[:, slot * 128:slot * 128 + 128], AF.Identity)
                        GP.wait(S)
                        dma(GP, qd_s.ap()[tau, it, :, :],
                            WQ[:, slot * 128:slot * 128 + 128])

        wprep_down()
        GP.ops.append(lambda e, s=GP.sem, v=GP.count: e.wait_ge(s, v))

        def ln_tile(xtok, c, gmul=None):
            red(V, STATS[:, c:c + 1], xtok, OP.add)
            S.wait(V)
            act(S, SQS, xtok, AF.Square, accum=STATS[:, 8 + c:9 + c])
            V.wait(S)
            tsc(V, STATS[:, 16 + c:17 + c], STATS[:, c:c + 1], 1.0 / D)
            tsc(V, STATS[:, 20 + c:21 + c], STATS[:, 8 + c:9 + c], 1.0 / D)
            tt_(V, STATS[:, 24 + c:25 + c], STATS[:, 16 + c:17 + c],
                STATS[:, 16 + c:17 + c], OP.mult)
            tt_(V, STATS[:, 20 + c:21 + c], STATS[:, 20 + c:21 + c],
                STATS[:, 24 + c:25 + c], OP.subtract)
            S.wait(V)
            act(S, STATS[:, 24 + c:25 + c], STATS[:, 20 + c:21 + c], AF.Sqrt,
                bias=SCL[:, 30:31])
            V.wait(S)
            recip(V, STATS[:, 28 + c:29 + c], STATS[:, 24 + c:25 + c])
            cp(V, STATS[:, 32:48], SUM[:, 0:16])
            cp(V, STATS[:, 48:64], SUM[:, 16:32])
            if gmul is not None:
                tt_(V, STATS[:, 28 + c:29 + c], STATS[:, 28 + c:29 + c],
                    gmul, OP.mult)
            tt_(V, STATS[:, 24 + c:25 + c], STATS[:, 16 + c:17 + c],
                STATS[:, 28 + c:29 + c], OP.mult)
            tsc(V, STATS[:, 24 + c:25 + c], STATS[:, 24 + c:25 + c], -1.0)
            return STATS[:, 28 + c:29 + c], STATS[:, 24 + c:25 + c]

        def transpose_fm_to_tok(tok0, dst):
            for g in range(G):
                sl = g % 4
                if sl == 0:
                    PE.wait(S)
                    PE.wait(V)
                tr(PTR[:, sl * 128:sl * 128 + 128],
                   X[:, g * t + tok0:g * t + tok0 + 128], IDF[:, :])
                if sl == 3:
                    S.wait(PE)
                    act(S, dst[:, (g - 3) * 128:(g + 1) * 128], PTR[:, :512],
                        AF.Identity)

        def build_x():
            for lt in range(t // 128):
                tok0 = lt * 128
                SY.wait(V)
                SY.wait(PE)
                dma(SY, SP4[:, :], x_in.ap()[tok0:tok0 + 128, :])
                PE.wait(SY)
                for g in range(G):
                    sl = g % 4
                    if sl == 0:
                        PE.wait(S)
                    tr(PTR[:, sl * 128:sl * 128 + 128],
                       SP4[:, g * 128:(g + 1) * 128], IDF[:, :])
                    if sl == 3:
                        S.wait(PE)
                        for gg in range(g - 3, g + 1):
                            act(S, X[:, gg * t + tok0:gg * t + tok0 + 128],
                                PTR[:, (gg % 4) * 128:(gg % 4) * 128 + 128],
                                AF.Identity)

        def ffn(sub):
            for it in range(ntt):
                for lt in range(nlt):
                    tok0 = it * tt + lt * 128
                    if sub == 1:
                        SY.wait(V)
                        SY.wait(S)
                        dma(SY, SP4[:, :], x_in.ap()[tok0:tok0 + 128, :])
                        V.wait(SY)
                        S.wait(SY)
                    else:
                        transpose_fm_to_tok(tok0, SP4)
                        V.wait(S)
                    scale, bias = ln_tile(SP4[:, :], lt, gmul=GX)
                    S.wait(V)
                    act(S, BUFH[:, lt * 1024:(lt + 1) * 1024], SP4[:, :],
                        AF.Identity, bias=bias, scale=scale)
                    V.wait(S)
                    red(V, AMB[:, lt * 64:(lt + 1) * 64],
                        BUFH[:, lt * 1024:(lt + 1) * 1024]
                        .rearrange("p (b k) -> p b k", k=16), OP.max, absval=True)
                nbt = nlt * 64
                tsc(V, SCH[:, :nbt], AMB[:, :nbt], 1.0 / 12.0, 224.0,
                    OP.mult, OP.min)
                cp(V, V8T[:, :nbt], SCH[:, :nbt])
                recip(V, SCH[:, :nbt], V8T[:, :nbt])
                tsc(V, S8B[:, :nbt], V8T[:, :nbt], 512.0)
                cp(V, STATS[:, 32:64], SUM[:, 0:32])
                tsc(V, RRB[:, :nbt], SCH[:, :nbt], 1e30, 0.5, OP.min, OP.mult)
                for lt in range(nlt):
                    fp4(BUFH[:, lt * 1024:(lt + 1) * 1024], 1024,
                        RRB[:, lt * 64:(lt + 1) * 64],
                        S8B[:, lt * 64:(lt + 1) * 64],
                        FQ0, FQ1, FQ2, QTOK[:, :1024], True, VQ8)
                    for g in range(G):
                        slot = g % 8
                        PE.wait(V)
                        PE.wait(S)
                        tr(PTB[:, slot * 128:slot * 128 + 128],
                           QTOK[:, g * 128:(g + 1) * 128], IDB[:, :])
                        S.wait(PE)
                        act(S, QXK[:, g * tt + lt * 128:g * tt + lt * 128 + 128],
                            PTB[:, slot * 128:slot * 128 + 128], AF.Identity)
                for hh in range(2):
                    for m16 in range(16):
                        m = hh * 16 + m16
                        bg, bu = PB[2 * (m % 2)], PB[2 * (m % 2) + 1]
                        wql = WQ[:, (m % 2) * 2048:(m % 2) * 2048 + 1024]
                        wqu = WQ[:, (m % 2) * 2048 + 1024:(m % 2) * 2048 + 2048]
                        GP.wait(PE)
                        dma(GP, wql.rearrange("p (g c) -> p g c", g=G),
                            qg_s.ap()[m, :, :, :].rearrange("g p c -> p g c"))
                        dma(GP, wqu.rearrange("p (g c) -> p g c", g=G),
                            qu_s.ap()[m, :, :, :].rearrange("g p c -> p g c"))
                        PE.wait(GP)
                        PE.wait(V)
                        PE.wait(S)
                        for k in range(G):
                            mm(bg[:, :tt], wql[:, k * 128:(k + 1) * 128],
                               QXK[:, k * tt:k * tt + tt], k == 0, k == G - 1,
                               inc=(1 if k == G - 1 else 0))
                        for k in range(G):
                            mm(bu[:, :tt], wqu[:, k * 128:(k + 1) * 128],
                               QXK[:, k * tt:k * tt + tt], k == 0, k == G - 1,
                               inc=(1 if k == G - 1 else 0))
                        S.wait(PE)
                        sgt = SP4[:, :512] if m % 2 == 0 else SP4[:, 512:1024]
                        act(S, sgt[:, :tt], bg[:, :tt], AF.Silu, scale=INVG1)
                        V.wait(S)
                        hdst = BUFH[:, m16 * 512:m16 * 512 + tt]
                        stt(V, hdst, bu[:, :tt], INVG2, sgt[:, :tt],
                            OP.mult, OP.mult)
                        S.wait(V)
                        act(S, HSQ[:, :tt], hdst, AF.Square)
                        V.wait(S)
                        if m16 == 0:
                            cp(V, AMH[:, :tt], HSQ[:, :tt])
                        else:
                            tt_(V, AMH[:, :tt], AMH[:, :tt], HSQ[:, :tt], OP.max)
                    S.wait(V)
                    act(S, AMH[:, :tt], AMH[:, :tt], AF.Sqrt)
                    V.wait(S)
                    sc_stage(AMH[:, :tt], tt, GD12, GD2, SCH, S8B, V8T)
                    for m16 in range(16):
                        k = hh * 16 + m16
                        qslot = QHD[:, (m16 % 3) * 512:(m16 % 3) * 512 + tt]
                        if m16 >= 2:
                            V.wait(PE)
                        fp4(BUFH[:, m16 * 512:m16 * 512 + tt], tt,
                            SCH[:, :tt], S8B[:, :tt],
                            HQ0, HQ1, HQ2, qslot, False, VQ8)
                        wqd = WQ[:, (m16 % 2) * 1024:(m16 % 2) * 1024 + 1024]
                        GP.wait(PE)
                        dma(GP, wqd.rearrange("p (g c) -> p g c", g=G),
                            qd_s.ap()[k, :, :, :].rearrange("g p c -> p g c"))
                        PE.wait(GP)
                        PE.wait(V)
                        for m in range(G):
                            mm(PB[m][:, :tt], wqd[:, m * 128:(m + 1) * 128],
                               qslot, m16 == 0, m16 == 15,
                               inc=(1 if m == G - 1 else 0))
                    V.wait(PE)
                    for m in range(G):
                        xsl = X[:, m * t + it * tt:m * t + it * tt + tt]
                        stt(V, xsl, PB[m][:, :tt], HINVD, xsl, OP.mult, OP.add)

        def attn():
            LGA = QXF[:, 0:512]
            PEX = QXF[:, 512:1024]
            WEX = QXF[:, 1024:1536]
            TM0 = QXF[:, 1536:2048]
            SIT = QTOK.ap().bitcast(F32)
            AT = AMH
            XG = BUFH[:, 0:G * 512]
            VC = BUFH[:, G * 512:2 * G * 512]
            U = BUFA

            GPC.wait(V)
            GPC.wait(S)
            GPC.emit(lambda e: e.iota(RRB[:, :], [[1, 512]],
                                      channel_multiplier=0,
                                      allow_small_or_imprecise_dtypes=True),
                     inc=1)
            S.wait(GPC)

            cp(V, SUM[:, 32:48], ZRO[:, 0:16])
            for c in range(nch):
                ct0 = c * ch
                for lt in range(ch // 128):
                    tok0 = ct0 + lt * 128
                    transpose_fm_to_tok(tok0, SP4)
                    V.wait(S)
                    scale, bias = ln_tile(SP4[:, :], lt % 4)
                    S.wait(V)
                    act(S, SQS, SP4[:, :], AF.Identity, bias=bias, scale=scale)
                    PE.wait(S)
                    for g in range(G):
                        sl = g % 4
                        if sl == 0:
                            PE.wait(S)
                            PE.wait(V)
                        tr(PTR[:, sl * 128:sl * 128 + 128],
                           SQS[:, g * 128:(g + 1) * 128], IDF[:, :])
                        if sl == 3:
                            q0 = (g // 4) * 512
                            S.wait(PE)
                            act(S, SP4[:, q0:q0 + 512], PTR[:, :512],
                                AF.Identity)
                            V.wait(S)
                            for gg in range(g - 3, g + 1):
                                pseg = SP4[:, q0 + (gg % 4) * 128:
                                            q0 + (gg % 4) * 128 + 128]
                                act(S, XG[:, gg * ch + lt * 128:
                                          gg * ch + lt * 128 + 128],
                                    pseg, AF.Gelu_apprx_tanh)
                                xsl = X[:, gg * t + tok0:gg * t + tok0 + 128]
                                tt_(V, xsl, xsl, pseg, OP.add)
                for wname, bcol, isU in (("Wf1", 2, True), ("Wbx", 4, False)):
                    for k in range(G):
                        SY.wait(PE)
                        dma(SY, WS[:, (k % 2) * 1024:(k % 2) * 1024 + 1024],
                            W[wname].ap()[k * 128:(k + 1) * 128, :])
                        PE.wait(SY)
                        if k == 0:
                            PE.wait(S)
                            PE.wait(V)
                        for m in range(G):
                            mm(PB[m][:, :ch],
                               WS[:, (k % 2) * 1024 + m * 128:
                                  (k % 2) * 1024 + m * 128 + 128],
                               XG[:, k * ch:(k + 1) * ch], k == 0, k == G - 1,
                               inc=(1 if m == G - 1 else 0))
                    S.wait(PE)
                    for m in range(G):
                        if isU:
                            dst = U[:, m * t + ct0:m * t + ct0 + ch]
                        else:
                            dst = VC[:, m * ch:(m + 1) * ch]
                        act(S, dst, PB[m][:, :ch], AF.Identity,
                            bias=CONS[:, bcol * 8 + m:bcol * 8 + m + 1])
                for g in range(G):
                    for wname, bcol, sdst in (("Wgx", 5, SP4[:, 0:512]),
                                              ("Wga", 6, SP4[:, 512:1024])):
                        for k in range(G):
                            SY.wait(PE)
                            dma(SY, WS[:, (k % 2) * 128:(k % 2) * 128 + 128],
                                W[wname].ap()[k * 128:(k + 1) * 128,
                                              g * 128:(g + 1) * 128])
                            PE.wait(SY)
                            if k == 0:
                                PE.wait(S)
                            mm(PB[4][:, :ch],
                               WS[:, (k % 2) * 128:(k % 2) * 128 + 128],
                               VC[:, k * ch:(k + 1) * ch], k == 0, k == G - 1,
                               inc=1)
                        S.wait(PE)
                        act(S, sdst[:, :ch], PB[4][:, :ch], AF.Sigmoid,
                            bias=CONS[:, bcol * 8 + g:bcol * 8 + g + 1])
                    V.wait(S)
                    tsc(V, LGA[:, :ch], SP4[:, 512:512 + ch], M8SP[:, g:g + 1])
                    S.wait(V)
                    act(S, AT[:, :ch], LGA[:, :ch], AF.Exp)
                    act(S, WEX[:, :ch], LGA[:, :ch], AF.Exp, scale=2.0)
                    GP.wait(S)
                    dma(GP, a_s.ap()[g, :, ct0:ct0 + ch], AT[:, :ch])
                    V.wait(S)
                    tsc(V, WEX[:, :ch], WEX[:, :ch], -1.0, 1.0, OP.mult, OP.add)
                    tsc(V, WEX[:, :ch], WEX[:, :ch], 0.0, None, OP.max)
                    S.wait(V)
                    act(S, TM0[:, :ch], WEX[:, :ch], AF.Sqrt)
                    V.wait(S)
                    tt_(V, SIT[:, :ch], SP4[:, 0:ch], VC[:, g * ch:g * ch + ch],
                        OP.mult)
                    tt_(V, SIT[:, :ch], SIT[:, :ch], TM0[:, :ch], OP.mult)
                    GP.wait(V)
                    dma(GP, si_s.ap()[g, :, ct0:ct0 + ch], SIT[:, :ch])
                    scan(V, PEX[:, :ch], ONECOL.broadcast_to([128, ch]),
                         LGA[:, :ch], 0.0)
                    tt_(V, PEX[:, :ch], PEX[:, :ch], LGA[:, :ch], OP.subtract)
                    S.wait(V)
                    act(S, WEX[:, :ch], PEX[:, :ch], AF.Exp,
                        bias=SUM[:, 32 + g:33 + g])
                    V.wait(S)
                    tt_(V, TM0[:, :ch], WEX[:, :ch], SIT[:, :ch], OP.mult)
                    red(V, STATS[:, 0:1], TM0[:, :ch], OP.add)
                    tt_(V, SUM[:, 40 + g:41 + g], SUM[:, 40 + g:41 + g],
                        STATS[:, 0:1], OP.add)
                    red(V, STATS[:, 1:2], LGA[:, :ch], OP.add)
                    tt_(V, SUM[:, 32 + g:33 + g], SUM[:, 32 + g:33 + g],
                        STATS[:, 1:2], OP.add)
            for g in range(G):
                for h2 in range(nh):
                    tsc(V, STATS[:, 2:3], LNAF[:, g:g + 1],
                        float(t - 1 - hw_ * h2))
                    tsc(V, STATS[:, 3:4], LNAF[:, g:g + 1], -1.0)
                    S.wait(V)
                    act(S, WEX[:, :hw_], RRB[:, :hw_], AF.Exp,
                        bias=STATS[:, 2:3], scale=STATS[:, 3:4])
                    V.wait(S)
                    tt_(V, TM0[:, :hw_],
                        U[:, g * t + h2 * hw_:g * t + h2 * hw_ + hw_],
                        WEX[:, :hw_], OP.mult)
                    red(V, STATS[:, 4 + h2:5 + h2], TM0[:, :hw_], OP.add)
                for h2 in range(1, nh):
                    tt_(V, STATS[:, 4:5], STATS[:, 4:5],
                        STATS[:, 4 + h2:5 + h2], OP.add)
                cp(V, SUM[:, 48 + g:49 + g], STATS[:, 4:5])
            S.wait(V)
            act(S, SUM[:, 56:64], SUM[:, 32:40], AF.Exp)
            V.wait(S)
            cp(V, EXC[:, 0:8], SUM[:, 48:56])
            cp(V, EXC[:, 8:16], SUM[:, 40:48])
            cp(V, EXC[:, 16:24], SUM[:, 56:64])
            GP.wait(V)
            dma(GP, cc_i.ap(), EXC[:, 0:24])
            GP.ops.append(lambda e, s=GP.sem, v=GP.count: e.wait_ge(s, v))

            def _cc(e):
                e.collective_compute(
                    "AllGather", OP.bypass,
                    replica_groups=[[0, 1, 2, 3], [4, 5, 6, 7]],
                    ins=[cc_i.ap().opt()], outs=[cc_o.ap().opt()]
                ).then_inc(CCSEM, 1)
            GP.ops.append(_cc)
            GP.ops.append(lambda e: e.wait_ge(CCSEM, 1))
            dma(GP, EXC[:, 0:96].rearrange("p (q c) -> p q c", q=4),
                cc_o.ap().rearrange("q p c -> p q c"))
            V.wait(GP)
            cp(V, STATS[:, 0:8], ZRO[:, 0:8])
            cp(V, STATS[:, 8:16], ZRO[:, 0:8])
            cp(V, STATS[:, 16:24], ZRO[:, 0:8])
            for qq in range(4):
                stt(V, STATS[:, 8:16], STATS[:, 0:8], SCL[:, 2 + qq:3 + qq],
                    STATS[:, 8:16], OP.mult, OP.add)
                tt_(V, STATS[:, 0:8], STATS[:, 0:8], ACORE, OP.mult)
                tt_(V, STATS[:, 0:8], STATS[:, 0:8],
                    EXC[:, qq * 24:qq * 24 + 8], OP.add)
            cp(V, STATS[:, 0:8], ZRO[:, 0:8])
            for qq in range(3, -1, -1):
                stt(V, STATS[:, 16:24], STATS[:, 0:8], SCL[:, 2 + qq:3 + qq],
                    STATS[:, 16:24], OP.mult, OP.add)
                tt_(V, STATS[:, 0:8], STATS[:, 0:8],
                    EXC[:, qq * 24 + 16:qq * 24 + 24], OP.mult)
                tt_(V, STATS[:, 0:8], STATS[:, 0:8],
                    EXC[:, qq * 24 + 8:qq * 24 + 16], OP.add)
            for g in range(G):
                cp(V, BUFH[:, 0:t], AFWD[:, g:g + 1].broadcast_to([128, t]))
                scan(V, U[:, g * t:(g + 1) * t], BUFH[:, 0:t],
                     U[:, g * t:(g + 1) * t], STATS[:, 8 + g:9 + g])
            for it in range(ntt):
                for k in range(G):
                    SY.wait(PE)
                    dma(SY, WS[:, (k % 2) * 1024:(k % 2) * 1024 + 1024],
                        W["Wf2"].ap()[k * 128:(k + 1) * 128, :])
                    PE.wait(SY)
                    if k == 0:
                        PE.wait(V)
                    for m in range(G):
                        mm(PB[m][:, :tt],
                           WS[:, (k % 2) * 1024 + m * 128:
                              (k % 2) * 1024 + m * 128 + 128],
                           U[:, k * t + it * tt:k * t + it * tt + tt],
                           k == 0, k == G - 1,
                           inc=(1 if m == G - 1 else 0))
                V.wait(PE)
                for m in range(G):
                    xsl = X[:, m * t + it * tt:m * t + it * tt + tt]
                    stt(V, xsl, PB[m][:, :tt], CONS[:, 3 * 8 + m:3 * 8 + m + 1],
                        xsl, OP.add, OP.add)
            SY.wait(GP)
            for g in range(G):
                SY.wait(V)
                dma(SY, BUFH[:, 0:t], si_s.ap()[g, :, :])
                dma(SY, BUFH[:, t:2 * t], a_s.ap()[g, :, :])
                V.wait(SY)
                scan(V, U[:, g * t:(g + 1) * t],
                     BUFH[:, 2 * t - 1:t - 1:-1] if t > 1 else BUFH[:, t:2 * t],
                     BUFH[:, t - 1::-1], STATS[:, 16 + g:17 + g])
            for it in range(ntt):
                for k in range(G):
                    SY.wait(PE)
                    dma(SY, WS[:, (k % 2) * 1024:(k % 2) * 1024 + 1024],
                        W["Wb2"].ap()[k * 128:(k + 1) * 128, :])
                    PE.wait(SY)
                    if k == 0:
                        PE.wait(V)
                    for m in range(G):
                        mm(PB[m][:, :tt],
                           WS[:, (k % 2) * 1024 + m * 128:
                              (k % 2) * 1024 + m * 128 + 128],
                           U[:, k * t + (t - (it + 1) * tt):
                              k * t + (t - it * tt)],
                           k == 0, k == G - 1,
                           inc=(1 if m == G - 1 else 0))
                V.wait(PE)
                for m in range(G):
                    xsl = X[:, m * t + it * tt:m * t + it * tt + tt]
                    stt(V, xsl, PB[m][:, tt - 1::-1],
                        CONS[:, 7 * 8 + m:7 * 8 + m + 1], xsl, OP.add, OP.add)

        def output():
            for lt in range(t // 128):
                tok0 = lt * 128
                transpose_fm_to_tok(tok0, SP4)
                V.wait(S)
                cp(V, BUFA[:, lt * 1024:(lt + 1) * 1024], SP4[:, :])
            SY.wait(V)
            dma(SY, y_out.ap().rearrange("(n p) c -> p n c", p=128),
                BUFA[:, 0:(t // 128) * 1024].rearrange("p (n c) -> p n c", c=D))
            SY.ops.append(lambda e, s=SY.sem, v=SY.count: e.wait_ge(s, v))

        if debug_out is not None:
            debug_out.update(locals())

        build_x()
        ffn(1)
        attn()
        ffn(3)
        output()

        with nc.Block() as block:
            block.sync(lambda e: [op(e) for op in SY.ops])
            block.gpsimd(lambda e: [op(e) for op in GP.ops])
            block.vector(lambda e: [op(e) for op in V.ops])
            block.scalar(lambda e: [op(e) for op in S.ops])
            block.tensor(lambda e: [op(e) for op in PE.ops])

    return nc


_PROG_CACHE = {}


def _get_prog(t):
    if t not in _PROG_CACHE:
        _PROG_CACHE[t] = build_program(t)
    return _PROG_CACHE[t]


def make_inmaps(x_full, wd, t=2048, ncore=8):
    vec_order = ("a_fwd", "a_param", "bf1", "bf2", "bbx", "bgx", "bga", "bb2")
    vecs8 = np.concatenate([np.asarray(wd[k], np.float32).reshape(8, 128)
                            for k in vec_order], axis=0)
    scal2 = np.stack([np.asarray(wd["s_in"], np.float32).reshape(1),
                      np.asarray(wd["s_in_down"], np.float32).reshape(1)])
    base = {
        "Wf1": np.asarray(wd["Wf1"], np.float32),
        "Wf2": np.asarray(wd["Wf2"], np.float32),
        "Wbx": np.asarray(wd["Wbx"], np.float32),
        "Wgx": np.asarray(wd["Wgx"], np.float32),
        "Wga": np.asarray(wd["Wga"], np.float32),
        "Wb2": np.asarray(wd["Wb2"], np.float32),
        "gate_w": np.asarray(wd["gate_w"], np.float32),
        "up_w": np.asarray(wd["up_w"], np.float32),
        "down_w": np.asarray(wd["down_w"], np.float32),
        "vecs8": vecs8, "scal2": scal2,
    }
    nq = ncore // x_full.shape[0]
    in_maps = []
    for c in range(ncore):
        b, q = divmod(c, nq)
        oh = np.zeros((128, 4), np.float32)
        oh[:, q % 4] = 1.0
        m = dict(base)
        m["x"] = np.ascontiguousarray(x_full[b, q * t:(q + 1) * t], dtype=np.float32)
        m["onehot"] = oh
        in_maps.append(m)
    return in_maps


def kernel(L0_attn_x, L0_attn_a_fwd, L0_attn_Wf1, L0_attn_bf1, L0_attn_Wf2,
           L0_attn_bf2, L0_attn_Wbx, L0_attn_bbx, L0_attn_Wgx, L0_attn_bgx,
           L0_attn_Wga, L0_attn_bga, L0_attn_a_param, L0_attn_Wb2, L0_attn_bb2,
           L0_ffn_gate_w, L0_ffn_up_w, L0_ffn_down_w, L0_ffn_s_in,
           L0_ffn_s_in_down):
    x = np.asarray(L0_attn_x, np.float32)
    wd = {
        "a_fwd": L0_attn_a_fwd, "a_param": L0_attn_a_param,
        "bf1": L0_attn_bf1, "bf2": L0_attn_bf2, "bbx": L0_attn_bbx,
        "bgx": L0_attn_bgx, "bga": L0_attn_bga, "bb2": L0_attn_bb2,
        "Wf1": L0_attn_Wf1, "Wf2": L0_attn_Wf2, "Wbx": L0_attn_Wbx,
        "Wgx": L0_attn_Wgx, "Wga": L0_attn_Wga, "Wb2": L0_attn_Wb2,
        "gate_w": L0_ffn_gate_w, "up_w": L0_ffn_up_w, "down_w": L0_ffn_down_w,
        "s_in": L0_ffn_s_in, "s_in_down": L0_ffn_s_in_down,
    }
    t = 2048
    nc = _get_prog(t)
    in_maps = make_inmaps(x, wd, t=t, ncore=8)
    res = run_bass_kernel_spmd(nc, in_maps, core_ids=list(range(8)))
    y = np.empty((2, 8192, 1024), np.float32)
    for c in range(8):
        b, q = divmod(c, 4)
        y[b, q * t:(q + 1) * t] = np.asarray(res.results[c]["y"], np.float32)
    return y


# version 04
# speedup vs baseline: 1.2833x; 1.2291x over previous
import numpy as np
import concourse.bass as bass
import concourse.mybir as mybir
from concourse.bass_utils import run_bass_kernel_spmd

F32 = mybir.dt.float32
BF16 = mybir.dt.bfloat16
FP8 = mybir.dt.float8e4
AF = mybir.ActivationFunctionType
OP = mybir.AluOpType
AX = mybir.AxisListType

D = 1024
G = 8
I_H = 4096
MI = 32
C8 = 2.0 ** -8
EPS = 1e-5


class Eng:
    def __init__(self, sem):
        self.ops = []
        self.sem = sem
        self.count = 0

    def emit(self, fn, inc=0):
        sem, n = self.sem, inc

        def op(e):
            ins = fn(e)
            if n:
                ins.then_inc(sem, n)
        self.ops.append(op)
        if inc:
            self.count += inc
        return self.count

    def wait(self, other, value=None):
        sem = other.sem
        v = other.count if value is None else value
        if v > 0:
            self.ops.append(lambda e, s=sem, vv=v: e.wait_ge(s, vv))


def build_program(t=2048, debug_out=None):
    tt = min(512, t)
    ntt = t // tt
    nlt = tt // 128
    ch = min(512, t)
    nch = t // ch
    hw_ = min(512, t)
    nh = t // hw_

    nc = bass.Bass("TRN2", target_bir_lowering=False, detect_race_conditions=False)

    x_in = nc.dram_tensor("x", [t, D], F32, kind="ExternalInput")
    y_out = nc.dram_tensor("y", [t, D], F32, kind="ExternalOutput")
    W = {}
    for nm in ("Wf1", "Wf2", "Wbx", "Wgx", "Wga", "Wb2"):
        W[nm] = nc.dram_tensor(nm, [D, D], F32, kind="ExternalInput")
    gate_w = nc.dram_tensor("gate_w", [I_H, D], F32, kind="ExternalInput")
    up_w = nc.dram_tensor("up_w", [I_H, D], F32, kind="ExternalInput")
    down_w = nc.dram_tensor("down_w", [D, I_H], F32, kind="ExternalInput")
    vecs8 = nc.dram_tensor("vecs8", [64, 128], F32, kind="ExternalInput")
    scal2 = nc.dram_tensor("scal2", [2, 1], F32, kind="ExternalInput")
    onehot = nc.dram_tensor("onehot", [128, 4], F32, kind="ExternalInput")

    qg_s = nc.dram_tensor("qg_s", [MI, G, 128, 128], BF16)
    qu_s = nc.dram_tensor("qu_s", [MI, G, 128, 128], BF16)
    qd_s = nc.dram_tensor("qd_s", [MI, G, 128, 128], BF16)
    si_s = nc.dram_tensor("si_s", [G, 128, t], F32)
    a_s = nc.dram_tensor("a_s", [G, 128, t], F32)
    cc_i = nc.dram_tensor("cc_i", [128, 24], F32)
    cc_o = nc.dram_tensor("cc_o", [4, 128, 24], F32)

    from contextlib import ExitStack
    stack = ExitStack()
    with stack:
        sbuf = lambda n_, s, d=F32: stack.enter_context(nc.sbuf_tensor(n_, s, d))

        BUFA = sbuf("BUFA", [128, t * G])
        X = sbuf("X", [128, t * G])
        BUFH = sbuf("BUFH", [128, 8192])
        WS = sbuf("WS", [128, 2048])
        WQ = sbuf("WQ", [128, 4096], BF16)
        QXK = sbuf("QXK", [128, G * 512], BF16)
        QHD = sbuf("QHD", [128, 1536], BF16)
        QTOK = sbuf("QTOK", [128, 1024], BF16)
        VQ8 = sbuf("VQ8", [128, 1024], FP8)
        AMB = sbuf("AMB", [128, 256])
        SCH = sbuf("SCH", [128, 512])
        AMH = sbuf("AMH", [128, 512])
        HSQ = sbuf("HSQ", [128, 512])
        V8T = sbuf("V8T", [128, 512], FP8)
        RRB = sbuf("RRB", [128, 512])
        S8B = sbuf("S8B", [128, 512])
        SP4 = sbuf("SP4", [128, 1024])
        IDF = sbuf("IDF", [128, 128])
        IDB = sbuf("IDB", [128, 128], BF16)
        ZRO = sbuf("ZRO", [128, 16])
        GAPB = sbuf("GAPB", [128, 8])
        STATS = sbuf("STATS", [128, 64])
        CONS = sbuf("CONS", [128, 64])
        SUM = sbuf("SUM", [128, 64])
        SCL = sbuf("SCL", [128, 32])
        EXC = sbuf("EXC", [128, 96])
        GWS = sbuf("GWS", [128, 16])

        PB = [stack.enter_context(nc.psum_tensor(f"PB{i}", [128, 512], F32))
              for i in range(8)]
        PTR = PB[6]
        PTB = PB[7].ap().bitcast(BF16)

        sems = {k: stack.enter_context(nc.semaphore("s_" + k))
                for k in ("sy", "gp", "gpc", "v", "s", "pe", "cc")}
        SY, GP, V, S, PE = (Eng(sems[k]) for k in ("sy", "gp", "v", "s", "pe"))
        GPC = Eng(sems["gpc"])
        GPC.ops = GP.ops
        CCSEM = sems["cc"]

        FQ0 = BUFH[:, 4096:5120]
        FQ1 = BUFH[:, 5120:6144]
        FQ2 = BUFH[:, 6144:7168]
        SQS = BUFH[:, 7168:8192]
        QXF = QXK.ap().bitcast(F32)
        HQ0, HQ1, HQ2 = SP4[:, 0:512], SP4[:, 512:1024], HSQ

        def dma(eng, out, in_):
            return eng.emit(lambda e, o=out, i=in_: e.dma_start(out=o, in_=i),
                            inc=16)

        _vw = []

        def _rngs(aps):
            out = []
            for ap in aps:
                if ap is None or isinstance(ap, (int, float)):
                    continue
                try:
                    nm = ap.tensor.name
                    off = int(ap.offset)
                    sh = list(ap.shape)
                    try:
                        stv = list(ap.stride)
                    except Exception:
                        stv = None
                    if stv and len(stv) == len(sh) and len(sh) > 1:
                        span = 1 + sum((int(d) - 1) * abs(int(s))
                                       for d, s in zip(sh[1:], stv[1:]))
                    else:
                        span = 1
                        for d in sh[1:]:
                            span *= int(d)
                    out.append((nm, off, off + max(span, 1)))
                except Exception:
                    out.append(("?", 0, 1 << 62))
            return out

        def _fsz(ap):
            try:
                return int(ap.free_size)
            except Exception:
                pass
            try:
                n = 1
                for d in list(ap.shape)[1:]:
                    n *= int(d)
                return n
            except Exception:
                return 1

        def _vsmall(aps):
            return all(_fsz(ap) <= 128 for ap in aps)

        def _vraw(outs, ins):
            rd = _rngs(ins)
            need = 0
            for age, (small, wr) in enumerate(_vw):
                if not small:
                    continue
                for nm, lo, hi in wr:
                    for nm2, lo2, hi2 in rd:
                        if ((nm == nm2 or nm == "?" or nm2 == "?")
                                and lo < hi2 and lo2 < hi):
                            need = max(need, 2 - age)
            for _ in range(need):
                V.emit(lambda e: e.tensor_copy(GAPB[:, 0:8], ZRO[:, 0:8]),
                       inc=0)
                _vw.insert(0, (False, []))
            _vw.insert(0, (_vsmall(outs), _rngs(outs)))
            del _vw[2:]

        def tsc(eng, out, in0, s1, s2=None, op0=OP.mult, op1=None):
            if eng is V:
                _vraw([out], [in0, s1, s2])
            if op1 is None:
                return eng.emit(lambda e, o=out, i=in0, a=s1, p0=op0:
                                e.tensor_scalar(o, i, a, None, p0), inc=1)
            return eng.emit(lambda e, o=out, i=in0, a=s1, b=s2, p0=op0, p1=op1:
                            e.tensor_scalar(o, i, a, b, p0, p1), inc=1)

        def tt_(eng, out, a, b, op):
            if eng is V:
                _vraw([out], [a, b])
            return eng.emit(lambda e, o=out, x=a, y=b, p=op:
                            e.tensor_tensor(o, x, y, p), inc=1)

        def stt(eng, out, in0, sc, in1, op0, op1):
            if eng is V:
                _vraw([out], [in0, sc, in1])
            return eng.emit(lambda e, o=out, i0=in0, s=sc, i1=in1, p0=op0, p1=op1:
                            e.scalar_tensor_tensor(o, i0, s, i1, p0, p1), inc=1)

        def act(eng, out, in_, fn, bias=0.0, scale=1.0, accum=None):
            return eng.emit(lambda e, o=out, i=in_, f=fn, b=bias, s=scale, a=accum:
                            e.activation(o, i, f, bias=b, scale=s, accum_out=a),
                            inc=1)

        def cp(eng, out, in_):
            if eng is V:
                _vraw([out], [in_])
            return eng.emit(lambda e, o=out, i=in_: e.tensor_copy(o, i), inc=1)

        def recip(eng, out, in_):
            if eng is V:
                _vraw([out], [in_])
                _vw[0] = (True, _vw[0][1])
            return eng.emit(lambda e, o=out, i=in_: e.reciprocal(o, i), inc=1)

        def red(eng, out, in_, op, absval=False):
            if eng is V:
                _vraw([out], [in_])
            return eng.emit(lambda e, o=out, i=in_, p=op, a=absval:
                            e.tensor_reduce(o, i, AX.X, p, apply_absolute_value=a),
                            inc=1)

        def mm(out, lhsT, rhs, start, stop, inc=0):
            return PE.emit(lambda e, o=out, l=lhsT, r=rhs, a=start, b=stop:
                           e.matmul(o, l, r, start=a, stop=b), inc=inc)

        def tr(out, in_, ident):
            return PE.emit(lambda e, o=out, i=in_, d=ident:
                           e.transpose(o, i, d), inc=1)

        def scan(eng, out, d0, d1, init):
            if eng is V:
                _vraw([out], [d0, d1, init])
            return eng.emit(lambda e, o=out, a=d0, b=d1, i=init:
                            e.tensor_tensor_scan(o, a, b, i, OP.mult, OP.add),
                            inc=1)

        dma(SY, SP4[:64, :128], vecs8.ap())
        dma(SY, SCL[:, 0:1], scal2[0:1, :].broadcast_to([128, 1]))
        dma(SY, SCL[:, 1:2], scal2[1:2, :].broadcast_to([128, 1]))
        dma(SY, SCL[:, 2:6], onehot.ap())
        GPC.emit(lambda e: e.iota(IDF[:, :], [[1, 128]], channel_multiplier=-1,
                                  allow_small_or_imprecise_dtypes=True), inc=1)
        GPC.emit(lambda e: e.memset(ZRO[:, :], 0.0), inc=1)
        GPC.emit(lambda e: e.memset(SP4[:, 512:640], 1.0), inc=1)
        V.wait(GPC)
        tsc(V, SCL[:, 31:32], ZRO[:, 0:1], 1.0, None, OP.add)
        ONECOL = SCL[:, 31:32]
        tsc(V, IDB[:, :], IDF[:, :], 0.0, None, OP.is_equal)
        tsc(V, IDF[:, :], IDF[:, :], 0.0, None, OP.is_equal)
        PE.wait(SY)
        PE.wait(V)
        tr(PTR[:, :64], SP4[:64, :128], IDF[:64, :64])
        S.wait(PE)
        act(S, CONS[:, :], PTR[:128, :64], AF.Identity)
        V.wait(S)
        AFWD = CONS[:, 0:8]
        S.wait(V)
        act(S, SUM[:, 0:8], CONS[:, 8:16], AF.Exp)
        V.wait(S)
        tsc(V, SUM[:, 0:8], SUM[:, 0:8], 1.0, None, OP.add)
        S.wait(V)
        act(S, SUM[:, 0:8], SUM[:, 0:8], AF.Ln)
        act(S, SUM[:, 16:24], CONS[:, 0:8], AF.Ln)
        V.wait(S)
        tsc(V, SUM[:, 8:16], SUM[:, 0:8], -8.0)
        tsc(V, SCL[:, 30:31], ZRO[:, 0:1], EPS, None, OP.add)
        M8SP = SUM[:, 8:16]
        LNAF = SUM[:, 16:24]
        cp(V, SUM[:, 24:32], CONS[:, 0:8])
        for _ in range(int(round(np.log2(t)))):
            tt_(V, SUM[:, 24:32], SUM[:, 24:32], SUM[:, 24:32], OP.mult)
        ACORE = SUM[:, 24:32]

        def gate_rows_ap(src, it):
            b0 = 2048 * (it // 16) + (it % 16)
            return bass.AP(src, b0 * D, [[16 * D, 128], [1, D]])

        def global_absmax(src_ap_fn, ntile, npiece, out_col):
            tt_(V, GWS[:, out_col:out_col + 1], ZRO[:, 0:1], ZRO[:, 0:1], OP.add)
            for it in range(ntile):
                for piece in range(npiece):
                    j = (it * npiece + piece) % 2
                    dst = WS[:, j * 1024:(j + 1) * 1024]
                    SY.wait(V)
                    dma(SY, dst, src_ap_fn(it, piece))
                    V.wait(SY)
                    red(V, GWS[:, out_col + 1:out_col + 2], dst, OP.max,
                        absval=True)
                    tt_(V, GWS[:, out_col:out_col + 1],
                        GWS[:, out_col:out_col + 1],
                        GWS[:, out_col + 1:out_col + 2], OP.max)
            PE.wait(V)
            PE.wait(S)
            tr(PTR[:1, 0:128], GWS[:, out_col:out_col + 1], IDF[:, :])
            S.wait(PE)
            act(S, AMB[:1, 128:256], PTR[:1, 0:128], AF.Identity)
            V.wait(S)
            red(V, GWS[:1, out_col:out_col + 1], AMB[:1, 128:256], OP.max)
            PE.wait(V)
            mm(PTR[:, 128:129], SP4[:1, 512:640], GWS[:1, out_col:out_col + 1],
               True, True, inc=1)
            V.wait(PE)
            cp(V, GWS[:, out_col:out_col + 1], PTR[:, 128:129])

        global_absmax(lambda it, p: gate_rows_ap(gate_w, it), MI, 1, 0)
        global_absmax(lambda it, p: gate_rows_ap(up_w, it), MI, 1, 4)
        global_absmax(lambda it, p: down_w.ap()[it * 128:(it + 1) * 128,
                                                p * 1024:(p + 1) * 1024],
                      G, 4, 8)
        for col in (0, 4, 8):
            recip(V, GWS[:, col + 2:col + 3], GWS[:, col:col + 1])
        cp(V, SP4[:, 0:128], IDF[:, :])
        cp(V, SP4[:, 128:256], IDF[:, :])
        for j, col in enumerate((0, 4, 8)):
            tsc(V, SCL[:, 6 + j:7 + j], GWS[:, col + 2:col + 3], 2688.0)
        tt_(V, SCL[:, 16:17], SCL[:, 0:1], SCL[:, 6:7], OP.mult)
        tt_(V, SCL[:, 17:18], SCL[:, 0:1], SCL[:, 7:8], OP.mult)
        tt_(V, SCL[:, 18:19], SCL[:, 1:2], SCL[:, 8:9], OP.mult)
        for j in range(3):
            recip(V, SCL[:, 19 + j:20 + j], SCL[:, 16 + j:17 + j])
        cp(V, SP4[:, 0:128], IDF[:, :])
        cp(V, SP4[:, 128:256], IDF[:, :])
        cp(V, SCL[:, 9:10], SCL[:, 19:20])
        cp(V, SCL[:, 10:11], SCL[:, 20:21])
        tsc(V, SCL[:, 11:12], SCL[:, 21:22], 0.5)
        tsc(V, SCL[:, 12:13], SCL[:, 1:2], 1.0 / 12.0)
        tsc(V, SCL[:, 13:14], SCL[:, 1:2], 0.5)
        tsc(V, SCL[:, 24:25], SCL[:, 6:7], 1.0 / 12.0)
        tsc(V, SCL[:, 26:27], SCL[:, 6:7], 0.5)
        tsc(V, SCL[:, 25:26], SCL[:, 7:8], 1.0 / 12.0)
        tsc(V, SCL[:, 27:28], SCL[:, 7:8], 0.5)
        tsc(V, SCL[:, 28:29], SCL[:, 8:9], 1.0 / 12.0)
        tsc(V, SCL[:, 29:30], SCL[:, 8:9], 0.5)
        GX = SCL[:, 0:1]
        INVG1, INVG2, HINVD = SCL[:, 9:10], SCL[:, 10:11], SCL[:, 11:12]
        GD12, GD2 = SCL[:, 12:13], SCL[:, 13:14]

        def fp4(src, width, rr, s8, t0, t1, t2, outq, blk16, vq):
            nb = width // 16
            if blk16:
                tt_(V, t0[:, :width].rearrange("p (b k) -> p b k", k=16),
                    src.rearrange("p (b k) -> p b k", k=16),
                    rr.unsqueeze(2).broadcast_to([128, nb, 16]), OP.mult)
            else:
                tt_(V, t0[:, :width], src, rr, OP.mult)
            tsc(V, t0[:, :width], t0[:, :width], 6.5, -6.5, OP.min, OP.max)
            tsc(V, t1[:, :width], t0[:, :width], 2.0, 0.0, OP.subtract, OP.max)
            tsc(V, t2[:, :width], t0[:, :width], -2.0, 0.0, OP.subtract, OP.min)
            tt_(V, t1[:, :width], t1[:, :width], t2[:, :width], OP.add)
            tsc(V, t2[:, :width], t0[:, :width], 4.0, 0.0, OP.subtract, OP.max)
            stt(V, t2[:, :width], t2[:, :width], -0.25, t0[:, :width],
                OP.mult, OP.add)
            tsc(V, t0[:, :width], t0[:, :width], -4.0, 0.0, OP.subtract, OP.min)
            stt(V, t0[:, :width], t0[:, :width], -0.25, t2[:, :width],
                OP.mult, OP.add)
            stt(V, t0[:, :width], t1[:, :width], -0.5, t0[:, :width],
                OP.mult, OP.add)
            tsc(V, vq[:, :width], t0[:, :width], C8)
            tsc(V, t1[:, :width], vq[:, :width], 2 * C8, 0.0, OP.subtract, OP.max)
            tsc(V, t2[:, :width], vq[:, :width], -2 * C8, 0.0, OP.subtract, OP.min)
            tt_(V, t1[:, :width], t1[:, :width], t2[:, :width], OP.add)
            tsc(V, t2[:, :width], vq[:, :width], 3 * C8, 0.0, OP.subtract, OP.max)
            stt(V, t2[:, :width], t2[:, :width], 2.0, t1[:, :width],
                OP.mult, OP.add)
            tsc(V, t1[:, :width], vq[:, :width], -3 * C8, 0.0, OP.subtract,
                OP.min)
            stt(V, t1[:, :width], t1[:, :width], 2.0, t2[:, :width],
                OP.mult, OP.add)
            tt_(V, t1[:, :width], vq[:, :width], t1[:, :width], OP.add)
            if blk16:
                tt_(V, outq.rearrange("p (b k) -> p b k", k=16),
                    t1[:, :width].rearrange("p (b k) -> p b k", k=16),
                    s8.unsqueeze(2).broadcast_to([128, nb, 16]), OP.mult)
            else:
                tt_(V, outq, t1[:, :width], s8, OP.mult)

        def sc_stage(amax_ap, n, mul_ap, half_ap, rr_out, s8_out, v8t):
            tt_(V, rr_out[:, :n], amax_ap, mul_ap.broadcast_to([128, n]), OP.mult)
            tsc(V, rr_out[:, :n], rr_out[:, :n], 1.0, 224.0, OP.mult, OP.min)
            cp(V, v8t[:, :n], rr_out[:, :n])
            recip(V, rr_out[:, :n], v8t[:, :n])
            tsc(V, s8_out[:, :n], v8t[:, :n], 512.0)
            cp(V, STATS[:, 32:64], SUM[:, 0:32])
            tsc(V, rr_out[:, :n], rr_out[:, :n], 1e30, None, OP.min)
            tt_(V, rr_out[:, :n], rr_out[:, :n],
                half_ap.broadcast_to([128, n]), OP.mult)

        def wprep_gateup(src, dst, which):
            mul_ap = SCL[:, 24 + which:25 + which]
            half_ap = SCL[:, 26 + which:27 + which]
            for pair in range(MI // 2):
                its = (2 * pair, 2 * pair + 1)
                for j, it in enumerate(its):
                    SY.wait(V)
                    SY.wait(PE)
                    dma(SY, WS[:, j * 1024:(j + 1) * 1024], gate_rows_ap(src, it))
                V.wait(SY)
                for j in range(2):
                    red(V, AMB[:, j * 64:(j + 1) * 64],
                        WS[:, j * 1024:(j + 1) * 1024]
                        .rearrange("p (b k) -> p b k", k=16), OP.max, absval=True)
                sc_stage(AMB[:, :128], 128, mul_ap, half_ap, RRB, S8B, V8T)
                for j, it in enumerate(its):
                    fp4(WS[:, j * 1024:(j + 1) * 1024], 1024,
                        RRB[:, j * 64:(j + 1) * 64], S8B[:, j * 64:(j + 1) * 64],
                        FQ0, FQ1, FQ2, QTOK[:, :1024], True, VQ8)
                    for g in range(G):
                        slot = g % 8
                        PE.wait(V)
                        PE.wait(S)
                        tr(PTB[:, slot * 128:slot * 128 + 128],
                           QTOK[:, g * 128:(g + 1) * 128], IDB[:, :])
                        S.wait(PE)
                        act(S, WQ[:, slot * 128:slot * 128 + 128],
                            PTB[:, slot * 128:slot * 128 + 128], AF.Identity)
                        GP.wait(S)
                        dma(GP, dst.ap()[it, g, :, :],
                            WQ[:, slot * 128:slot * 128 + 128])

        wprep_gateup(gate_w, qg_s, 0)
        wprep_gateup(up_w, qu_s, 1)

        def wprep_down():
            for it in range(G):
                for half in range(2):
                    SY.wait(V)
                    SY.wait(PE)
                    dma(SY, WS[:, :2048],
                        down_w.ap()[it * 128:(it + 1) * 128,
                                    half * 2048:(half + 1) * 2048])
                    V.wait(SY)
                    red(V, AMB[:, 0:128],
                        WS[:, :2048].rearrange("p (b k) -> p b k", k=16),
                        OP.max, absval=True)
                    sc_stage(AMB[:, :128], 128, SCL[:, 28:29], SCL[:, 29:30],
                             RRB, S8B, V8T)
                    for j in range(2):
                        fp4(WS[:, j * 1024:(j + 1) * 1024], 1024,
                            RRB[:, j * 64:(j + 1) * 64],
                            S8B[:, j * 64:(j + 1) * 64],
                            FQ0, FQ1, FQ2, QTOK[:, :1024], True, VQ8)
                        cp(V, WQ[:, (2 + j) * 1024:(3 + j) * 1024],
                           QTOK[:, :1024])
                    for kk in range(16):
                        tau = half * 16 + kk
                        slot = kk % 8
                        PE.wait(V)
                        PE.wait(S)
                        tr(PTB[:, slot * 128:slot * 128 + 128],
                           WQ[:, 2048 + kk:4096:16], IDB[:, :])
                        S.wait(PE)
                        act(S, WQ[:, slot * 128:slot * 128 + 128],
                            PTB[:, slot * 128:slot * 128 + 128], AF.Identity)
                        GP.wait(S)
                        dma(GP, qd_s.ap()[tau, it, :, :],
                            WQ[:, slot * 128:slot * 128 + 128])

        wprep_down()
        GP.ops.append(lambda e, s=GP.sem, v=GP.count: e.wait_ge(s, v))

        def ln_tile(xtok, c, gmul=None):
            red(V, STATS[:, c:c + 1], xtok, OP.add)
            S.wait(V)
            act(S, SQS, xtok, AF.Square, accum=STATS[:, 8 + c:9 + c])
            V.wait(S)
            tsc(V, STATS[:, 16 + c:17 + c], STATS[:, c:c + 1], 1.0 / D)
            tsc(V, STATS[:, 20 + c:21 + c], STATS[:, 8 + c:9 + c], 1.0 / D)
            tt_(V, STATS[:, 24 + c:25 + c], STATS[:, 16 + c:17 + c],
                STATS[:, 16 + c:17 + c], OP.mult)
            tt_(V, STATS[:, 20 + c:21 + c], STATS[:, 20 + c:21 + c],
                STATS[:, 24 + c:25 + c], OP.subtract)
            S.wait(V)
            act(S, STATS[:, 24 + c:25 + c], STATS[:, 20 + c:21 + c], AF.Sqrt,
                bias=SCL[:, 30:31])
            V.wait(S)
            recip(V, STATS[:, 28 + c:29 + c], STATS[:, 24 + c:25 + c])
            cp(V, STATS[:, 32:48], SUM[:, 0:16])
            cp(V, STATS[:, 48:64], SUM[:, 16:32])
            if gmul is not None:
                tt_(V, STATS[:, 28 + c:29 + c], STATS[:, 28 + c:29 + c],
                    gmul, OP.mult)
            tt_(V, STATS[:, 24 + c:25 + c], STATS[:, 16 + c:17 + c],
                STATS[:, 28 + c:29 + c], OP.mult)
            tsc(V, STATS[:, 24 + c:25 + c], STATS[:, 24 + c:25 + c], -1.0)
            return STATS[:, 28 + c:29 + c], STATS[:, 24 + c:25 + c]

        def transpose_fm_to_tok(tok0, dst):
            for g in range(G):
                sl = g % 4
                if sl == 0:
                    PE.wait(S)
                    PE.wait(V)
                tr(PTR[:, sl * 128:sl * 128 + 128],
                   X[:, g * t + tok0:g * t + tok0 + 128], IDF[:, :])
                if sl == 3:
                    S.wait(PE)
                    act(S, dst[:, (g - 3) * 128:(g + 1) * 128], PTR[:, :512],
                        AF.Identity)

        def build_x():
            for lt in range(t // 128):
                tok0 = lt * 128
                SY.wait(V)
                SY.wait(PE)
                dma(SY, SP4[:, :], x_in.ap()[tok0:tok0 + 128, :])
                PE.wait(SY)
                for g in range(G):
                    sl = g % 4
                    if sl == 0:
                        PE.wait(S)
                    tr(PTR[:, sl * 128:sl * 128 + 128],
                       SP4[:, g * 128:(g + 1) * 128], IDF[:, :])
                    if sl == 3:
                        S.wait(PE)
                        for gg in range(g - 3, g + 1):
                            act(S, X[:, gg * t + tok0:gg * t + tok0 + 128],
                                PTR[:, (gg % 4) * 128:(gg % 4) * 128 + 128],
                                AF.Identity)

        def ffn(sub):
            for it in range(ntt):
                for lt in range(nlt):
                    tok0 = it * tt + lt * 128
                    if sub == 1:
                        SY.wait(V)
                        SY.wait(S)
                        dma(SY, SP4[:, :], x_in.ap()[tok0:tok0 + 128, :])
                        V.wait(SY)
                        S.wait(SY)
                    else:
                        transpose_fm_to_tok(tok0, SP4)
                        V.wait(S)
                    scale, bias = ln_tile(SP4[:, :], lt, gmul=GX)
                    S.wait(V)
                    act(S, BUFH[:, lt * 1024:(lt + 1) * 1024], SP4[:, :],
                        AF.Identity, bias=bias, scale=scale)
                    V.wait(S)
                    red(V, AMB[:, lt * 64:(lt + 1) * 64],
                        BUFH[:, lt * 1024:(lt + 1) * 1024]
                        .rearrange("p (b k) -> p b k", k=16), OP.max, absval=True)
                nbt = nlt * 64
                tsc(V, SCH[:, :nbt], AMB[:, :nbt], 1.0 / 12.0, 224.0,
                    OP.mult, OP.min)
                cp(V, V8T[:, :nbt], SCH[:, :nbt])
                recip(V, SCH[:, :nbt], V8T[:, :nbt])
                tsc(V, S8B[:, :nbt], V8T[:, :nbt], 512.0)
                cp(V, STATS[:, 32:64], SUM[:, 0:32])
                tsc(V, RRB[:, :nbt], SCH[:, :nbt], 1e30, 0.5, OP.min, OP.mult)
                for lt in range(nlt):
                    fp4(BUFH[:, lt * 1024:(lt + 1) * 1024], 1024,
                        RRB[:, lt * 64:(lt + 1) * 64],
                        S8B[:, lt * 64:(lt + 1) * 64],
                        FQ0, FQ1, FQ2, QTOK[:, :1024], True, VQ8)
                    for g in range(G):
                        slot = g % 8
                        PE.wait(V)
                        PE.wait(S)
                        tr(PTB[:, slot * 128:slot * 128 + 128],
                           QTOK[:, g * 128:(g + 1) * 128], IDB[:, :])
                        S.wait(PE)
                        act(S, QXK[:, g * tt + lt * 128:g * tt + lt * 128 + 128],
                            PTB[:, slot * 128:slot * 128 + 128], AF.Identity)
                for hh in range(2):
                    gu_pe, gu_v, gu_s = [], [], []
                    for m16 in range(16):
                        m = hh * 16 + m16
                        bg, bu = PB[2 * (m % 2)], PB[2 * (m % 2) + 1]
                        wql = WQ[:, (m % 2) * 2048:(m % 2) * 2048 + 1024]
                        wqu = WQ[:, (m % 2) * 2048 + 1024:(m % 2) * 2048 + 2048]
                        if m16 < 2:
                            SY.wait(PE)
                            GP.wait(PE)
                        else:
                            SY.wait(PE, gu_pe[m16 - 2])
                            GP.wait(PE, gu_pe[m16 - 2])
                        dma(SY, wql.rearrange("p (g c) -> p g c", g=G),
                            qg_s.ap()[m, :, :, :].rearrange("g p c -> p g c"))
                        dma(GP, wqu.rearrange("p (g c) -> p g c", g=G),
                            qu_s.ap()[m, :, :, :].rearrange("g p c -> p g c"))
                        PE.wait(SY)
                        PE.wait(GP)
                        if m16 < 2:
                            PE.wait(V)
                            PE.wait(S)
                        else:
                            PE.wait(V, gu_v[m16 - 2])
                            PE.wait(S, gu_s[m16 - 2])
                        for k in range(G):
                            mm(bg[:, :tt], wql[:, k * 128:(k + 1) * 128],
                               QXK[:, k * tt:k * tt + tt], k == 0, k == G - 1,
                               inc=(1 if k == G - 1 else 0))
                        for k in range(G):
                            mm(bu[:, :tt], wqu[:, k * 128:(k + 1) * 128],
                               QXK[:, k * tt:k * tt + tt], k == 0, k == G - 1,
                               inc=(1 if k == G - 1 else 0))
                        gu_pe.append(PE.count)
                        S.wait(PE)
                        sgt = SP4[:, :512] if m % 2 == 0 else SP4[:, 512:1024]
                        act(S, sgt[:, :tt], bg[:, :tt], AF.Silu, scale=INVG1)
                        V.wait(S)
                        hdst = BUFH[:, m16 * 512:m16 * 512 + tt]
                        stt(V, hdst, bu[:, :tt], INVG2, sgt[:, :tt],
                            OP.mult, OP.mult)
                        gu_v.append(V.count)
                        S.wait(V)
                        act(S, HSQ[:, :tt], hdst, AF.Square)
                        gu_s.append(S.count)
                        V.wait(S)
                        if m16 == 0:
                            cp(V, AMH[:, :tt], HSQ[:, :tt])
                        else:
                            tt_(V, AMH[:, :tt], AMH[:, :tt], HSQ[:, :tt], OP.max)
                    S.wait(V)
                    act(S, AMH[:, :tt], AMH[:, :tt], AF.Sqrt)
                    V.wait(S)
                    sc_stage(AMH[:, :tt], tt, GD12, GD2, SCH, S8B, V8T)
                    dn_pe = []
                    for m16 in range(16):
                        k = hh * 16 + m16
                        qslot = QHD[:, (m16 % 3) * 512:(m16 % 3) * 512 + tt]
                        if m16 >= 2:
                            V.wait(PE)
                        fp4(BUFH[:, m16 * 512:m16 * 512 + tt], tt,
                            SCH[:, :tt], S8B[:, :tt],
                            HQ0, HQ1, HQ2, qslot, False, VQ8)
                        wqd = WQ[:, (m16 % 2) * 1024:(m16 % 2) * 1024 + 1024]
                        wq_q = SY if m16 % 2 == 0 else GP
                        if m16 < 2:
                            wq_q.wait(PE)
                        else:
                            wq_q.wait(PE, dn_pe[m16 - 2])
                        dma(wq_q, wqd.rearrange("p (g c) -> p g c", g=G),
                            qd_s.ap()[k, :, :, :].rearrange("g p c -> p g c"))
                        PE.wait(wq_q)
                        PE.wait(V)
                        for m in range(G):
                            mm(PB[m][:, :tt], wqd[:, m * 128:(m + 1) * 128],
                               qslot, m16 == 0, m16 == 15,
                               inc=(1 if m == G - 1 else 0))
                        dn_pe.append(PE.count)
                    V.wait(PE)
                    for m in range(G):
                        xsl = X[:, m * t + it * tt:m * t + it * tt + tt]
                        stt(V, xsl, PB[m][:, :tt], HINVD, xsl, OP.mult, OP.add)

        def attn():
            LGA = QXF[:, 0:512]
            PEX = QXF[:, 512:1024]
            WEX = QXF[:, 1024:1536]
            TM0 = QXF[:, 1536:2048]
            SIT = QTOK.ap().bitcast(F32)
            AT = AMH
            XG = BUFH[:, 0:G * 512]
            VC = BUFH[:, G * 512:2 * G * 512]
            U = BUFA

            GPC.wait(V)
            GPC.wait(S)
            GPC.emit(lambda e: e.iota(RRB[:, :], [[1, 512]],
                                      channel_multiplier=0,
                                      allow_small_or_imprecise_dtypes=True),
                     inc=1)
            S.wait(GPC)

            cp(V, SUM[:, 32:48], ZRO[:, 0:16])
            for c in range(nch):
                ct0 = c * ch
                for lt in range(ch // 128):
                    tok0 = ct0 + lt * 128
                    transpose_fm_to_tok(tok0, SP4)
                    V.wait(S)
                    scale, bias = ln_tile(SP4[:, :], lt % 4)
                    S.wait(V)
                    act(S, SQS, SP4[:, :], AF.Identity, bias=bias, scale=scale)
                    PE.wait(S)
                    for g in range(G):
                        sl = g % 4
                        if sl == 0:
                            PE.wait(S)
                            PE.wait(V)
                        tr(PTR[:, sl * 128:sl * 128 + 128],
                           SQS[:, g * 128:(g + 1) * 128], IDF[:, :])
                        if sl == 3:
                            q0 = (g // 4) * 512
                            S.wait(PE)
                            act(S, SP4[:, q0:q0 + 512], PTR[:, :512],
                                AF.Identity)
                            V.wait(S)
                            for gg in range(g - 3, g + 1):
                                pseg = SP4[:, q0 + (gg % 4) * 128:
                                            q0 + (gg % 4) * 128 + 128]
                                act(S, XG[:, gg * ch + lt * 128:
                                          gg * ch + lt * 128 + 128],
                                    pseg, AF.Gelu_apprx_tanh)
                                xsl = X[:, gg * t + tok0:gg * t + tok0 + 128]
                                tt_(V, xsl, xsl, pseg, OP.add)
                for wname, bcol, isU in (("Wf1", 2, True), ("Wbx", 4, False)):
                    wf_pe = []
                    for k in range(G):
                        wq_q = SY if k % 2 == 0 else GP
                        if k < 2:
                            wq_q.wait(PE)
                        else:
                            wq_q.wait(PE, wf_pe[k - 2])
                        dma(wq_q, WS[:, (k % 2) * 1024:(k % 2) * 1024 + 1024],
                            W[wname].ap()[k * 128:(k + 1) * 128, :])
                        PE.wait(wq_q)
                        if k == 0:
                            PE.wait(S)
                            PE.wait(V)
                        for m in range(G):
                            mm(PB[m][:, :ch],
                               WS[:, (k % 2) * 1024 + m * 128:
                                  (k % 2) * 1024 + m * 128 + 128],
                               XG[:, k * ch:(k + 1) * ch], k == 0, k == G - 1,
                               inc=(1 if m == G - 1 else 0))
                        wf_pe.append(PE.count)
                    S.wait(PE)
                    for m in range(G):
                        if isU:
                            dst = U[:, m * t + ct0:m * t + ct0 + ch]
                        else:
                            dst = VC[:, m * ch:(m + 1) * ch]
                        act(S, dst, PB[m][:, :ch], AF.Identity,
                            bias=CONS[:, bcol * 8 + m:bcol * 8 + m + 1])
                for g in range(G):
                    for wname, bcol, sdst in (("Wgx", 5, SP4[:, 0:512]),
                                              ("Wga", 6, SP4[:, 512:1024])):
                        wg_pe = []
                        for k in range(G):
                            wq_q = SY if k % 2 == 0 else GP
                            if k < 2:
                                wq_q.wait(PE)
                            else:
                                wq_q.wait(PE, wg_pe[k - 2])
                            dma(wq_q, WS[:, (k % 2) * 128:(k % 2) * 128 + 128],
                                W[wname].ap()[k * 128:(k + 1) * 128,
                                              g * 128:(g + 1) * 128])
                            PE.wait(wq_q)
                            if k == 0:
                                PE.wait(S)
                            mm(PB[4][:, :ch],
                               WS[:, (k % 2) * 128:(k % 2) * 128 + 128],
                               VC[:, k * ch:(k + 1) * ch], k == 0, k == G - 1,
                               inc=1)
                            wg_pe.append(PE.count)
                        S.wait(PE)
                        act(S, sdst[:, :ch], PB[4][:, :ch], AF.Sigmoid,
                            bias=CONS[:, bcol * 8 + g:bcol * 8 + g + 1])
                    V.wait(S)
                    tsc(V, LGA[:, :ch], SP4[:, 512:512 + ch], M8SP[:, g:g + 1])
                    S.wait(V)
                    act(S, AT[:, :ch], LGA[:, :ch], AF.Exp)
                    act(S, WEX[:, :ch], LGA[:, :ch], AF.Exp, scale=2.0)
                    GP.wait(S)
                    dma(GP, a_s.ap()[g, :, ct0:ct0 + ch], AT[:, :ch])
                    V.wait(S)
                    tsc(V, WEX[:, :ch], WEX[:, :ch], -1.0, 1.0, OP.mult, OP.add)
                    tsc(V, WEX[:, :ch], WEX[:, :ch], 0.0, None, OP.max)
                    S.wait(V)
                    act(S, TM0[:, :ch], WEX[:, :ch], AF.Sqrt)
                    V.wait(S)
                    tt_(V, SIT[:, :ch], SP4[:, 0:ch], VC[:, g * ch:g * ch + ch],
                        OP.mult)
                    tt_(V, SIT[:, :ch], SIT[:, :ch], TM0[:, :ch], OP.mult)
                    GP.wait(V)
                    dma(GP, si_s.ap()[g, :, ct0:ct0 + ch], SIT[:, :ch])
                    scan(V, PEX[:, :ch], ONECOL.broadcast_to([128, ch]),
                         LGA[:, :ch], 0.0)
                    tt_(V, PEX[:, :ch], PEX[:, :ch], LGA[:, :ch], OP.subtract)
                    S.wait(V)
                    act(S, WEX[:, :ch], PEX[:, :ch], AF.Exp,
                        bias=SUM[:, 32 + g:33 + g])
                    V.wait(S)
                    tt_(V, TM0[:, :ch], WEX[:, :ch], SIT[:, :ch], OP.mult)
                    red(V, STATS[:, 0:1], TM0[:, :ch], OP.add)
                    tt_(V, SUM[:, 40 + g:41 + g], SUM[:, 40 + g:41 + g],
                        STATS[:, 0:1], OP.add)
                    red(V, STATS[:, 1:2], LGA[:, :ch], OP.add)
                    tt_(V, SUM[:, 32 + g:33 + g], SUM[:, 32 + g:33 + g],
                        STATS[:, 1:2], OP.add)
            for g in range(G):
                for h2 in range(nh):
                    tsc(V, STATS[:, 2:3], LNAF[:, g:g + 1],
                        float(t - 1 - hw_ * h2))
                    tsc(V, STATS[:, 3:4], LNAF[:, g:g + 1], -1.0)
                    S.wait(V)
                    act(S, WEX[:, :hw_], RRB[:, :hw_], AF.Exp,
                        bias=STATS[:, 2:3], scale=STATS[:, 3:4])
                    V.wait(S)
                    tt_(V, TM0[:, :hw_],
                        U[:, g * t + h2 * hw_:g * t + h2 * hw_ + hw_],
                        WEX[:, :hw_], OP.mult)
                    red(V, STATS[:, 4 + h2:5 + h2], TM0[:, :hw_], OP.add)
                for h2 in range(1, nh):
                    tt_(V, STATS[:, 4:5], STATS[:, 4:5],
                        STATS[:, 4 + h2:5 + h2], OP.add)
                cp(V, SUM[:, 48 + g:49 + g], STATS[:, 4:5])
            S.wait(V)
            act(S, SUM[:, 56:64], SUM[:, 32:40], AF.Exp)
            V.wait(S)
            cp(V, EXC[:, 0:8], SUM[:, 48:56])
            cp(V, EXC[:, 8:16], SUM[:, 40:48])
            cp(V, EXC[:, 16:24], SUM[:, 56:64])
            GP.wait(V)
            dma(GP, cc_i.ap(), EXC[:, 0:24])
            GP.ops.append(lambda e, s=GP.sem, v=GP.count: e.wait_ge(s, v))

            def _cc(e):
                e.collective_compute(
                    "AllGather", OP.bypass,
                    replica_groups=[[0, 1, 2, 3], [4, 5, 6, 7]],
                    ins=[cc_i.ap().opt()], outs=[cc_o.ap().opt()]
                ).then_inc(CCSEM, 1)
            GP.ops.append(_cc)
            GP.ops.append(lambda e: e.wait_ge(CCSEM, 1))
            dma(GP, EXC[:, 0:96].rearrange("p (q c) -> p q c", q=4),
                cc_o.ap().rearrange("q p c -> p q c"))
            V.wait(GP)
            cp(V, STATS[:, 0:8], ZRO[:, 0:8])
            cp(V, STATS[:, 8:16], ZRO[:, 0:8])
            cp(V, STATS[:, 16:24], ZRO[:, 0:8])
            for qq in range(4):
                stt(V, STATS[:, 8:16], STATS[:, 0:8], SCL[:, 2 + qq:3 + qq],
                    STATS[:, 8:16], OP.mult, OP.add)
                tt_(V, STATS[:, 0:8], STATS[:, 0:8], ACORE, OP.mult)
                tt_(V, STATS[:, 0:8], STATS[:, 0:8],
                    EXC[:, qq * 24:qq * 24 + 8], OP.add)
            cp(V, STATS[:, 0:8], ZRO[:, 0:8])
            for qq in range(3, -1, -1):
                stt(V, STATS[:, 16:24], STATS[:, 0:8], SCL[:, 2 + qq:3 + qq],
                    STATS[:, 16:24], OP.mult, OP.add)
                tt_(V, STATS[:, 0:8], STATS[:, 0:8],
                    EXC[:, qq * 24 + 16:qq * 24 + 24], OP.mult)
                tt_(V, STATS[:, 0:8], STATS[:, 0:8],
                    EXC[:, qq * 24 + 8:qq * 24 + 16], OP.add)
            for g in range(G):
                scan(V, U[:, g * t:(g + 1) * t],
                     AFWD[:, g:g + 1].broadcast_to([128, t]),
                     U[:, g * t:(g + 1) * t], STATS[:, 8 + g:9 + g])
            for it in range(ntt):
                w2_pe = []
                for k in range(G):
                    wq_q = SY if k % 2 == 0 else GP
                    if k < 2:
                        wq_q.wait(PE)
                    else:
                        wq_q.wait(PE, w2_pe[k - 2])
                    dma(wq_q, WS[:, (k % 2) * 1024:(k % 2) * 1024 + 1024],
                        W["Wf2"].ap()[k * 128:(k + 1) * 128, :])
                    PE.wait(wq_q)
                    if k == 0:
                        PE.wait(V)
                    for m in range(G):
                        mm(PB[m][:, :tt],
                           WS[:, (k % 2) * 1024 + m * 128:
                              (k % 2) * 1024 + m * 128 + 128],
                           U[:, k * t + it * tt:k * t + it * tt + tt],
                           k == 0, k == G - 1,
                           inc=(1 if m == G - 1 else 0))
                    w2_pe.append(PE.count)
                V.wait(PE)
                for m in range(G):
                    xsl = X[:, m * t + it * tt:m * t + it * tt + tt]
                    stt(V, xsl, PB[m][:, :tt], CONS[:, 3 * 8 + m:3 * 8 + m + 1],
                        xsl, OP.add, OP.add)
            sc_v = []
            SY.wait(GP)
            for g in range(G):
                b0 = (g % 2) * 2 * t
                if g < 2:
                    SY.wait(V)
                else:
                    SY.wait(V, sc_v[g - 2])
                dma(SY, BUFH[:, b0:b0 + t], si_s.ap()[g, :, :])
                dma(SY, BUFH[:, b0 + t:b0 + 2 * t], a_s.ap()[g, :, :])
                V.wait(SY)
                scan(V, U[:, g * t:(g + 1) * t],
                     BUFH[:, b0 + 2 * t - 1:b0 + t - 1:-1],
                     (BUFH[:, t - 1::-1] if b0 == 0 else
                      BUFH[:, b0 + t - 1:b0 - 1:-1]),
                     STATS[:, 16 + g:17 + g])
                sc_v.append(V.count)
            for it in range(ntt):
                wb_pe = []
                for k in range(G):
                    wq_q = SY if k % 2 == 0 else GP
                    if k < 2:
                        wq_q.wait(PE)
                    else:
                        wq_q.wait(PE, wb_pe[k - 2])
                    dma(wq_q, WS[:, (k % 2) * 1024:(k % 2) * 1024 + 1024],
                        W["Wb2"].ap()[k * 128:(k + 1) * 128, :])
                    PE.wait(wq_q)
                    if k == 0:
                        PE.wait(V)
                    for m in range(G):
                        mm(PB[m][:, :tt],
                           WS[:, (k % 2) * 1024 + m * 128:
                              (k % 2) * 1024 + m * 128 + 128],
                           U[:, k * t + (t - (it + 1) * tt):
                              k * t + (t - it * tt)],
                           k == 0, k == G - 1,
                           inc=(1 if m == G - 1 else 0))
                    wb_pe.append(PE.count)
                V.wait(PE)
                for m in range(G):
                    xsl = X[:, m * t + it * tt:m * t + it * tt + tt]
                    stt(V, xsl, PB[m][:, tt - 1::-1],
                        CONS[:, 7 * 8 + m:7 * 8 + m + 1], xsl, OP.add, OP.add)

        def output():
            for lt in range(t // 128):
                tok0 = lt * 128
                transpose_fm_to_tok(tok0, SP4)
                V.wait(S)
                cp(V, BUFA[:, lt * 1024:(lt + 1) * 1024], SP4[:, :])
            SY.wait(V)
            dma(SY, y_out.ap().rearrange("(n p) c -> p n c", p=128),
                BUFA[:, 0:(t // 128) * 1024].rearrange("p (n c) -> p n c", c=D))
            SY.ops.append(lambda e, s=SY.sem, v=SY.count: e.wait_ge(s, v))

        if debug_out is not None:
            debug_out.update(locals())

        build_x()
        ffn(1)
        attn()
        ffn(3)
        output()

        with nc.Block() as block:
            block.sync(lambda e: [op(e) for op in SY.ops])
            block.gpsimd(lambda e: [op(e) for op in GP.ops])
            block.vector(lambda e: [op(e) for op in V.ops])
            block.scalar(lambda e: [op(e) for op in S.ops])
            block.tensor(lambda e: [op(e) for op in PE.ops])

    return nc


_PROG_CACHE = {}


def _get_prog(t):
    if t not in _PROG_CACHE:
        _PROG_CACHE[t] = build_program(t)
    return _PROG_CACHE[t]


def make_inmaps(x_full, wd, t=2048, ncore=8):
    vec_order = ("a_fwd", "a_param", "bf1", "bf2", "bbx", "bgx", "bga", "bb2")
    vecs8 = np.concatenate([np.asarray(wd[k], np.float32).reshape(8, 128)
                            for k in vec_order], axis=0)
    scal2 = np.stack([np.asarray(wd["s_in"], np.float32).reshape(1),
                      np.asarray(wd["s_in_down"], np.float32).reshape(1)])
    base = {
        "Wf1": np.asarray(wd["Wf1"], np.float32),
        "Wf2": np.asarray(wd["Wf2"], np.float32),
        "Wbx": np.asarray(wd["Wbx"], np.float32),
        "Wgx": np.asarray(wd["Wgx"], np.float32),
        "Wga": np.asarray(wd["Wga"], np.float32),
        "Wb2": np.asarray(wd["Wb2"], np.float32),
        "gate_w": np.asarray(wd["gate_w"], np.float32),
        "up_w": np.asarray(wd["up_w"], np.float32),
        "down_w": np.asarray(wd["down_w"], np.float32),
        "vecs8": vecs8, "scal2": scal2,
    }
    nq = ncore // x_full.shape[0]
    in_maps = []
    for c in range(ncore):
        b, q = divmod(c, nq)
        oh = np.zeros((128, 4), np.float32)
        oh[:, q % 4] = 1.0
        m = dict(base)
        m["x"] = np.ascontiguousarray(x_full[b, q * t:(q + 1) * t], dtype=np.float32)
        m["onehot"] = oh
        in_maps.append(m)
    return in_maps


def kernel(L0_attn_x, L0_attn_a_fwd, L0_attn_Wf1, L0_attn_bf1, L0_attn_Wf2,
           L0_attn_bf2, L0_attn_Wbx, L0_attn_bbx, L0_attn_Wgx, L0_attn_bgx,
           L0_attn_Wga, L0_attn_bga, L0_attn_a_param, L0_attn_Wb2, L0_attn_bb2,
           L0_ffn_gate_w, L0_ffn_up_w, L0_ffn_down_w, L0_ffn_s_in,
           L0_ffn_s_in_down):
    x = np.asarray(L0_attn_x, np.float32)
    wd = {
        "a_fwd": L0_attn_a_fwd, "a_param": L0_attn_a_param,
        "bf1": L0_attn_bf1, "bf2": L0_attn_bf2, "bbx": L0_attn_bbx,
        "bgx": L0_attn_bgx, "bga": L0_attn_bga, "bb2": L0_attn_bb2,
        "Wf1": L0_attn_Wf1, "Wf2": L0_attn_Wf2, "Wbx": L0_attn_Wbx,
        "Wgx": L0_attn_Wgx, "Wga": L0_attn_Wga, "Wb2": L0_attn_Wb2,
        "gate_w": L0_ffn_gate_w, "up_w": L0_ffn_up_w, "down_w": L0_ffn_down_w,
        "s_in": L0_ffn_s_in, "s_in_down": L0_ffn_s_in_down,
    }
    t = 2048
    nc = _get_prog(t)
    in_maps = make_inmaps(x, wd, t=t, ncore=8)
    res = run_bass_kernel_spmd(nc, in_maps, core_ids=list(range(8)))
    y = np.empty((2, 8192, 1024), np.float32)
    for c in range(8):
        b, q = divmod(c, 4)
        y[b, q * t:(q + 1) * t] = np.asarray(res.results[c]["y"], np.float32)
    return y


# version 05
# speedup vs baseline: 1.3638x; 1.0627x over previous
import numpy as np
import concourse.bass as bass
import concourse.mybir as mybir
from concourse.bass_utils import run_bass_kernel_spmd

F32 = mybir.dt.float32
BF16 = mybir.dt.bfloat16
FP8 = mybir.dt.float8e4
AF = mybir.ActivationFunctionType
OP = mybir.AluOpType
AX = mybir.AxisListType

D = 1024
G = 8
I_H = 4096
MI = 32
C8 = 2.0 ** -8
EPS = 1e-5


class Eng:
    def __init__(self, sem):
        self.ops = []
        self.sem = sem
        self.count = 0

    def emit(self, fn, inc=0):
        sem, n = self.sem, inc

        def op(e):
            ins = fn(e)
            if n:
                ins.then_inc(sem, n)
        self.ops.append(op)
        if inc:
            self.count += inc
        return self.count

    def wait(self, other, value=None):
        sem = other.sem
        v = other.count if value is None else value
        if v > 0:
            self.ops.append(lambda e, s=sem, vv=v: e.wait_ge(s, vv))


def build_program(t=2048, debug_out=None):
    tt = min(512, t)
    ntt = t // tt
    nlt = tt // 128
    ch = min(512, t)
    nch = t // ch
    hw_ = min(512, t)
    nh = t // hw_

    nc = bass.Bass("TRN2", target_bir_lowering=False, detect_race_conditions=False)

    x_in = nc.dram_tensor("x", [t, D], F32, kind="ExternalInput")
    y_out = nc.dram_tensor("y", [t, D], F32, kind="ExternalOutput")
    W = {}
    for nm in ("Wf1", "Wf2", "Wbx", "Wgx", "Wga", "Wb2"):
        W[nm] = nc.dram_tensor(nm, [D, D], F32, kind="ExternalInput")
    gate_w = nc.dram_tensor("gate_w", [I_H, D], F32, kind="ExternalInput")
    up_w = nc.dram_tensor("up_w", [I_H, D], F32, kind="ExternalInput")
    down_w = nc.dram_tensor("down_w", [D, I_H], F32, kind="ExternalInput")
    vecs8 = nc.dram_tensor("vecs8", [64, 128], F32, kind="ExternalInput")
    scal2 = nc.dram_tensor("scal2", [2, 1], F32, kind="ExternalInput")
    onehot = nc.dram_tensor("onehot", [128, 4], F32, kind="ExternalInput")

    qg_s = nc.dram_tensor("qg_s", [MI, G, 128, 128], BF16)
    qu_s = nc.dram_tensor("qu_s", [MI, G, 128, 128], BF16)
    qd_s = nc.dram_tensor("qd_s", [MI, G, 128, 128], BF16)
    si_s = nc.dram_tensor("si_s", [G, 128, t], F32)
    a_s = nc.dram_tensor("a_s", [G, 128, t], F32)
    cc_i = nc.dram_tensor("cc_i", [128, 24], F32)
    cc_o = nc.dram_tensor("cc_o", [4, 128, 24], F32)

    from contextlib import ExitStack
    stack = ExitStack()
    with stack:
        sbuf = lambda n_, s, d=F32: stack.enter_context(nc.sbuf_tensor(n_, s, d))

        BUFA = sbuf("BUFA", [128, t * G])
        X = sbuf("X", [128, t * G])
        BUFH = sbuf("BUFH", [128, 8192])
        WS = sbuf("WS", [128, 2048])
        WQ = sbuf("WQ", [128, 4096], BF16)
        QXK = sbuf("QXK", [128, G * 512], BF16)
        QHD = sbuf("QHD", [128, 1536], BF16)
        QTOK = sbuf("QTOK", [128, 1024], BF16)
        VQ8 = sbuf("VQ8", [128, 1024], FP8)
        AMB = sbuf("AMB", [128, 256])
        SCH = sbuf("SCH", [128, 512])
        AMH = sbuf("AMH", [128, 512])
        HSQ = sbuf("HSQ", [128, 512])
        V8T = sbuf("V8T", [128, 512], FP8)
        RRB = sbuf("RRB", [128, 512])
        S8B = sbuf("S8B", [128, 512])
        SP4 = sbuf("SP4", [128, 1024])
        IDF = sbuf("IDF", [128, 128])
        IDB = sbuf("IDB", [128, 128], BF16)
        ZRO = sbuf("ZRO", [128, 16])
        GAPB = sbuf("GAPB", [128, 8])
        STATS = sbuf("STATS", [128, 64])
        CONS = sbuf("CONS", [128, 64])
        SUM = sbuf("SUM", [128, 64])
        SCL = sbuf("SCL", [128, 32])
        EXC = sbuf("EXC", [128, 96])
        GWS = sbuf("GWS", [128, 16])

        PB = [stack.enter_context(nc.psum_tensor(f"PB{i}", [128, 512], F32))
              for i in range(8)]
        PTR = PB[6]
        PTB = PB[7].ap().bitcast(BF16)

        sems = {k: stack.enter_context(nc.semaphore("s_" + k))
                for k in ("sy", "gp", "gpc", "v", "s", "pe", "cc")}
        SY, GP, V, S, PE = (Eng(sems[k]) for k in ("sy", "gp", "v", "s", "pe"))
        GPC = Eng(sems["gpc"])
        GPC.ops = GP.ops
        CCSEM = sems["cc"]

        FQ0 = BUFH[:, 4096:5120]
        FQ1 = BUFH[:, 5120:6144]
        FQ2 = BUFH[:, 6144:7168]
        SQS = BUFH[:, 7168:8192]
        QXF = QXK.ap().bitcast(F32)
        HQ0, HQ1, HQ2 = SP4[:, 0:512], SP4[:, 512:1024], HSQ

        def dma(eng, out, in_):
            return eng.emit(lambda e, o=out, i=in_: e.dma_start(out=o, in_=i),
                            inc=16)

        _vw = []

        def _rngs(aps):
            out = []
            for ap in aps:
                if ap is None or isinstance(ap, (int, float)):
                    continue
                try:
                    nm = ap.tensor.name
                    off = int(ap.offset)
                    sh = list(ap.shape)
                    try:
                        stv = list(ap.stride)
                    except Exception:
                        stv = None
                    if stv and len(stv) == len(sh) and len(sh) > 1:
                        span = 1 + sum((int(d) - 1) * abs(int(s))
                                       for d, s in zip(sh[1:], stv[1:]))
                    else:
                        span = 1
                        for d in sh[1:]:
                            span *= int(d)
                    out.append((nm, off, off + max(span, 1)))
                except Exception:
                    out.append(("?", 0, 1 << 62))
            return out

        def _fsz(ap):
            try:
                return int(ap.free_size)
            except Exception:
                pass
            try:
                n = 1
                for d in list(ap.shape)[1:]:
                    n *= int(d)
                return n
            except Exception:
                return 1

        def _vsmall(aps):
            return all(_fsz(ap) <= 128 for ap in aps)

        def _vraw(outs, ins):
            rd = _rngs(ins)
            need = 0
            for age, (small, wr) in enumerate(_vw):
                if not small:
                    continue
                for nm, lo, hi in wr:
                    for nm2, lo2, hi2 in rd:
                        if ((nm == nm2 or nm == "?" or nm2 == "?")
                                and lo < hi2 and lo2 < hi):
                            need = max(need, 2 - age)
            for _ in range(need):
                V.emit(lambda e: e.tensor_copy(GAPB[:, 0:8], ZRO[:, 0:8]),
                       inc=0)
                _vw.insert(0, (False, []))
            _vw.insert(0, (_vsmall(outs), _rngs(outs)))
            del _vw[2:]

        def tsc(eng, out, in0, s1, s2=None, op0=OP.mult, op1=None):
            if eng is V:
                _vraw([out], [in0, s1, s2])
            if op1 is None:
                return eng.emit(lambda e, o=out, i=in0, a=s1, p0=op0:
                                e.tensor_scalar(o, i, a, None, p0), inc=1)
            return eng.emit(lambda e, o=out, i=in0, a=s1, b=s2, p0=op0, p1=op1:
                            e.tensor_scalar(o, i, a, b, p0, p1), inc=1)

        def tt_(eng, out, a, b, op):
            if eng is V:
                _vraw([out], [a, b])
            return eng.emit(lambda e, o=out, x=a, y=b, p=op:
                            e.tensor_tensor(o, x, y, p), inc=1)

        def stt(eng, out, in0, sc, in1, op0, op1):
            if eng is V:
                _vraw([out], [in0, sc, in1])
            return eng.emit(lambda e, o=out, i0=in0, s=sc, i1=in1, p0=op0, p1=op1:
                            e.scalar_tensor_tensor(o, i0, s, i1, p0, p1), inc=1)

        def act(eng, out, in_, fn, bias=0.0, scale=1.0, accum=None):
            return eng.emit(lambda e, o=out, i=in_, f=fn, b=bias, s=scale, a=accum:
                            e.activation(o, i, f, bias=b, scale=s, accum_out=a),
                            inc=1)

        def cp(eng, out, in_):
            if eng is V:
                _vraw([out], [in_])
            return eng.emit(lambda e, o=out, i=in_: e.tensor_copy(o, i), inc=1)

        def recip(eng, out, in_):
            if eng is V:
                _vraw([out], [in_])
                _vw[0] = (True, _vw[0][1])
            return eng.emit(lambda e, o=out, i=in_: e.reciprocal(o, i), inc=1)

        def red(eng, out, in_, op, absval=False):
            if eng is V:
                _vraw([out], [in_])
            return eng.emit(lambda e, o=out, i=in_, p=op, a=absval:
                            e.tensor_reduce(o, i, AX.X, p, apply_absolute_value=a),
                            inc=1)

        def mm(out, lhsT, rhs, start, stop, inc=0):
            return PE.emit(lambda e, o=out, l=lhsT, r=rhs, a=start, b=stop:
                           e.matmul(o, l, r, start=a, stop=b), inc=inc)

        def tr(out, in_, ident):
            return PE.emit(lambda e, o=out, i=in_, d=ident:
                           e.transpose(o, i, d), inc=1)

        def scan(eng, out, d0, d1, init):
            if eng is V:
                _vraw([out], [d0, d1, init])
            return eng.emit(lambda e, o=out, a=d0, b=d1, i=init:
                            e.tensor_tensor_scan(o, a, b, i, OP.mult, OP.add),
                            inc=1)

        dma(SY, SP4[:64, :128], vecs8.ap())
        dma(SY, SCL[:, 0:1], scal2[0:1, :].broadcast_to([128, 1]))
        dma(SY, SCL[:, 1:2], scal2[1:2, :].broadcast_to([128, 1]))
        dma(SY, SCL[:, 2:6], onehot.ap())
        GPC.emit(lambda e: e.iota(IDF[:, :], [[1, 128]], channel_multiplier=-1,
                                  allow_small_or_imprecise_dtypes=True), inc=1)
        GPC.emit(lambda e: e.memset(ZRO[:, :], 0.0), inc=1)
        GPC.emit(lambda e: e.memset(SP4[:, 512:640], 1.0), inc=1)
        V.wait(GPC)
        tsc(V, SCL[:, 31:32], ZRO[:, 0:1], 1.0, None, OP.add)
        ONECOL = SCL[:, 31:32]
        tsc(V, IDB[:, :], IDF[:, :], 0.0, None, OP.is_equal)
        tsc(V, IDF[:, :], IDF[:, :], 0.0, None, OP.is_equal)
        PE.wait(SY)
        PE.wait(V)
        tr(PTR[:, :64], SP4[:64, :128], IDF[:64, :64])
        S.wait(PE)
        act(S, CONS[:, :], PTR[:128, :64], AF.Identity)
        V.wait(S)
        AFWD = CONS[:, 0:8]
        S.wait(V)
        act(S, SUM[:, 0:8], CONS[:, 8:16], AF.Exp)
        V.wait(S)
        tsc(V, SUM[:, 0:8], SUM[:, 0:8], 1.0, None, OP.add)
        S.wait(V)
        act(S, SUM[:, 0:8], SUM[:, 0:8], AF.Ln)
        act(S, SUM[:, 16:24], CONS[:, 0:8], AF.Ln)
        V.wait(S)
        tsc(V, SUM[:, 8:16], SUM[:, 0:8], -8.0)
        tsc(V, SCL[:, 30:31], ZRO[:, 0:1], EPS, None, OP.add)
        M8SP = SUM[:, 8:16]
        LNAF = SUM[:, 16:24]
        cp(V, SUM[:, 24:32], CONS[:, 0:8])
        for _ in range(int(round(np.log2(t)))):
            tt_(V, SUM[:, 24:32], SUM[:, 24:32], SUM[:, 24:32], OP.mult)
        ACORE = SUM[:, 24:32]

        def gate_rows_ap(src, it):
            b0 = 2048 * (it // 16) + (it % 16)
            return bass.AP(src, b0 * D, [[16 * D, 128], [1, D]])

        def global_absmax(src_ap_fn, ntile, npiece, out_col):
            tt_(V, GWS[:, out_col:out_col + 1], ZRO[:, 0:1], ZRO[:, 0:1], OP.add)
            for it in range(ntile):
                for piece in range(npiece):
                    j = (it * npiece + piece) % 2
                    dst = WS[:, j * 1024:(j + 1) * 1024]
                    SY.wait(V)
                    dma(SY, dst, src_ap_fn(it, piece))
                    V.wait(SY)
                    red(V, GWS[:, out_col + 1:out_col + 2], dst, OP.max,
                        absval=True)
                    tt_(V, GWS[:, out_col:out_col + 1],
                        GWS[:, out_col:out_col + 1],
                        GWS[:, out_col + 1:out_col + 2], OP.max)
            PE.wait(V)
            PE.wait(S)
            tr(PTR[:1, 0:128], GWS[:, out_col:out_col + 1], IDF[:, :])
            S.wait(PE)
            act(S, AMB[:1, 128:256], PTR[:1, 0:128], AF.Identity)
            V.wait(S)
            red(V, GWS[:1, out_col:out_col + 1], AMB[:1, 128:256], OP.max)
            PE.wait(V)
            mm(PTR[:, 128:129], SP4[:1, 512:640], GWS[:1, out_col:out_col + 1],
               True, True, inc=1)
            V.wait(PE)
            cp(V, GWS[:, out_col:out_col + 1], PTR[:, 128:129])

        global_absmax(lambda it, p: gate_rows_ap(gate_w, it), MI, 1, 0)
        global_absmax(lambda it, p: gate_rows_ap(up_w, it), MI, 1, 4)
        global_absmax(lambda it, p: down_w.ap()[it * 128:(it + 1) * 128,
                                                p * 1024:(p + 1) * 1024],
                      G, 4, 8)
        for col in (0, 4, 8):
            recip(V, GWS[:, col + 2:col + 3], GWS[:, col:col + 1])
        cp(V, SP4[:, 0:128], IDF[:, :])
        cp(V, SP4[:, 128:256], IDF[:, :])
        for j, col in enumerate((0, 4, 8)):
            tsc(V, SCL[:, 6 + j:7 + j], GWS[:, col + 2:col + 3], 2688.0)
        tt_(V, SCL[:, 16:17], SCL[:, 0:1], SCL[:, 6:7], OP.mult)
        tt_(V, SCL[:, 17:18], SCL[:, 0:1], SCL[:, 7:8], OP.mult)
        tt_(V, SCL[:, 18:19], SCL[:, 1:2], SCL[:, 8:9], OP.mult)
        for j in range(3):
            recip(V, SCL[:, 19 + j:20 + j], SCL[:, 16 + j:17 + j])
        cp(V, SP4[:, 0:128], IDF[:, :])
        cp(V, SP4[:, 128:256], IDF[:, :])
        cp(V, SCL[:, 9:10], SCL[:, 19:20])
        cp(V, SCL[:, 10:11], SCL[:, 20:21])
        tsc(V, SCL[:, 11:12], SCL[:, 21:22], 0.5)
        tsc(V, SCL[:, 12:13], SCL[:, 1:2], 1.0 / 12.0)
        tsc(V, SCL[:, 13:14], SCL[:, 1:2], 0.5)
        tsc(V, SCL[:, 24:25], SCL[:, 6:7], 1.0 / 12.0)
        tsc(V, SCL[:, 26:27], SCL[:, 6:7], 0.5)
        tsc(V, SCL[:, 25:26], SCL[:, 7:8], 1.0 / 12.0)
        tsc(V, SCL[:, 27:28], SCL[:, 7:8], 0.5)
        tsc(V, SCL[:, 28:29], SCL[:, 8:9], 1.0 / 12.0)
        tsc(V, SCL[:, 29:30], SCL[:, 8:9], 0.5)
        GX = SCL[:, 0:1]
        INVG1, INVG2, HINVD = SCL[:, 9:10], SCL[:, 10:11], SCL[:, 11:12]
        GD12, GD2 = SCL[:, 12:13], SCL[:, 13:14]

        def fp4(src, width, rr, s8, t0, t1, t2, outq, blk16, vq):
            nb = width // 16
            if blk16:
                tt_(V, t0[:, :width].rearrange("p (b k) -> p b k", k=16),
                    src.rearrange("p (b k) -> p b k", k=16),
                    rr.unsqueeze(2).broadcast_to([128, nb, 16]), OP.mult)
            else:
                tt_(V, t0[:, :width], src, rr, OP.mult)
            tsc(V, t0[:, :width], t0[:, :width], 6.5, -6.5, OP.min, OP.max)
            tsc(V, t1[:, :width], t0[:, :width], 2.0, 0.0, OP.subtract, OP.max)
            tsc(V, t2[:, :width], t0[:, :width], -2.0, 0.0, OP.subtract, OP.min)
            tt_(V, t1[:, :width], t1[:, :width], t2[:, :width], OP.add)
            tsc(V, t2[:, :width], t0[:, :width], 4.0, 0.0, OP.subtract, OP.max)
            stt(V, t2[:, :width], t2[:, :width], -0.25, t0[:, :width],
                OP.mult, OP.add)
            tsc(V, t0[:, :width], t0[:, :width], -4.0, 0.0, OP.subtract, OP.min)
            stt(V, t0[:, :width], t0[:, :width], -0.25, t2[:, :width],
                OP.mult, OP.add)
            stt(V, t0[:, :width], t1[:, :width], -0.5, t0[:, :width],
                OP.mult, OP.add)
            tsc(V, vq[:, :width], t0[:, :width], C8)
            tsc(V, t1[:, :width], vq[:, :width], 2 * C8, 0.0, OP.subtract, OP.max)
            tsc(V, t2[:, :width], vq[:, :width], -2 * C8, 0.0, OP.subtract, OP.min)
            tt_(V, t1[:, :width], t1[:, :width], t2[:, :width], OP.add)
            tsc(V, t2[:, :width], vq[:, :width], 3 * C8, 0.0, OP.subtract, OP.max)
            stt(V, t2[:, :width], t2[:, :width], 2.0, t1[:, :width],
                OP.mult, OP.add)
            tsc(V, t1[:, :width], vq[:, :width], -3 * C8, 0.0, OP.subtract,
                OP.min)
            stt(V, t1[:, :width], t1[:, :width], 2.0, t2[:, :width],
                OP.mult, OP.add)
            tt_(V, t1[:, :width], vq[:, :width], t1[:, :width], OP.add)
            if blk16:
                tt_(V, outq.rearrange("p (b k) -> p b k", k=16),
                    t1[:, :width].rearrange("p (b k) -> p b k", k=16),
                    s8.unsqueeze(2).broadcast_to([128, nb, 16]), OP.mult)
            else:
                tt_(V, outq, t1[:, :width], s8, OP.mult)

        def sc_stage(amax_ap, n, mul_ap, half_ap, rr_out, s8_out, v8t):
            tt_(V, rr_out[:, :n], amax_ap, mul_ap.broadcast_to([128, n]), OP.mult)
            tsc(V, rr_out[:, :n], rr_out[:, :n], 1.0, 224.0, OP.mult, OP.min)
            cp(V, v8t[:, :n], rr_out[:, :n])
            recip(V, rr_out[:, :n], v8t[:, :n])
            tsc(V, s8_out[:, :n], v8t[:, :n], 512.0)
            cp(V, STATS[:, 32:64], SUM[:, 0:32])
            tsc(V, rr_out[:, :n], rr_out[:, :n], 1e30, None, OP.min)
            tt_(V, rr_out[:, :n], rr_out[:, :n],
                half_ap.broadcast_to([128, n]), OP.mult)

        def wprep_gateup(src, dst, which):
            mul_ap = SCL[:, 24 + which:25 + which]
            half_ap = SCL[:, 26 + which:27 + which]
            for pair in range(MI // 2):
                its = (2 * pair, 2 * pair + 1)
                for j, it in enumerate(its):
                    SY.wait(V)
                    SY.wait(PE)
                    dma(SY, WS[:, j * 1024:(j + 1) * 1024], gate_rows_ap(src, it))
                V.wait(SY)
                for j in range(2):
                    red(V, AMB[:, j * 64:(j + 1) * 64],
                        WS[:, j * 1024:(j + 1) * 1024]
                        .rearrange("p (b k) -> p b k", k=16), OP.max, absval=True)
                sc_stage(AMB[:, :128], 128, mul_ap, half_ap, RRB, S8B, V8T)
                for j, it in enumerate(its):
                    fp4(WS[:, j * 1024:(j + 1) * 1024], 1024,
                        RRB[:, j * 64:(j + 1) * 64], S8B[:, j * 64:(j + 1) * 64],
                        FQ0, FQ1, FQ2, QTOK[:, :1024], True, VQ8)
                    for g in range(G):
                        slot = g % 8
                        PE.wait(V)
                        PE.wait(S)
                        tr(PTB[:, slot * 128:slot * 128 + 128],
                           QTOK[:, g * 128:(g + 1) * 128], IDB[:, :])
                        S.wait(PE)
                        act(S, WQ[:, slot * 128:slot * 128 + 128],
                            PTB[:, slot * 128:slot * 128 + 128], AF.Identity)
                        GP.wait(S)
                        dma(GP, dst.ap()[it, g, :, :],
                            WQ[:, slot * 128:slot * 128 + 128])

        wprep_gateup(gate_w, qg_s, 0)
        wprep_gateup(up_w, qu_s, 1)

        def wprep_down():
            for it in range(G):
                for half in range(2):
                    SY.wait(V)
                    SY.wait(PE)
                    dma(SY, WS[:, :2048],
                        down_w.ap()[it * 128:(it + 1) * 128,
                                    half * 2048:(half + 1) * 2048])
                    V.wait(SY)
                    red(V, AMB[:, 0:128],
                        WS[:, :2048].rearrange("p (b k) -> p b k", k=16),
                        OP.max, absval=True)
                    sc_stage(AMB[:, :128], 128, SCL[:, 28:29], SCL[:, 29:30],
                             RRB, S8B, V8T)
                    for j in range(2):
                        fp4(WS[:, j * 1024:(j + 1) * 1024], 1024,
                            RRB[:, j * 64:(j + 1) * 64],
                            S8B[:, j * 64:(j + 1) * 64],
                            FQ0, FQ1, FQ2, QTOK[:, :1024], True, VQ8)
                        cp(V, WQ[:, (2 + j) * 1024:(3 + j) * 1024],
                           QTOK[:, :1024])
                    for kk in range(16):
                        tau = half * 16 + kk
                        slot = kk % 8
                        PE.wait(V)
                        PE.wait(S)
                        tr(PTB[:, slot * 128:slot * 128 + 128],
                           WQ[:, 2048 + kk:4096:16], IDB[:, :])
                        S.wait(PE)
                        act(S, WQ[:, slot * 128:slot * 128 + 128],
                            PTB[:, slot * 128:slot * 128 + 128], AF.Identity)
                        GP.wait(S)
                        dma(GP, qd_s.ap()[tau, it, :, :],
                            WQ[:, slot * 128:slot * 128 + 128])

        wprep_down()
        GP.ops.append(lambda e, s=GP.sem, v=GP.count: e.wait_ge(s, v))

        def ln_tile(xtok, c, gmul=None):
            red(V, STATS[:, c:c + 1], xtok, OP.add)
            S.wait(V)
            act(S, SQS, xtok, AF.Square, accum=STATS[:, 8 + c:9 + c])
            V.wait(S)
            tsc(V, STATS[:, 16 + c:17 + c], STATS[:, c:c + 1], 1.0 / D)
            tsc(V, STATS[:, 20 + c:21 + c], STATS[:, 8 + c:9 + c], 1.0 / D)
            tt_(V, STATS[:, 24 + c:25 + c], STATS[:, 16 + c:17 + c],
                STATS[:, 16 + c:17 + c], OP.mult)
            tt_(V, STATS[:, 20 + c:21 + c], STATS[:, 20 + c:21 + c],
                STATS[:, 24 + c:25 + c], OP.subtract)
            S.wait(V)
            act(S, STATS[:, 24 + c:25 + c], STATS[:, 20 + c:21 + c], AF.Sqrt,
                bias=SCL[:, 30:31])
            V.wait(S)
            recip(V, STATS[:, 28 + c:29 + c], STATS[:, 24 + c:25 + c])
            cp(V, STATS[:, 32:48], SUM[:, 0:16])
            cp(V, STATS[:, 48:64], SUM[:, 16:32])
            if gmul is not None:
                tt_(V, STATS[:, 28 + c:29 + c], STATS[:, 28 + c:29 + c],
                    gmul, OP.mult)
            tt_(V, STATS[:, 24 + c:25 + c], STATS[:, 16 + c:17 + c],
                STATS[:, 28 + c:29 + c], OP.mult)
            tsc(V, STATS[:, 24 + c:25 + c], STATS[:, 24 + c:25 + c], -1.0)
            return STATS[:, 28 + c:29 + c], STATS[:, 24 + c:25 + c]

        def transpose_fm_to_tok(tok0, dst):
            for g in range(G):
                sl = g % 4
                if sl == 0:
                    PE.wait(S)
                    PE.wait(V)
                tr(PTR[:, sl * 128:sl * 128 + 128],
                   X[:, g * t + tok0:g * t + tok0 + 128], IDF[:, :])
                if sl == 3:
                    S.wait(PE)
                    act(S, dst[:, (g - 3) * 128:(g + 1) * 128], PTR[:, :512],
                        AF.Identity)

        def build_x():
            for lt in range(t // 128):
                tok0 = lt * 128
                SY.wait(V)
                SY.wait(PE)
                dma(SY, SP4[:, :], x_in.ap()[tok0:tok0 + 128, :])
                PE.wait(SY)
                for g in range(G):
                    sl = g % 4
                    if sl == 0:
                        PE.wait(S)
                    tr(PTR[:, sl * 128:sl * 128 + 128],
                       SP4[:, g * 128:(g + 1) * 128], IDF[:, :])
                    if sl == 3:
                        S.wait(PE)
                        for gg in range(g - 3, g + 1):
                            act(S, X[:, gg * t + tok0:gg * t + tok0 + 128],
                                PTR[:, (gg % 4) * 128:(gg % 4) * 128 + 128],
                                AF.Identity)

        def ffn(sub):
            for it in range(ntt):
                for lt in range(nlt):
                    tok0 = it * tt + lt * 128
                    if sub == 1:
                        SY.wait(V)
                        SY.wait(S)
                        dma(SY, SP4[:, :], x_in.ap()[tok0:tok0 + 128, :])
                        V.wait(SY)
                        S.wait(SY)
                    else:
                        transpose_fm_to_tok(tok0, SP4)
                        V.wait(S)
                    scale, bias = ln_tile(SP4[:, :], lt, gmul=GX)
                    S.wait(V)
                    act(S, BUFH[:, lt * 1024:(lt + 1) * 1024], SP4[:, :],
                        AF.Identity, bias=bias, scale=scale)
                    V.wait(S)
                    red(V, AMB[:, lt * 64:(lt + 1) * 64],
                        BUFH[:, lt * 1024:(lt + 1) * 1024]
                        .rearrange("p (b k) -> p b k", k=16), OP.max, absval=True)
                nbt = nlt * 64
                tsc(V, SCH[:, :nbt], AMB[:, :nbt], 1.0 / 12.0, 224.0,
                    OP.mult, OP.min)
                cp(V, V8T[:, :nbt], SCH[:, :nbt])
                recip(V, SCH[:, :nbt], V8T[:, :nbt])
                tsc(V, S8B[:, :nbt], V8T[:, :nbt], 512.0)
                cp(V, STATS[:, 32:64], SUM[:, 0:32])
                tsc(V, RRB[:, :nbt], SCH[:, :nbt], 1e30, 0.5, OP.min, OP.mult)
                for lt in range(nlt):
                    fp4(BUFH[:, lt * 1024:(lt + 1) * 1024], 1024,
                        RRB[:, lt * 64:(lt + 1) * 64],
                        S8B[:, lt * 64:(lt + 1) * 64],
                        FQ0, FQ1, FQ2, QTOK[:, :1024], True, VQ8)
                    for g in range(G):
                        slot = g % 8
                        PE.wait(V)
                        PE.wait(S)
                        tr(PTB[:, slot * 128:slot * 128 + 128],
                           QTOK[:, g * 128:(g + 1) * 128], IDB[:, :])
                        S.wait(PE)
                        act(S, QXK[:, g * tt + lt * 128:g * tt + lt * 128 + 128],
                            PTB[:, slot * 128:slot * 128 + 128], AF.Identity)
                for hh in range(2):
                    gu_pe, gu_v, gu_s = [], [], []
                    for m16 in range(16):
                        m = hh * 16 + m16
                        bg, bu = PB[2 * (m % 2)], PB[2 * (m % 2) + 1]
                        wql = WQ[:, (m % 2) * 2048:(m % 2) * 2048 + 1024]
                        wqu = WQ[:, (m % 2) * 2048 + 1024:(m % 2) * 2048 + 2048]
                        if m16 < 2:
                            SY.wait(PE)
                            GP.wait(PE)
                        else:
                            SY.wait(PE, gu_pe[m16 - 2])
                            GP.wait(PE, gu_pe[m16 - 2])
                        dma(SY, wql.rearrange("p (g c) -> p g c", g=G),
                            qg_s.ap()[m, :, :, :].rearrange("g p c -> p g c"))
                        dma(GP, wqu.rearrange("p (g c) -> p g c", g=G),
                            qu_s.ap()[m, :, :, :].rearrange("g p c -> p g c"))
                        PE.wait(SY)
                        PE.wait(GP)
                        if m16 < 2:
                            PE.wait(V)
                            PE.wait(S)
                        else:
                            PE.wait(V, gu_v[m16 - 2])
                            PE.wait(S, gu_s[m16 - 2])
                        for k in range(G):
                            mm(bg[:, :tt], wql[:, k * 128:(k + 1) * 128],
                               QXK[:, k * tt:k * tt + tt], k == 0, k == G - 1,
                               inc=(1 if k == G - 1 else 0))
                        for k in range(G):
                            mm(bu[:, :tt], wqu[:, k * 128:(k + 1) * 128],
                               QXK[:, k * tt:k * tt + tt], k == 0, k == G - 1,
                               inc=(1 if k == G - 1 else 0))
                        gu_pe.append(PE.count)
                        S.wait(PE)
                        sgt = SP4[:, :512] if m % 2 == 0 else SP4[:, 512:1024]
                        act(S, sgt[:, :tt], bg[:, :tt], AF.Silu, scale=INVG1)
                        V.wait(S)
                        hdst = BUFH[:, m16 * 512:m16 * 512 + tt]
                        stt(V, hdst, bu[:, :tt], INVG2, sgt[:, :tt],
                            OP.mult, OP.mult)
                        gu_v.append(V.count)
                        S.wait(V)
                        act(S, HSQ[:, :tt], hdst, AF.Square)
                        gu_s.append(S.count)
                        V.wait(S)
                        if m16 == 0:
                            cp(V, AMH[:, :tt], HSQ[:, :tt])
                        else:
                            tt_(V, AMH[:, :tt], AMH[:, :tt], HSQ[:, :tt], OP.max)
                    S.wait(V)
                    act(S, AMH[:, :tt], AMH[:, :tt], AF.Sqrt)
                    V.wait(S)
                    sc_stage(AMH[:, :tt], tt, GD12, GD2, SCH, S8B, V8T)
                    dn_pe = []
                    for m16 in range(16):
                        k = hh * 16 + m16
                        qslot = QHD[:, (m16 % 3) * 512:(m16 % 3) * 512 + tt]
                        if m16 >= 2:
                            V.wait(PE)
                        fp4(BUFH[:, m16 * 512:m16 * 512 + tt], tt,
                            SCH[:, :tt], S8B[:, :tt],
                            HQ0, HQ1, HQ2, qslot, False, VQ8)
                        wqd = WQ[:, (m16 % 2) * 1024:(m16 % 2) * 1024 + 1024]
                        wq_q = SY if m16 % 2 == 0 else GP
                        if m16 < 2:
                            wq_q.wait(PE)
                        else:
                            wq_q.wait(PE, dn_pe[m16 - 2])
                        dma(wq_q, wqd.rearrange("p (g c) -> p g c", g=G),
                            qd_s.ap()[k, :, :, :].rearrange("g p c -> p g c"))
                        PE.wait(wq_q)
                        PE.wait(V)
                        for m in range(G):
                            mm(PB[m][:, :tt], wqd[:, m * 128:(m + 1) * 128],
                               qslot, m16 == 0, m16 == 15,
                               inc=(1 if m == G - 1 else 0))
                        dn_pe.append(PE.count)
                    V.wait(PE)
                    for m in range(G):
                        xsl = X[:, m * t + it * tt:m * t + it * tt + tt]
                        stt(V, xsl, PB[m][:, :tt], HINVD, xsl, OP.mult, OP.add)

        def attn():
            LGA = QXF[:, 0:512]
            PEX = QXF[:, 512:1024]
            WEX = QXF[:, 1024:1536]
            TM0 = QXF[:, 1536:2048]
            SIT = QTOK.ap().bitcast(F32)
            AT = AMH
            XG = BUFH[:, 0:G * 512]
            VC = BUFH[:, G * 512:2 * G * 512]
            U = BUFA

            GPC.wait(V)
            GPC.wait(S)
            GPC.emit(lambda e: e.iota(RRB[:, :], [[1, 512]],
                                      channel_multiplier=0,
                                      allow_small_or_imprecise_dtypes=True),
                     inc=1)
            S.wait(GPC)

            cp(V, SUM[:, 32:48], ZRO[:, 0:16])
            for c in range(nch):
                ct0 = c * ch
                for lt in range(ch // 128):
                    tok0 = ct0 + lt * 128
                    transpose_fm_to_tok(tok0, SP4)
                    V.wait(S)
                    scale, bias = ln_tile(SP4[:, :], lt % 4)
                    S.wait(V)
                    act(S, SQS, SP4[:, :], AF.Identity, bias=bias, scale=scale)
                    PE.wait(S)
                    for g in range(G):
                        sl = g % 4
                        if sl == 0:
                            PE.wait(S)
                            PE.wait(V)
                        tr(PTR[:, sl * 128:sl * 128 + 128],
                           SQS[:, g * 128:(g + 1) * 128], IDF[:, :])
                        if sl == 3:
                            q0 = (g // 4) * 512
                            S.wait(PE)
                            act(S, SP4[:, q0:q0 + 512], PTR[:, :512],
                                AF.Identity)
                            V.wait(S)
                            for gg in range(g - 3, g + 1):
                                pseg = SP4[:, q0 + (gg % 4) * 128:
                                            q0 + (gg % 4) * 128 + 128]
                                act(S, XG[:, gg * ch + lt * 128:
                                          gg * ch + lt * 128 + 128],
                                    pseg, AF.Gelu_apprx_tanh)
                                xsl = X[:, gg * t + tok0:gg * t + tok0 + 128]
                                tt_(V, xsl, xsl, pseg, OP.add)
                for wname, bcol, isU in (("Wf1", 2, True), ("Wbx", 4, False)):
                    wf_pe = []
                    for k in range(G):
                        wq_q = SY if k % 2 == 0 else GP
                        if k < 2:
                            wq_q.wait(PE)
                        else:
                            wq_q.wait(PE, wf_pe[k - 2])
                        dma(wq_q, WS[:, (k % 2) * 1024:(k % 2) * 1024 + 1024],
                            W[wname].ap()[k * 128:(k + 1) * 128, :])
                        PE.wait(wq_q)
                        if k == 0:
                            PE.wait(S)
                            PE.wait(V)
                        for m in range(G):
                            mm(PB[m][:, :ch],
                               WS[:, (k % 2) * 1024 + m * 128:
                                  (k % 2) * 1024 + m * 128 + 128],
                               XG[:, k * ch:(k + 1) * ch], k == 0, k == G - 1,
                               inc=(1 if m == G - 1 else 0))
                        wf_pe.append(PE.count)
                    S.wait(PE)
                    for m in range(G):
                        if isU:
                            dst = U[:, m * t + ct0:m * t + ct0 + ch]
                        else:
                            dst = VC[:, m * ch:(m + 1) * ch]
                        act(S, dst, PB[m][:, :ch], AF.Identity,
                            bias=CONS[:, bcol * 8 + m:bcol * 8 + m + 1])
                for wname, bcol in (("Wgx", 5), ("Wga", 6)):
                    wg_pe = []
                    for k in range(G):
                        wq_q = SY if k % 2 == 0 else GP
                        if k < 2:
                            wq_q.wait(PE)
                        else:
                            wq_q.wait(PE, wg_pe[k - 2])
                        dma(wq_q, WS[:, (k % 2) * 1024:(k % 2) * 1024 + 1024],
                            W[wname].ap()[k * 128:(k + 1) * 128, :])
                        PE.wait(wq_q)
                        if k == 0:
                            PE.wait(S)
                            PE.wait(V)
                        for gg in range(G):
                            mm(PB[gg][:, :ch],
                               WS[:, (k % 2) * 1024 + gg * 128:
                                  (k % 2) * 1024 + gg * 128 + 128],
                               VC[:, k * ch:(k + 1) * ch], k == 0, k == G - 1,
                               inc=(1 if gg == G - 1 else 0))
                        wg_pe.append(PE.count)
                    S.wait(PE)
                    if wname == "Wgx":
                        for gg in range(G):
                            act(S, XG[:, gg * ch:gg * ch + ch], PB[gg][:, :ch],
                                AF.Sigmoid,
                                bias=CONS[:, bcol * 8 + gg:bcol * 8 + gg + 1])
                for g in range(G):
                    act(S, SP4[:, 512:512 + ch], PB[g][:, :ch], AF.Sigmoid,
                        bias=CONS[:, 6 * 8 + g:6 * 8 + g + 1])
                    V.wait(S)
                    tsc(V, LGA[:, :ch], SP4[:, 512:512 + ch], M8SP[:, g:g + 1])
                    S.wait(V)
                    act(S, AT[:, :ch], LGA[:, :ch], AF.Exp)
                    act(S, WEX[:, :ch], LGA[:, :ch], AF.Exp, scale=2.0)
                    GP.wait(S)
                    dma(GP, a_s.ap()[g, :, ct0:ct0 + ch], AT[:, :ch])
                    V.wait(S)
                    tsc(V, WEX[:, :ch], WEX[:, :ch], -1.0, 1.0, OP.mult, OP.add)
                    tsc(V, WEX[:, :ch], WEX[:, :ch], 0.0, None, OP.max)
                    S.wait(V)
                    act(S, TM0[:, :ch], WEX[:, :ch], AF.Sqrt)
                    V.wait(S)
                    tt_(V, SIT[:, :ch], XG[:, g * ch:g * ch + ch],
                        VC[:, g * ch:g * ch + ch], OP.mult)
                    tt_(V, SIT[:, :ch], SIT[:, :ch], TM0[:, :ch], OP.mult)
                    GP.wait(V)
                    dma(GP, si_s.ap()[g, :, ct0:ct0 + ch], SIT[:, :ch])
                    scan(V, PEX[:, :ch], ONECOL.broadcast_to([128, ch]),
                         LGA[:, :ch], 0.0)
                    tt_(V, PEX[:, :ch], PEX[:, :ch], LGA[:, :ch], OP.subtract)
                    S.wait(V)
                    act(S, WEX[:, :ch], PEX[:, :ch], AF.Exp,
                        bias=SUM[:, 32 + g:33 + g])
                    V.wait(S)
                    tt_(V, TM0[:, :ch], WEX[:, :ch], SIT[:, :ch], OP.mult)
                    red(V, STATS[:, 0:1], TM0[:, :ch], OP.add)
                    tt_(V, SUM[:, 40 + g:41 + g], SUM[:, 40 + g:41 + g],
                        STATS[:, 0:1], OP.add)
                    red(V, STATS[:, 1:2], LGA[:, :ch], OP.add)
                    tt_(V, SUM[:, 32 + g:33 + g], SUM[:, 32 + g:33 + g],
                        STATS[:, 1:2], OP.add)
            for g in range(G):
                for h2 in range(nh):
                    tsc(V, STATS[:, 2:3], LNAF[:, g:g + 1],
                        float(t - 1 - hw_ * h2))
                    tsc(V, STATS[:, 3:4], LNAF[:, g:g + 1], -1.0)
                    S.wait(V)
                    act(S, WEX[:, :hw_], RRB[:, :hw_], AF.Exp,
                        bias=STATS[:, 2:3], scale=STATS[:, 3:4])
                    V.wait(S)
                    tt_(V, TM0[:, :hw_],
                        U[:, g * t + h2 * hw_:g * t + h2 * hw_ + hw_],
                        WEX[:, :hw_], OP.mult)
                    red(V, STATS[:, 4 + h2:5 + h2], TM0[:, :hw_], OP.add)
                for h2 in range(1, nh):
                    tt_(V, STATS[:, 4:5], STATS[:, 4:5],
                        STATS[:, 4 + h2:5 + h2], OP.add)
                cp(V, SUM[:, 48 + g:49 + g], STATS[:, 4:5])
            S.wait(V)
            act(S, SUM[:, 56:64], SUM[:, 32:40], AF.Exp)
            V.wait(S)
            cp(V, EXC[:, 0:8], SUM[:, 48:56])
            cp(V, EXC[:, 8:16], SUM[:, 40:48])
            cp(V, EXC[:, 16:24], SUM[:, 56:64])
            GP.wait(V)
            dma(GP, cc_i.ap(), EXC[:, 0:24])
            GP.ops.append(lambda e, s=GP.sem, v=GP.count: e.wait_ge(s, v))

            def _cc(e):
                e.collective_compute(
                    "AllGather", OP.bypass,
                    replica_groups=[[0, 1, 2, 3], [4, 5, 6, 7]],
                    ins=[cc_i.ap().opt()], outs=[cc_o.ap().opt()]
                ).then_inc(CCSEM, 1)
            GP.ops.append(_cc)
            GP.ops.append(lambda e: e.wait_ge(CCSEM, 1))
            dma(GP, EXC[:, 0:96].rearrange("p (q c) -> p q c", q=4),
                cc_o.ap().rearrange("q p c -> p q c"))
            V.wait(GP)
            cp(V, STATS[:, 0:8], ZRO[:, 0:8])
            cp(V, STATS[:, 8:16], ZRO[:, 0:8])
            cp(V, STATS[:, 16:24], ZRO[:, 0:8])
            for qq in range(4):
                stt(V, STATS[:, 8:16], STATS[:, 0:8], SCL[:, 2 + qq:3 + qq],
                    STATS[:, 8:16], OP.mult, OP.add)
                tt_(V, STATS[:, 0:8], STATS[:, 0:8], ACORE, OP.mult)
                tt_(V, STATS[:, 0:8], STATS[:, 0:8],
                    EXC[:, qq * 24:qq * 24 + 8], OP.add)
            cp(V, STATS[:, 0:8], ZRO[:, 0:8])
            for qq in range(3, -1, -1):
                stt(V, STATS[:, 16:24], STATS[:, 0:8], SCL[:, 2 + qq:3 + qq],
                    STATS[:, 16:24], OP.mult, OP.add)
                tt_(V, STATS[:, 0:8], STATS[:, 0:8],
                    EXC[:, qq * 24 + 16:qq * 24 + 24], OP.mult)
                tt_(V, STATS[:, 0:8], STATS[:, 0:8],
                    EXC[:, qq * 24 + 8:qq * 24 + 16], OP.add)
            for g in range(G):
                scan(V, U[:, g * t:(g + 1) * t],
                     AFWD[:, g:g + 1].broadcast_to([128, t]),
                     U[:, g * t:(g + 1) * t], STATS[:, 8 + g:9 + g])
            for it in range(ntt):
                w2_pe = []
                for k in range(G):
                    wq_q = SY if k % 2 == 0 else GP
                    if k < 2:
                        wq_q.wait(PE)
                    else:
                        wq_q.wait(PE, w2_pe[k - 2])
                    dma(wq_q, WS[:, (k % 2) * 1024:(k % 2) * 1024 + 1024],
                        W["Wf2"].ap()[k * 128:(k + 1) * 128, :])
                    PE.wait(wq_q)
                    if k == 0:
                        PE.wait(V)
                    for m in range(G):
                        mm(PB[m][:, :tt],
                           WS[:, (k % 2) * 1024 + m * 128:
                              (k % 2) * 1024 + m * 128 + 128],
                           U[:, k * t + it * tt:k * t + it * tt + tt],
                           k == 0, k == G - 1,
                           inc=(1 if m == G - 1 else 0))
                    w2_pe.append(PE.count)
                V.wait(PE)
                for m in range(G):
                    xsl = X[:, m * t + it * tt:m * t + it * tt + tt]
                    stt(V, xsl, PB[m][:, :tt], CONS[:, 3 * 8 + m:3 * 8 + m + 1],
                        xsl, OP.add, OP.add)
            sc_v = []
            SY.wait(GP)
            for g in range(G):
                b0 = (g % 2) * 2 * t
                if g < 2:
                    SY.wait(V)
                else:
                    SY.wait(V, sc_v[g - 2])
                dma(SY, BUFH[:, b0:b0 + t], si_s.ap()[g, :, :])
                dma(SY, BUFH[:, b0 + t:b0 + 2 * t], a_s.ap()[g, :, :])
                V.wait(SY)
                scan(V, U[:, g * t:(g + 1) * t],
                     BUFH[:, b0 + 2 * t - 1:b0 + t - 1:-1],
                     (BUFH[:, t - 1::-1] if b0 == 0 else
                      BUFH[:, b0 + t - 1:b0 - 1:-1]),
                     STATS[:, 16 + g:17 + g])
                sc_v.append(V.count)
            for it in range(ntt):
                wb_pe = []
                for k in range(G):
                    wq_q = SY if k % 2 == 0 else GP
                    if k < 2:
                        wq_q.wait(PE)
                    else:
                        wq_q.wait(PE, wb_pe[k - 2])
                    dma(wq_q, WS[:, (k % 2) * 1024:(k % 2) * 1024 + 1024],
                        W["Wb2"].ap()[k * 128:(k + 1) * 128, :])
                    PE.wait(wq_q)
                    if k == 0:
                        PE.wait(V)
                    for m in range(G):
                        mm(PB[m][:, :tt],
                           WS[:, (k % 2) * 1024 + m * 128:
                              (k % 2) * 1024 + m * 128 + 128],
                           U[:, k * t + (t - (it + 1) * tt):
                              k * t + (t - it * tt)],
                           k == 0, k == G - 1,
                           inc=(1 if m == G - 1 else 0))
                    wb_pe.append(PE.count)
                V.wait(PE)
                for m in range(G):
                    xsl = X[:, m * t + it * tt:m * t + it * tt + tt]
                    stt(V, xsl, PB[m][:, tt - 1::-1],
                        CONS[:, 7 * 8 + m:7 * 8 + m + 1], xsl, OP.add, OP.add)

        def output():
            for lt in range(t // 128):
                tok0 = lt * 128
                transpose_fm_to_tok(tok0, SP4)
                V.wait(S)
                cp(V, BUFA[:, lt * 1024:(lt + 1) * 1024], SP4[:, :])
            SY.wait(V)
            dma(SY, y_out.ap().rearrange("(n p) c -> p n c", p=128),
                BUFA[:, 0:(t // 128) * 1024].rearrange("p (n c) -> p n c", c=D))
            SY.ops.append(lambda e, s=SY.sem, v=SY.count: e.wait_ge(s, v))

        if debug_out is not None:
            debug_out.update(locals())

        build_x()
        ffn(1)
        attn()
        ffn(3)
        output()

        with nc.Block() as block:
            block.sync(lambda e: [op(e) for op in SY.ops])
            block.gpsimd(lambda e: [op(e) for op in GP.ops])
            block.vector(lambda e: [op(e) for op in V.ops])
            block.scalar(lambda e: [op(e) for op in S.ops])
            block.tensor(lambda e: [op(e) for op in PE.ops])

    return nc


_PROG_CACHE = {}


def _get_prog(t):
    if t not in _PROG_CACHE:
        _PROG_CACHE[t] = build_program(t)
    return _PROG_CACHE[t]


def make_inmaps(x_full, wd, t=2048, ncore=8):
    vec_order = ("a_fwd", "a_param", "bf1", "bf2", "bbx", "bgx", "bga", "bb2")
    vecs8 = np.concatenate([np.asarray(wd[k], np.float32).reshape(8, 128)
                            for k in vec_order], axis=0)
    scal2 = np.stack([np.asarray(wd["s_in"], np.float32).reshape(1),
                      np.asarray(wd["s_in_down"], np.float32).reshape(1)])
    base = {
        "Wf1": np.asarray(wd["Wf1"], np.float32),
        "Wf2": np.asarray(wd["Wf2"], np.float32),
        "Wbx": np.asarray(wd["Wbx"], np.float32),
        "Wgx": np.asarray(wd["Wgx"], np.float32),
        "Wga": np.asarray(wd["Wga"], np.float32),
        "Wb2": np.asarray(wd["Wb2"], np.float32),
        "gate_w": np.asarray(wd["gate_w"], np.float32),
        "up_w": np.asarray(wd["up_w"], np.float32),
        "down_w": np.asarray(wd["down_w"], np.float32),
        "vecs8": vecs8, "scal2": scal2,
    }
    nq = ncore // x_full.shape[0]
    in_maps = []
    for c in range(ncore):
        b, q = divmod(c, nq)
        oh = np.zeros((128, 4), np.float32)
        oh[:, q % 4] = 1.0
        m = dict(base)
        m["x"] = np.ascontiguousarray(x_full[b, q * t:(q + 1) * t], dtype=np.float32)
        m["onehot"] = oh
        in_maps.append(m)
    return in_maps


def kernel(L0_attn_x, L0_attn_a_fwd, L0_attn_Wf1, L0_attn_bf1, L0_attn_Wf2,
           L0_attn_bf2, L0_attn_Wbx, L0_attn_bbx, L0_attn_Wgx, L0_attn_bgx,
           L0_attn_Wga, L0_attn_bga, L0_attn_a_param, L0_attn_Wb2, L0_attn_bb2,
           L0_ffn_gate_w, L0_ffn_up_w, L0_ffn_down_w, L0_ffn_s_in,
           L0_ffn_s_in_down):
    x = np.asarray(L0_attn_x, np.float32)
    wd = {
        "a_fwd": L0_attn_a_fwd, "a_param": L0_attn_a_param,
        "bf1": L0_attn_bf1, "bf2": L0_attn_bf2, "bbx": L0_attn_bbx,
        "bgx": L0_attn_bgx, "bga": L0_attn_bga, "bb2": L0_attn_bb2,
        "Wf1": L0_attn_Wf1, "Wf2": L0_attn_Wf2, "Wbx": L0_attn_Wbx,
        "Wgx": L0_attn_Wgx, "Wga": L0_attn_Wga, "Wb2": L0_attn_Wb2,
        "gate_w": L0_ffn_gate_w, "up_w": L0_ffn_up_w, "down_w": L0_ffn_down_w,
        "s_in": L0_ffn_s_in, "s_in_down": L0_ffn_s_in_down,
    }
    t = 2048
    nc = _get_prog(t)
    in_maps = make_inmaps(x, wd, t=t, ncore=8)
    res = run_bass_kernel_spmd(nc, in_maps, core_ids=list(range(8)))
    y = np.empty((2, 8192, 1024), np.float32)
    for c in range(8):
        b, q = divmod(c, 4)
        y[b, q * t:(q + 1) * t] = np.asarray(res.results[c]["y"], np.float32)
    return y


# version 06
# speedup vs baseline: 1.6608x; 1.2178x over previous
import numpy as np
import concourse.bass as bass
import concourse.mybir as mybir
from concourse.bass_utils import run_bass_kernel_spmd

F32 = mybir.dt.float32
BF16 = mybir.dt.bfloat16
FP8 = mybir.dt.float8e4
AF = mybir.ActivationFunctionType
OP = mybir.AluOpType
AX = mybir.AxisListType

D = 1024
G = 8
I_H = 4096
MI = 32
C8 = 2.0 ** -8
EPS = 1e-5


class Eng:
    def __init__(self, sem):
        self.ops = []
        self.sem = sem
        self.count = 0

    def emit(self, fn, inc=0):
        sem, n = self.sem, inc

        def op(e):
            ins = fn(e)
            if n:
                ins.then_inc(sem, n)
        self.ops.append(op)
        if inc:
            self.count += inc
        return self.count

    def wait(self, other, value=None):
        sem = other.sem
        v = other.count if value is None else value
        if v > 0:
            self.ops.append(lambda e, s=sem, vv=v: e.wait_ge(s, vv))


def build_program(t=2048, debug_out=None):
    tt = min(512, t)
    ntt = t // tt
    nlt = tt // 128
    ch = min(512, t)
    nch = t // ch
    hw_ = min(512, t)
    nh = t // hw_

    nc = bass.Bass("TRN2", target_bir_lowering=False, detect_race_conditions=False)

    x_in = nc.dram_tensor("x", [t, D], F32, kind="ExternalInput")
    y_out = nc.dram_tensor("y", [t, D], F32, kind="ExternalOutput")
    W = {}
    for nm in ("Wf1", "Wf2", "Wbx", "Wgx", "Wga", "Wb2"):
        W[nm] = nc.dram_tensor(nm, [D, D], F32, kind="ExternalInput")
    gate_w = nc.dram_tensor("gate_w", [I_H, D], F32, kind="ExternalInput")
    up_w = nc.dram_tensor("up_w", [I_H, D], F32, kind="ExternalInput")
    down_w = nc.dram_tensor("down_w", [D, I_H], F32, kind="ExternalInput")
    vecs8 = nc.dram_tensor("vecs8", [64, 128], F32, kind="ExternalInput")
    scal2 = nc.dram_tensor("scal2", [2, 1], F32, kind="ExternalInput")
    onehot = nc.dram_tensor("onehot", [128, 4], F32, kind="ExternalInput")

    qg_s = nc.dram_tensor("qg_s", [MI, G, 128, 128], BF16)
    qu_s = nc.dram_tensor("qu_s", [MI, G, 128, 128], BF16)
    qd_s = nc.dram_tensor("qd_s", [MI, G, 128, 128], BF16)
    si_s = nc.dram_tensor("si_s", [G, 128, t], F32)
    a_s = nc.dram_tensor("a_s", [G, 128, t], F32)
    cc_i = nc.dram_tensor("cc_i", [128, 24], F32)
    gate_sh = nc.dram_tensor("gate_sh", [4, 128, D], F32, kind="ExternalInput")
    up_sh = nc.dram_tensor("up_sh", [4, 128, D], F32, kind="ExternalInput")
    down_sh = nc.dram_tensor("down_sh", [128, I_H], F32, kind="ExternalInput")
    wsh_i = nc.dram_tensor("wsh_i", [96, 128, 128], BF16)
    wsh_o = nc.dram_tensor("wsh_o", [8, 96, 128, 128], BF16)
    cc_o = nc.dram_tensor("cc_o", [4, 128, 24], F32)

    from contextlib import ExitStack
    stack = ExitStack()
    with stack:
        sbuf = lambda n_, s, d=F32: stack.enter_context(nc.sbuf_tensor(n_, s, d))

        BUFA = sbuf("BUFA", [128, t * G])
        X = sbuf("X", [128, t * G])
        BUFH = sbuf("BUFH", [128, 8192])
        WS = sbuf("WS", [128, 2048])
        WQ = sbuf("WQ", [128, 4096], BF16)
        QXK = sbuf("QXK", [128, G * 512], BF16)
        QHD = sbuf("QHD", [128, 1536], BF16)
        QTOK = sbuf("QTOK", [128, 1024], BF16)
        VQ8 = sbuf("VQ8", [128, 1024], FP8)
        AMB = sbuf("AMB", [128, 256])
        SCH = sbuf("SCH", [128, 512])
        AMH = sbuf("AMH", [128, 512])
        HSQ = sbuf("HSQ", [128, 512])
        V8T = sbuf("V8T", [128, 512], FP8)
        RRB = sbuf("RRB", [128, 512])
        S8B = sbuf("S8B", [128, 512])
        SP4 = sbuf("SP4", [128, 1024])
        IDF = sbuf("IDF", [128, 128])
        IDB = sbuf("IDB", [128, 128], BF16)
        ZRO = sbuf("ZRO", [128, 16])
        GAPB = sbuf("GAPB", [128, 8])
        STATS = sbuf("STATS", [128, 64])
        CONS = sbuf("CONS", [128, 64])
        SUM = sbuf("SUM", [128, 64])
        SCL = sbuf("SCL", [128, 32])
        EXC = sbuf("EXC", [128, 96])
        GWS = sbuf("GWS", [128, 16])

        PB = [stack.enter_context(nc.psum_tensor(f"PB{i}", [128, 512], F32))
              for i in range(8)]
        PTR = PB[6]
        PTB = PB[7].ap().bitcast(BF16)

        sems = {k: stack.enter_context(nc.semaphore("s_" + k))
                for k in ("sy", "gp", "gpc", "v", "s", "pe", "cc")}
        SY, GP, V, S, PE = (Eng(sems[k]) for k in ("sy", "gp", "v", "s", "pe"))
        GPC = Eng(sems["gpc"])
        GPC.ops = GP.ops
        CCSEM = sems["cc"]

        FQ0 = BUFH[:, 4096:5120]
        FQ1 = BUFH[:, 5120:6144]
        FQ2 = BUFH[:, 6144:7168]
        SQS = BUFH[:, 7168:8192]
        QXF = QXK.ap().bitcast(F32)
        HQ0, HQ1, HQ2 = SP4[:, 0:512], SP4[:, 512:1024], HSQ

        def dma(eng, out, in_):
            return eng.emit(lambda e, o=out, i=in_: e.dma_start(out=o, in_=i),
                            inc=16)

        _vw = []

        def _rngs(aps):
            out = []
            for ap in aps:
                if ap is None or isinstance(ap, (int, float)):
                    continue
                try:
                    nm = ap.tensor.name
                    off = int(ap.offset)
                    sh = list(ap.shape)
                    try:
                        stv = list(ap.stride)
                    except Exception:
                        stv = None
                    if stv and len(stv) == len(sh) and len(sh) > 1:
                        span = 1 + sum((int(d) - 1) * abs(int(s))
                                       for d, s in zip(sh[1:], stv[1:]))
                    else:
                        span = 1
                        for d in sh[1:]:
                            span *= int(d)
                    out.append((nm, off, off + max(span, 1)))
                except Exception:
                    out.append(("?", 0, 1 << 62))
            return out

        def _fsz(ap):
            try:
                return int(ap.free_size)
            except Exception:
                pass
            try:
                n = 1
                for d in list(ap.shape)[1:]:
                    n *= int(d)
                return n
            except Exception:
                return 1

        def _vsmall(aps):
            return all(_fsz(ap) <= 128 for ap in aps)

        def _vraw(outs, ins):
            rd = _rngs(ins)
            need = 0
            for age, (small, wr) in enumerate(_vw):
                if not small:
                    continue
                for nm, lo, hi in wr:
                    for nm2, lo2, hi2 in rd:
                        if ((nm == nm2 or nm == "?" or nm2 == "?")
                                and lo < hi2 and lo2 < hi):
                            need = max(need, 2 - age)
            for _ in range(need):
                V.emit(lambda e: e.tensor_copy(GAPB[:, 0:8], ZRO[:, 0:8]),
                       inc=0)
                _vw.insert(0, (False, []))
            _vw.insert(0, (_vsmall(outs), _rngs(outs)))
            del _vw[2:]

        def tsc(eng, out, in0, s1, s2=None, op0=OP.mult, op1=None):
            if eng is V:
                _vraw([out], [in0, s1, s2])
            if op1 is None:
                return eng.emit(lambda e, o=out, i=in0, a=s1, p0=op0:
                                e.tensor_scalar(o, i, a, None, p0), inc=1)
            return eng.emit(lambda e, o=out, i=in0, a=s1, b=s2, p0=op0, p1=op1:
                            e.tensor_scalar(o, i, a, b, p0, p1), inc=1)

        def tt_(eng, out, a, b, op):
            if eng is V:
                _vraw([out], [a, b])
            return eng.emit(lambda e, o=out, x=a, y=b, p=op:
                            e.tensor_tensor(o, x, y, p), inc=1)

        def stt(eng, out, in0, sc, in1, op0, op1):
            if eng is V:
                _vraw([out], [in0, sc, in1])
            return eng.emit(lambda e, o=out, i0=in0, s=sc, i1=in1, p0=op0, p1=op1:
                            e.scalar_tensor_tensor(o, i0, s, i1, p0, p1), inc=1)

        def act(eng, out, in_, fn, bias=0.0, scale=1.0, accum=None):
            return eng.emit(lambda e, o=out, i=in_, f=fn, b=bias, s=scale, a=accum:
                            e.activation(o, i, f, bias=b, scale=s, accum_out=a),
                            inc=1)

        def cp(eng, out, in_):
            if eng is V:
                _vraw([out], [in_])
            return eng.emit(lambda e, o=out, i=in_: e.tensor_copy(o, i), inc=1)

        def recip(eng, out, in_):
            if eng is V:
                _vraw([out], [in_])
                _vw[0] = (True, _vw[0][1])
            return eng.emit(lambda e, o=out, i=in_: e.reciprocal(o, i), inc=1)

        def red(eng, out, in_, op, absval=False):
            if eng is V:
                _vraw([out], [in_])
            return eng.emit(lambda e, o=out, i=in_, p=op, a=absval:
                            e.tensor_reduce(o, i, AX.X, p, apply_absolute_value=a),
                            inc=1)

        def mm(out, lhsT, rhs, start, stop, inc=0):
            return PE.emit(lambda e, o=out, l=lhsT, r=rhs, a=start, b=stop:
                           e.matmul(o, l, r, start=a, stop=b), inc=inc)

        def tr(out, in_, ident):
            return PE.emit(lambda e, o=out, i=in_, d=ident:
                           e.transpose(o, i, d), inc=1)

        def scan(eng, out, d0, d1, init):
            if eng is V:
                _vraw([out], [d0, d1, init])
            return eng.emit(lambda e, o=out, a=d0, b=d1, i=init:
                            e.tensor_tensor_scan(o, a, b, i, OP.mult, OP.add),
                            inc=1)

        dma(SY, SP4[:64, :128], vecs8.ap())
        dma(SY, SCL[:, 0:1], scal2[0:1, :].broadcast_to([128, 1]))
        dma(SY, SCL[:, 1:2], scal2[1:2, :].broadcast_to([128, 1]))
        dma(SY, SCL[:, 2:6], onehot.ap())
        GPC.emit(lambda e: e.iota(IDF[:, :], [[1, 128]], channel_multiplier=-1,
                                  allow_small_or_imprecise_dtypes=True), inc=1)
        GPC.emit(lambda e: e.memset(ZRO[:, :], 0.0), inc=1)
        GPC.emit(lambda e: e.memset(SP4[:, 512:640], 1.0), inc=1)
        V.wait(GPC)
        tsc(V, SCL[:, 31:32], ZRO[:, 0:1], 1.0, None, OP.add)
        ONECOL = SCL[:, 31:32]
        tsc(V, IDB[:, :], IDF[:, :], 0.0, None, OP.is_equal)
        tsc(V, IDF[:, :], IDF[:, :], 0.0, None, OP.is_equal)
        PE.wait(SY)
        PE.wait(V)
        tr(PTR[:, :64], SP4[:64, :128], IDF[:64, :64])
        S.wait(PE)
        act(S, CONS[:, :], PTR[:128, :64], AF.Identity)
        V.wait(S)
        AFWD = CONS[:, 0:8]
        S.wait(V)
        act(S, SUM[:, 0:8], CONS[:, 8:16], AF.Exp)
        V.wait(S)
        tsc(V, SUM[:, 0:8], SUM[:, 0:8], 1.0, None, OP.add)
        S.wait(V)
        act(S, SUM[:, 0:8], SUM[:, 0:8], AF.Ln)
        act(S, SUM[:, 16:24], CONS[:, 0:8], AF.Ln)
        V.wait(S)
        tsc(V, SUM[:, 8:16], SUM[:, 0:8], -8.0)
        tsc(V, SCL[:, 30:31], ZRO[:, 0:1], EPS, None, OP.add)
        M8SP = SUM[:, 8:16]
        LNAF = SUM[:, 16:24]
        cp(V, SUM[:, 24:32], CONS[:, 0:8])
        for _ in range(int(round(np.log2(t)))):
            tt_(V, SUM[:, 24:32], SUM[:, 24:32], SUM[:, 24:32], OP.mult)
        ACORE = SUM[:, 24:32]

        def gate_rows_ap(src, it):
            b0 = 2048 * (it // 16) + (it % 16)
            return bass.AP(src, b0 * D, [[16 * D, 128], [1, D]])

        def global_absmax(src_ap_fn, ntile, npiece, out_col):
            tt_(V, GWS[:, out_col:out_col + 1], ZRO[:, 0:1], ZRO[:, 0:1], OP.add)
            for it in range(ntile):
                for piece in range(npiece):
                    j = (it * npiece + piece) % 2
                    dst = WS[:, j * 1024:(j + 1) * 1024]
                    SY.wait(V)
                    dma(SY, dst, src_ap_fn(it, piece))
                    V.wait(SY)
                    red(V, GWS[:, out_col + 1:out_col + 2], dst, OP.max,
                        absval=True)
                    tt_(V, GWS[:, out_col:out_col + 1],
                        GWS[:, out_col:out_col + 1],
                        GWS[:, out_col + 1:out_col + 2], OP.max)
            PE.wait(V)
            PE.wait(S)
            tr(PTR[:1, 0:128], GWS[:, out_col:out_col + 1], IDF[:, :])
            S.wait(PE)
            act(S, AMB[:1, 128:256], PTR[:1, 0:128], AF.Identity)
            V.wait(S)
            red(V, GWS[:1, out_col:out_col + 1], AMB[:1, 128:256], OP.max)
            PE.wait(V)
            mm(PTR[:, 128:129], SP4[:1, 512:640], GWS[:1, out_col:out_col + 1],
               True, True, inc=1)
            V.wait(PE)
            cp(V, GWS[:, out_col:out_col + 1], PTR[:, 128:129])

        global_absmax(lambda it, p: gate_rows_ap(gate_w, it), MI, 1, 0)
        global_absmax(lambda it, p: gate_rows_ap(up_w, it), MI, 1, 4)
        global_absmax(lambda it, p: down_w.ap()[it * 128:(it + 1) * 128,
                                                p * 1024:(p + 1) * 1024],
                      G, 4, 8)
        for col in (0, 4, 8):
            recip(V, GWS[:, col + 2:col + 3], GWS[:, col:col + 1])
        cp(V, SP4[:, 0:128], IDF[:, :])
        cp(V, SP4[:, 128:256], IDF[:, :])
        for j, col in enumerate((0, 4, 8)):
            tsc(V, SCL[:, 6 + j:7 + j], GWS[:, col + 2:col + 3], 2688.0)
        tt_(V, SCL[:, 16:17], SCL[:, 0:1], SCL[:, 6:7], OP.mult)
        tt_(V, SCL[:, 17:18], SCL[:, 0:1], SCL[:, 7:8], OP.mult)
        tt_(V, SCL[:, 18:19], SCL[:, 1:2], SCL[:, 8:9], OP.mult)
        for j in range(3):
            recip(V, SCL[:, 19 + j:20 + j], SCL[:, 16 + j:17 + j])
        cp(V, SP4[:, 0:128], IDF[:, :])
        cp(V, SP4[:, 128:256], IDF[:, :])
        cp(V, SCL[:, 9:10], SCL[:, 19:20])
        cp(V, SCL[:, 10:11], SCL[:, 20:21])
        tsc(V, SCL[:, 11:12], SCL[:, 21:22], 0.5)
        tsc(V, SCL[:, 12:13], SCL[:, 1:2], 1.0 / 12.0)
        tsc(V, SCL[:, 13:14], SCL[:, 1:2], 0.5)
        tsc(V, SCL[:, 24:25], SCL[:, 6:7], 1.0 / 12.0)
        tsc(V, SCL[:, 26:27], SCL[:, 6:7], 0.5)
        tsc(V, SCL[:, 25:26], SCL[:, 7:8], 1.0 / 12.0)
        tsc(V, SCL[:, 27:28], SCL[:, 7:8], 0.5)
        tsc(V, SCL[:, 28:29], SCL[:, 8:9], 1.0 / 12.0)
        tsc(V, SCL[:, 29:30], SCL[:, 8:9], 0.5)
        GX = SCL[:, 0:1]
        INVG1, INVG2, HINVD = SCL[:, 9:10], SCL[:, 10:11], SCL[:, 11:12]
        GD12, GD2 = SCL[:, 12:13], SCL[:, 13:14]

        def fp4(src, width, rr, s8, t0, t1, t2, outq, blk16, vq):
            nb = width // 16
            if blk16:
                tt_(V, t0[:, :width].rearrange("p (b k) -> p b k", k=16),
                    src.rearrange("p (b k) -> p b k", k=16),
                    rr.unsqueeze(2).broadcast_to([128, nb, 16]), OP.mult)
            else:
                tt_(V, t0[:, :width], src, rr, OP.mult)
            tsc(V, t0[:, :width], t0[:, :width], 6.5, -6.5, OP.min, OP.max)
            tsc(V, t1[:, :width], t0[:, :width], 2.0, 0.0, OP.subtract, OP.max)
            tsc(V, t2[:, :width], t0[:, :width], -2.0, 0.0, OP.subtract, OP.min)
            tt_(V, t1[:, :width], t1[:, :width], t2[:, :width], OP.add)
            tsc(V, t2[:, :width], t0[:, :width], 4.0, 0.0, OP.subtract, OP.max)
            stt(V, t2[:, :width], t2[:, :width], -0.25, t0[:, :width],
                OP.mult, OP.add)
            tsc(V, t0[:, :width], t0[:, :width], -4.0, 0.0, OP.subtract, OP.min)
            stt(V, t0[:, :width], t0[:, :width], -0.25, t2[:, :width],
                OP.mult, OP.add)
            stt(V, t0[:, :width], t1[:, :width], -0.5, t0[:, :width],
                OP.mult, OP.add)
            tsc(V, vq[:, :width], t0[:, :width], C8)
            tsc(V, t1[:, :width], vq[:, :width], 2 * C8, 0.0, OP.subtract, OP.max)
            tsc(V, t2[:, :width], vq[:, :width], -2 * C8, 0.0, OP.subtract, OP.min)
            tt_(V, t1[:, :width], t1[:, :width], t2[:, :width], OP.add)
            tsc(V, t2[:, :width], vq[:, :width], 3 * C8, 0.0, OP.subtract, OP.max)
            stt(V, t2[:, :width], t2[:, :width], 2.0, t1[:, :width],
                OP.mult, OP.add)
            tsc(V, t1[:, :width], vq[:, :width], -3 * C8, 0.0, OP.subtract,
                OP.min)
            stt(V, t1[:, :width], t1[:, :width], 2.0, t2[:, :width],
                OP.mult, OP.add)
            tt_(V, t1[:, :width], vq[:, :width], t1[:, :width], OP.add)
            if blk16:
                tt_(V, outq.rearrange("p (b k) -> p b k", k=16),
                    t1[:, :width].rearrange("p (b k) -> p b k", k=16),
                    s8.unsqueeze(2).broadcast_to([128, nb, 16]), OP.mult)
            else:
                tt_(V, outq, t1[:, :width], s8, OP.mult)

        def sc_stage(amax_ap, n, mul_ap, half_ap, rr_out, s8_out, v8t):
            tt_(V, rr_out[:, :n], amax_ap, mul_ap.broadcast_to([128, n]), OP.mult)
            tsc(V, rr_out[:, :n], rr_out[:, :n], 1.0, 224.0, OP.mult, OP.min)
            cp(V, v8t[:, :n], rr_out[:, :n])
            recip(V, rr_out[:, :n], v8t[:, :n])
            tsc(V, s8_out[:, :n], v8t[:, :n], 512.0)
            cp(V, STATS[:, 32:64], SUM[:, 0:32])
            tsc(V, rr_out[:, :n], rr_out[:, :n], 1e30, None, OP.min)
            tt_(V, rr_out[:, :n], rr_out[:, :n],
                half_ap.broadcast_to([128, n]), OP.mult)

        def wprep_gateup(src, base, which):
            mul_ap = SCL[:, 24 + which:25 + which]
            half_ap = SCL[:, 26 + which:27 + which]
            for pair in range(2):
                its = (2 * pair, 2 * pair + 1)
                for j, it in enumerate(its):
                    SY.wait(V)
                    SY.wait(PE)
                    dma(SY, WS[:, j * 1024:(j + 1) * 1024], src.ap()[it])
                V.wait(SY)
                for j in range(2):
                    red(V, AMB[:, j * 64:(j + 1) * 64],
                        WS[:, j * 1024:(j + 1) * 1024]
                        .rearrange("p (b k) -> p b k", k=16), OP.max, absval=True)
                sc_stage(AMB[:, :128], 128, mul_ap, half_ap, RRB, S8B, V8T)
                for j, it in enumerate(its):
                    fp4(WS[:, j * 1024:(j + 1) * 1024], 1024,
                        RRB[:, j * 64:(j + 1) * 64], S8B[:, j * 64:(j + 1) * 64],
                        FQ0, FQ1, FQ2, QTOK[:, :1024], True, VQ8)
                    for g in range(G):
                        slot = g % 8
                        PE.wait(V)
                        PE.wait(S)
                        tr(PTB[:, slot * 128:slot * 128 + 128],
                           QTOK[:, g * 128:(g + 1) * 128], IDB[:, :])
                        S.wait(PE)
                        act(S, WQ[:, slot * 128:slot * 128 + 128],
                            PTB[:, slot * 128:slot * 128 + 128], AF.Identity)
                        GP.wait(S)
                        dma(GP, wsh_i.ap()[base + it * 8 + g, :, :],
                            WQ[:, slot * 128:slot * 128 + 128])

        wprep_gateup(gate_sh, 0, 0)
        wprep_gateup(up_sh, 32, 1)

        def wprep_down():
            for it in range(1):
                for half in range(2):
                    SY.wait(V)
                    SY.wait(PE)
                    dma(SY, WS[:, :2048],
                        down_sh.ap()[:, half * 2048:(half + 1) * 2048])
                    V.wait(SY)
                    red(V, AMB[:, 0:128],
                        WS[:, :2048].rearrange("p (b k) -> p b k", k=16),
                        OP.max, absval=True)
                    sc_stage(AMB[:, :128], 128, SCL[:, 28:29], SCL[:, 29:30],
                             RRB, S8B, V8T)
                    for j in range(2):
                        fp4(WS[:, j * 1024:(j + 1) * 1024], 1024,
                            RRB[:, j * 64:(j + 1) * 64],
                            S8B[:, j * 64:(j + 1) * 64],
                            FQ0, FQ1, FQ2, QTOK[:, :1024], True, VQ8)
                        cp(V, WQ[:, (2 + j) * 1024:(3 + j) * 1024],
                           QTOK[:, :1024])
                    for kk in range(16):
                        tau = half * 16 + kk
                        slot = kk % 8
                        PE.wait(V)
                        PE.wait(S)
                        tr(PTB[:, slot * 128:slot * 128 + 128],
                           WQ[:, 2048 + kk:4096:16], IDB[:, :])
                        S.wait(PE)
                        act(S, WQ[:, slot * 128:slot * 128 + 128],
                            PTB[:, slot * 128:slot * 128 + 128], AF.Identity)
                        GP.wait(S)
                        dma(GP, wsh_i.ap()[64 + tau, :, :],
                            WQ[:, slot * 128:slot * 128 + 128])

        wprep_down()
        GP.ops.append(lambda e, s=GP.sem, v=GP.count: e.wait_ge(s, v))

        def _wcc(e):
            e.collective_compute(
                "AllGather", OP.bypass,
                replica_groups=[[0, 1, 2, 3, 4, 5, 6, 7]],
                ins=[wsh_i.ap().opt()], outs=[wsh_o.ap().opt()]
            ).then_inc(CCSEM, 1)
        GP.ops.append(_wcc)

        def ln_tile(xtok, c, gmul=None):
            red(V, STATS[:, c:c + 1], xtok, OP.add)
            S.wait(V)
            act(S, SQS, xtok, AF.Square, accum=STATS[:, 8 + c:9 + c])
            V.wait(S)
            tsc(V, STATS[:, 16 + c:17 + c], STATS[:, c:c + 1], 1.0 / D)
            tsc(V, STATS[:, 20 + c:21 + c], STATS[:, 8 + c:9 + c], 1.0 / D)
            tt_(V, STATS[:, 24 + c:25 + c], STATS[:, 16 + c:17 + c],
                STATS[:, 16 + c:17 + c], OP.mult)
            tt_(V, STATS[:, 20 + c:21 + c], STATS[:, 20 + c:21 + c],
                STATS[:, 24 + c:25 + c], OP.subtract)
            S.wait(V)
            act(S, STATS[:, 24 + c:25 + c], STATS[:, 20 + c:21 + c], AF.Sqrt,
                bias=SCL[:, 30:31])
            V.wait(S)
            recip(V, STATS[:, 28 + c:29 + c], STATS[:, 24 + c:25 + c])
            cp(V, STATS[:, 32:48], SUM[:, 0:16])
            cp(V, STATS[:, 48:64], SUM[:, 16:32])
            if gmul is not None:
                tt_(V, STATS[:, 28 + c:29 + c], STATS[:, 28 + c:29 + c],
                    gmul, OP.mult)
            tt_(V, STATS[:, 24 + c:25 + c], STATS[:, 16 + c:17 + c],
                STATS[:, 28 + c:29 + c], OP.mult)
            tsc(V, STATS[:, 24 + c:25 + c], STATS[:, 24 + c:25 + c], -1.0)
            return STATS[:, 28 + c:29 + c], STATS[:, 24 + c:25 + c]

        def transpose_fm_to_tok(tok0, dst):
            for g in range(G):
                sl = g % 4
                if sl == 0:
                    PE.wait(S)
                    PE.wait(V)
                tr(PTR[:, sl * 128:sl * 128 + 128],
                   X[:, g * t + tok0:g * t + tok0 + 128], IDF[:, :])
                if sl == 3:
                    S.wait(PE)
                    act(S, dst[:, (g - 3) * 128:(g + 1) * 128], PTR[:, :512],
                        AF.Identity)

        def build_x():
            for lt in range(t // 128):
                tok0 = lt * 128
                SY.wait(V)
                SY.wait(PE)
                dma(SY, SP4[:, :], x_in.ap()[tok0:tok0 + 128, :])
                PE.wait(SY)
                for g in range(G):
                    sl = g % 4
                    if sl == 0:
                        PE.wait(S)
                    tr(PTR[:, sl * 128:sl * 128 + 128],
                       SP4[:, g * 128:(g + 1) * 128], IDF[:, :])
                    if sl == 3:
                        S.wait(PE)
                        for gg in range(g - 3, g + 1):
                            act(S, X[:, gg * t + tok0:gg * t + tok0 + 128],
                                PTR[:, (gg % 4) * 128:(gg % 4) * 128 + 128],
                                AF.Identity)

        def ffn(sub):
            if sub == 1:
                SY.ops.append(lambda e: e.wait_ge(CCSEM, 1))
                GP.ops.append(lambda e: e.wait_ge(CCSEM, 1))
            for it in range(ntt):
                for lt in range(nlt):
                    tok0 = it * tt + lt * 128
                    if sub == 1:
                        SY.wait(V)
                        SY.wait(S)
                        dma(SY, SP4[:, :], x_in.ap()[tok0:tok0 + 128, :])
                        V.wait(SY)
                        S.wait(SY)
                    else:
                        transpose_fm_to_tok(tok0, SP4)
                        V.wait(S)
                    scale, bias = ln_tile(SP4[:, :], lt, gmul=GX)
                    S.wait(V)
                    act(S, BUFH[:, lt * 1024:(lt + 1) * 1024], SP4[:, :],
                        AF.Identity, bias=bias, scale=scale)
                    V.wait(S)
                    red(V, AMB[:, lt * 64:(lt + 1) * 64],
                        BUFH[:, lt * 1024:(lt + 1) * 1024]
                        .rearrange("p (b k) -> p b k", k=16), OP.max, absval=True)
                nbt = nlt * 64
                tsc(V, SCH[:, :nbt], AMB[:, :nbt], 1.0 / 12.0, 224.0,
                    OP.mult, OP.min)
                cp(V, V8T[:, :nbt], SCH[:, :nbt])
                recip(V, SCH[:, :nbt], V8T[:, :nbt])
                tsc(V, S8B[:, :nbt], V8T[:, :nbt], 512.0)
                cp(V, STATS[:, 32:64], SUM[:, 0:32])
                tsc(V, RRB[:, :nbt], SCH[:, :nbt], 1e30, 0.5, OP.min, OP.mult)
                for lt in range(nlt):
                    fp4(BUFH[:, lt * 1024:(lt + 1) * 1024], 1024,
                        RRB[:, lt * 64:(lt + 1) * 64],
                        S8B[:, lt * 64:(lt + 1) * 64],
                        FQ0, FQ1, FQ2, QTOK[:, :1024], True, VQ8)
                    for g in range(G):
                        slot = g % 8
                        PE.wait(V)
                        PE.wait(S)
                        tr(PTB[:, slot * 128:slot * 128 + 128],
                           QTOK[:, g * 128:(g + 1) * 128], IDB[:, :])
                        S.wait(PE)
                        act(S, QXK[:, g * tt + lt * 128:g * tt + lt * 128 + 128],
                            PTB[:, slot * 128:slot * 128 + 128], AF.Identity)
                for hh in range(2):
                    gu_pe, gu_v, gu_s = [], [], []
                    for m16 in range(16):
                        m = hh * 16 + m16
                        bg, bu = PB[2 * (m % 2)], PB[2 * (m % 2) + 1]
                        wql = WQ[:, (m % 2) * 2048:(m % 2) * 2048 + 1024]
                        wqu = WQ[:, (m % 2) * 2048 + 1024:(m % 2) * 2048 + 2048]
                        if m16 < 2:
                            SY.wait(PE)
                            GP.wait(PE)
                        else:
                            SY.wait(PE, gu_pe[m16 - 2])
                            GP.wait(PE, gu_pe[m16 - 2])
                        dma(SY, wql.rearrange("p (g c) -> p g c", g=G),
                            wsh_o.ap()[m // 4, (m % 4) * 8:(m % 4) * 8 + 8,
                                       :, :].rearrange("g p c -> p g c"))
                        dma(GP, wqu.rearrange("p (g c) -> p g c", g=G),
                            wsh_o.ap()[m // 4, 32 + (m % 4) * 8:
                                       32 + (m % 4) * 8 + 8,
                                       :, :].rearrange("g p c -> p g c"))
                        PE.wait(SY)
                        PE.wait(GP)
                        if m16 < 2:
                            PE.wait(V)
                            PE.wait(S)
                        else:
                            PE.wait(V, gu_v[m16 - 2])
                            PE.wait(S, gu_s[m16 - 2])
                        for k in range(G):
                            mm(bg[:, :tt], wql[:, k * 128:(k + 1) * 128],
                               QXK[:, k * tt:k * tt + tt], k == 0, k == G - 1,
                               inc=(1 if k == G - 1 else 0))
                        for k in range(G):
                            mm(bu[:, :tt], wqu[:, k * 128:(k + 1) * 128],
                               QXK[:, k * tt:k * tt + tt], k == 0, k == G - 1,
                               inc=(1 if k == G - 1 else 0))
                        gu_pe.append(PE.count)
                        S.wait(PE)
                        sgt = SP4[:, :512] if m % 2 == 0 else SP4[:, 512:1024]
                        act(S, sgt[:, :tt], bg[:, :tt], AF.Silu, scale=INVG1)
                        V.wait(S)
                        hdst = BUFH[:, m16 * 512:m16 * 512 + tt]
                        stt(V, hdst, bu[:, :tt], INVG2, sgt[:, :tt],
                            OP.mult, OP.mult)
                        gu_v.append(V.count)
                        S.wait(V)
                        act(S, HSQ[:, :tt], hdst, AF.Square)
                        gu_s.append(S.count)
                        V.wait(S)
                        if m16 == 0:
                            cp(V, AMH[:, :tt], HSQ[:, :tt])
                        else:
                            tt_(V, AMH[:, :tt], AMH[:, :tt], HSQ[:, :tt], OP.max)
                    S.wait(V)
                    act(S, AMH[:, :tt], AMH[:, :tt], AF.Sqrt)
                    V.wait(S)
                    sc_stage(AMH[:, :tt], tt, GD12, GD2, SCH, S8B, V8T)
                    dn_pe = []
                    for m16 in range(16):
                        k = hh * 16 + m16
                        qslot = QHD[:, (m16 % 3) * 512:(m16 % 3) * 512 + tt]
                        if m16 >= 2:
                            V.wait(PE)
                        fp4(BUFH[:, m16 * 512:m16 * 512 + tt], tt,
                            SCH[:, :tt], S8B[:, :tt],
                            HQ0, HQ1, HQ2, qslot, False, VQ8)
                        wqd = WQ[:, (m16 % 2) * 1024:(m16 % 2) * 1024 + 1024]
                        wq_q = SY if m16 % 2 == 0 else GP
                        if m16 < 2:
                            wq_q.wait(PE)
                        else:
                            wq_q.wait(PE, dn_pe[m16 - 2])
                        dma(wq_q, wqd.rearrange("p (g c) -> p g c", g=G),
                            wsh_o.ap()[:, 64 + k, :, :]
                            .rearrange("g p c -> p g c"))
                        PE.wait(wq_q)
                        PE.wait(V)
                        for m in range(G):
                            mm(PB[m][:, :tt], wqd[:, m * 128:(m + 1) * 128],
                               qslot, m16 == 0, m16 == 15,
                               inc=(1 if m == G - 1 else 0))
                        dn_pe.append(PE.count)
                    V.wait(PE)
                    for m in range(G):
                        xsl = X[:, m * t + it * tt:m * t + it * tt + tt]
                        stt(V, xsl, PB[m][:, :tt], HINVD, xsl, OP.mult, OP.add)

        def attn():
            LGA = QXF[:, 0:512]
            PEX = QXF[:, 512:1024]
            WEX = QXF[:, 1024:1536]
            TM0 = QXF[:, 1536:2048]
            SIT = QTOK.ap().bitcast(F32)
            AT = AMH
            XG = BUFH[:, 0:G * 512]
            VC = BUFH[:, G * 512:2 * G * 512]
            U = BUFA

            GPC.wait(V)
            GPC.wait(S)
            GPC.emit(lambda e: e.iota(RRB[:, :], [[1, 512]],
                                      channel_multiplier=0,
                                      allow_small_or_imprecise_dtypes=True),
                     inc=1)
            S.wait(GPC)

            cp(V, SUM[:, 32:48], ZRO[:, 0:16])
            for c in range(nch):
                ct0 = c * ch
                for lt in range(ch // 128):
                    tok0 = ct0 + lt * 128
                    transpose_fm_to_tok(tok0, SP4)
                    V.wait(S)
                    scale, bias = ln_tile(SP4[:, :], lt % 4)
                    S.wait(V)
                    act(S, SQS, SP4[:, :], AF.Identity, bias=bias, scale=scale)
                    PE.wait(S)
                    for g in range(G):
                        sl = g % 4
                        if sl == 0:
                            PE.wait(S)
                            PE.wait(V)
                        tr(PTR[:, sl * 128:sl * 128 + 128],
                           SQS[:, g * 128:(g + 1) * 128], IDF[:, :])
                        if sl == 3:
                            q0 = (g // 4) * 512
                            S.wait(PE)
                            act(S, SP4[:, q0:q0 + 512], PTR[:, :512],
                                AF.Identity)
                            V.wait(S)
                            for gg in range(g - 3, g + 1):
                                pseg = SP4[:, q0 + (gg % 4) * 128:
                                            q0 + (gg % 4) * 128 + 128]
                                act(S, XG[:, gg * ch + lt * 128:
                                          gg * ch + lt * 128 + 128],
                                    pseg, AF.Gelu_apprx_tanh)
                                xsl = X[:, gg * t + tok0:gg * t + tok0 + 128]
                                tt_(V, xsl, xsl, pseg, OP.add)
                for wname, bcol, isU in (("Wf1", 2, True), ("Wbx", 4, False)):
                    wf_pe = []
                    for k in range(G):
                        wq_q = SY if k % 2 == 0 else GP
                        if k < 2:
                            wq_q.wait(PE)
                        else:
                            wq_q.wait(PE, wf_pe[k - 2])
                        dma(wq_q, WS[:, (k % 2) * 1024:(k % 2) * 1024 + 1024],
                            W[wname].ap()[k * 128:(k + 1) * 128, :])
                        PE.wait(wq_q)
                        if k == 0:
                            PE.wait(S)
                            PE.wait(V)
                        for m in range(G):
                            mm(PB[m][:, :ch],
                               WS[:, (k % 2) * 1024 + m * 128:
                                  (k % 2) * 1024 + m * 128 + 128],
                               XG[:, k * ch:(k + 1) * ch], k == 0, k == G - 1,
                               inc=(1 if m == G - 1 else 0))
                        wf_pe.append(PE.count)
                    S.wait(PE)
                    for m in range(G):
                        if isU:
                            dst = U[:, m * t + ct0:m * t + ct0 + ch]
                        else:
                            dst = VC[:, m * ch:(m + 1) * ch]
                        act(S, dst, PB[m][:, :ch], AF.Identity,
                            bias=CONS[:, bcol * 8 + m:bcol * 8 + m + 1])
                for wname, bcol in (("Wgx", 5), ("Wga", 6)):
                    wg_pe = []
                    for k in range(G):
                        wq_q = SY if k % 2 == 0 else GP
                        if k < 2:
                            wq_q.wait(PE)
                        else:
                            wq_q.wait(PE, wg_pe[k - 2])
                        dma(wq_q, WS[:, (k % 2) * 1024:(k % 2) * 1024 + 1024],
                            W[wname].ap()[k * 128:(k + 1) * 128, :])
                        PE.wait(wq_q)
                        if k == 0:
                            PE.wait(S)
                            PE.wait(V)
                        for gg in range(G):
                            mm(PB[gg][:, :ch],
                               WS[:, (k % 2) * 1024 + gg * 128:
                                  (k % 2) * 1024 + gg * 128 + 128],
                               VC[:, k * ch:(k + 1) * ch], k == 0, k == G - 1,
                               inc=(1 if gg == G - 1 else 0))
                        wg_pe.append(PE.count)
                    S.wait(PE)
                    if wname == "Wgx":
                        for gg in range(G):
                            act(S, XG[:, gg * ch:gg * ch + ch], PB[gg][:, :ch],
                                AF.Sigmoid,
                                bias=CONS[:, bcol * 8 + gg:bcol * 8 + gg + 1])
                for g in range(G):
                    act(S, SP4[:, 512:512 + ch], PB[g][:, :ch], AF.Sigmoid,
                        bias=CONS[:, 6 * 8 + g:6 * 8 + g + 1])
                    V.wait(S)
                    tsc(V, LGA[:, :ch], SP4[:, 512:512 + ch], M8SP[:, g:g + 1])
                    S.wait(V)
                    act(S, AT[:, :ch], LGA[:, :ch], AF.Exp)
                    act(S, WEX[:, :ch], LGA[:, :ch], AF.Exp, scale=2.0)
                    GP.wait(S)
                    dma(GP, a_s.ap()[g, :, ct0:ct0 + ch], AT[:, :ch])
                    V.wait(S)
                    tsc(V, WEX[:, :ch], WEX[:, :ch], -1.0, 1.0, OP.mult, OP.add)
                    tsc(V, WEX[:, :ch], WEX[:, :ch], 0.0, None, OP.max)
                    S.wait(V)
                    act(S, TM0[:, :ch], WEX[:, :ch], AF.Sqrt)
                    V.wait(S)
                    tt_(V, SIT[:, :ch], XG[:, g * ch:g * ch + ch],
                        VC[:, g * ch:g * ch + ch], OP.mult)
                    tt_(V, SIT[:, :ch], SIT[:, :ch], TM0[:, :ch], OP.mult)
                    GP.wait(V)
                    dma(GP, si_s.ap()[g, :, ct0:ct0 + ch], SIT[:, :ch])
                    scan(V, PEX[:, :ch], ONECOL.broadcast_to([128, ch]),
                         LGA[:, :ch], 0.0)
                    tt_(V, PEX[:, :ch], PEX[:, :ch], LGA[:, :ch], OP.subtract)
                    S.wait(V)
                    act(S, WEX[:, :ch], PEX[:, :ch], AF.Exp,
                        bias=SUM[:, 32 + g:33 + g])
                    V.wait(S)
                    tt_(V, TM0[:, :ch], WEX[:, :ch], SIT[:, :ch], OP.mult)
                    red(V, STATS[:, 0:1], TM0[:, :ch], OP.add)
                    tt_(V, SUM[:, 40 + g:41 + g], SUM[:, 40 + g:41 + g],
                        STATS[:, 0:1], OP.add)
                    red(V, STATS[:, 1:2], LGA[:, :ch], OP.add)
                    tt_(V, SUM[:, 32 + g:33 + g], SUM[:, 32 + g:33 + g],
                        STATS[:, 1:2], OP.add)
            for g in range(G):
                for h2 in range(nh):
                    tsc(V, STATS[:, 2:3], LNAF[:, g:g + 1],
                        float(t - 1 - hw_ * h2))
                    tsc(V, STATS[:, 3:4], LNAF[:, g:g + 1], -1.0)
                    S.wait(V)
                    act(S, WEX[:, :hw_], RRB[:, :hw_], AF.Exp,
                        bias=STATS[:, 2:3], scale=STATS[:, 3:4])
                    V.wait(S)
                    tt_(V, TM0[:, :hw_],
                        U[:, g * t + h2 * hw_:g * t + h2 * hw_ + hw_],
                        WEX[:, :hw_], OP.mult)
                    red(V, STATS[:, 4 + h2:5 + h2], TM0[:, :hw_], OP.add)
                for h2 in range(1, nh):
                    tt_(V, STATS[:, 4:5], STATS[:, 4:5],
                        STATS[:, 4 + h2:5 + h2], OP.add)
                cp(V, SUM[:, 48 + g:49 + g], STATS[:, 4:5])
            S.wait(V)
            act(S, SUM[:, 56:64], SUM[:, 32:40], AF.Exp)
            V.wait(S)
            cp(V, EXC[:, 0:8], SUM[:, 48:56])
            cp(V, EXC[:, 8:16], SUM[:, 40:48])
            cp(V, EXC[:, 16:24], SUM[:, 56:64])
            GP.wait(V)
            dma(GP, cc_i.ap(), EXC[:, 0:24])
            GP.ops.append(lambda e, s=GP.sem, v=GP.count: e.wait_ge(s, v))

            def _cc(e):
                e.collective_compute(
                    "AllGather", OP.bypass,
                    replica_groups=[[0, 1, 2, 3], [4, 5, 6, 7]],
                    ins=[cc_i.ap().opt()], outs=[cc_o.ap().opt()]
                ).then_inc(CCSEM, 1)
            GP.ops.append(_cc)
            GP.ops.append(lambda e: e.wait_ge(CCSEM, 2))
            dma(GP, EXC[:, 0:96].rearrange("p (q c) -> p q c", q=4),
                cc_o.ap().rearrange("q p c -> p q c"))
            V.wait(GP)
            cp(V, STATS[:, 0:8], ZRO[:, 0:8])
            cp(V, STATS[:, 8:16], ZRO[:, 0:8])
            cp(V, STATS[:, 16:24], ZRO[:, 0:8])
            for qq in range(4):
                stt(V, STATS[:, 8:16], STATS[:, 0:8], SCL[:, 2 + qq:3 + qq],
                    STATS[:, 8:16], OP.mult, OP.add)
                tt_(V, STATS[:, 0:8], STATS[:, 0:8], ACORE, OP.mult)
                tt_(V, STATS[:, 0:8], STATS[:, 0:8],
                    EXC[:, qq * 24:qq * 24 + 8], OP.add)
            cp(V, STATS[:, 0:8], ZRO[:, 0:8])
            for qq in range(3, -1, -1):
                stt(V, STATS[:, 16:24], STATS[:, 0:8], SCL[:, 2 + qq:3 + qq],
                    STATS[:, 16:24], OP.mult, OP.add)
                tt_(V, STATS[:, 0:8], STATS[:, 0:8],
                    EXC[:, qq * 24 + 16:qq * 24 + 24], OP.mult)
                tt_(V, STATS[:, 0:8], STATS[:, 0:8],
                    EXC[:, qq * 24 + 8:qq * 24 + 16], OP.add)
            for g in range(G):
                scan(V, U[:, g * t:(g + 1) * t],
                     AFWD[:, g:g + 1].broadcast_to([128, t]),
                     U[:, g * t:(g + 1) * t], STATS[:, 8 + g:9 + g])
            for it in range(ntt):
                w2_pe = []
                for k in range(G):
                    wq_q = SY if k % 2 == 0 else GP
                    if k < 2:
                        wq_q.wait(PE)
                    else:
                        wq_q.wait(PE, w2_pe[k - 2])
                    dma(wq_q, WS[:, (k % 2) * 1024:(k % 2) * 1024 + 1024],
                        W["Wf2"].ap()[k * 128:(k + 1) * 128, :])
                    PE.wait(wq_q)
                    if k == 0:
                        PE.wait(V)
                    for m in range(G):
                        mm(PB[m][:, :tt],
                           WS[:, (k % 2) * 1024 + m * 128:
                              (k % 2) * 1024 + m * 128 + 128],
                           U[:, k * t + it * tt:k * t + it * tt + tt],
                           k == 0, k == G - 1,
                           inc=(1 if m == G - 1 else 0))
                    w2_pe.append(PE.count)
                V.wait(PE)
                for m in range(G):
                    xsl = X[:, m * t + it * tt:m * t + it * tt + tt]
                    stt(V, xsl, PB[m][:, :tt], CONS[:, 3 * 8 + m:3 * 8 + m + 1],
                        xsl, OP.add, OP.add)
            sc_v = []
            SY.wait(GP)
            for g in range(G):
                b0 = (g % 2) * 2 * t
                if g < 2:
                    SY.wait(V)
                else:
                    SY.wait(V, sc_v[g - 2])
                dma(SY, BUFH[:, b0:b0 + t], si_s.ap()[g, :, :])
                dma(SY, BUFH[:, b0 + t:b0 + 2 * t], a_s.ap()[g, :, :])
                V.wait(SY)
                scan(V, U[:, g * t:(g + 1) * t],
                     BUFH[:, b0 + 2 * t - 1:b0 + t - 1:-1],
                     (BUFH[:, t - 1::-1] if b0 == 0 else
                      BUFH[:, b0 + t - 1:b0 - 1:-1]),
                     STATS[:, 16 + g:17 + g])
                sc_v.append(V.count)
            for it in range(ntt):
                wb_pe = []
                for k in range(G):
                    wq_q = SY if k % 2 == 0 else GP
                    if k < 2:
                        wq_q.wait(PE)
                    else:
                        wq_q.wait(PE, wb_pe[k - 2])
                    dma(wq_q, WS[:, (k % 2) * 1024:(k % 2) * 1024 + 1024],
                        W["Wb2"].ap()[k * 128:(k + 1) * 128, :])
                    PE.wait(wq_q)
                    if k == 0:
                        PE.wait(V)
                    for m in range(G):
                        mm(PB[m][:, :tt],
                           WS[:, (k % 2) * 1024 + m * 128:
                              (k % 2) * 1024 + m * 128 + 128],
                           U[:, k * t + (t - (it + 1) * tt):
                              k * t + (t - it * tt)],
                           k == 0, k == G - 1,
                           inc=(1 if m == G - 1 else 0))
                    wb_pe.append(PE.count)
                V.wait(PE)
                for m in range(G):
                    xsl = X[:, m * t + it * tt:m * t + it * tt + tt]
                    stt(V, xsl, PB[m][:, tt - 1::-1],
                        CONS[:, 7 * 8 + m:7 * 8 + m + 1], xsl, OP.add, OP.add)

        def output():
            for lt in range(t // 128):
                tok0 = lt * 128
                transpose_fm_to_tok(tok0, SP4)
                V.wait(S)
                cp(V, BUFA[:, lt * 1024:(lt + 1) * 1024], SP4[:, :])
            SY.wait(V)
            dma(SY, y_out.ap().rearrange("(n p) c -> p n c", p=128),
                BUFA[:, 0:(t // 128) * 1024].rearrange("p (n c) -> p n c", c=D))
            SY.ops.append(lambda e, s=SY.sem, v=SY.count: e.wait_ge(s, v))

        if debug_out is not None:
            debug_out.update(locals())

        build_x()
        ffn(1)
        attn()
        ffn(3)
        output()

        with nc.Block() as block:
            block.sync(lambda e: [op(e) for op in SY.ops])
            block.gpsimd(lambda e: [op(e) for op in GP.ops])
            block.vector(lambda e: [op(e) for op in V.ops])
            block.scalar(lambda e: [op(e) for op in S.ops])
            block.tensor(lambda e: [op(e) for op in PE.ops])

    return nc


_PROG_CACHE = {}


def _get_prog(t):
    if t not in _PROG_CACHE:
        _PROG_CACHE[t] = build_program(t)
    return _PROG_CACHE[t]


def make_inmaps(x_full, wd, t=2048, ncore=8):
    vec_order = ("a_fwd", "a_param", "bf1", "bf2", "bbx", "bgx", "bga", "bb2")
    vecs8 = np.concatenate([np.asarray(wd[k], np.float32).reshape(8, 128)
                            for k in vec_order], axis=0)
    scal2 = np.stack([np.asarray(wd["s_in"], np.float32).reshape(1),
                      np.asarray(wd["s_in_down"], np.float32).reshape(1)])
    base = {
        "Wf1": np.asarray(wd["Wf1"], np.float32),
        "Wf2": np.asarray(wd["Wf2"], np.float32),
        "Wbx": np.asarray(wd["Wbx"], np.float32),
        "Wgx": np.asarray(wd["Wgx"], np.float32),
        "Wga": np.asarray(wd["Wga"], np.float32),
        "Wb2": np.asarray(wd["Wb2"], np.float32),
        "gate_w": np.asarray(wd["gate_w"], np.float32),
        "up_w": np.asarray(wd["up_w"], np.float32),
        "down_w": np.asarray(wd["down_w"], np.float32),
        "vecs8": vecs8, "scal2": scal2,
    }
    gw = np.asarray(wd["gate_w"], np.float32)
    uw = np.asarray(wd["up_w"], np.float32)
    dw = np.asarray(wd["down_w"], np.float32)
    p128 = 16 * np.arange(128)

    def tile_rows(w, it):
        return w[2048 * (it // 16) + (it % 16) + p128]

    nq = ncore // x_full.shape[0]
    in_maps = []
    for c in range(ncore):
        b, q = divmod(c, nq)
        oh = np.zeros((128, 4), np.float32)
        oh[:, q % 4] = 1.0
        m = dict(base)
        m["x"] = np.ascontiguousarray(x_full[b, q * t:(q + 1) * t], dtype=np.float32)
        m["onehot"] = oh
        m["gate_sh"] = np.stack([tile_rows(gw, 4 * c + j) for j in range(4)])
        m["up_sh"] = np.stack([tile_rows(uw, 4 * c + j) for j in range(4)])
        m["down_sh"] = np.ascontiguousarray(dw[c * 128:(c + 1) * 128, :])
        in_maps.append(m)
    return in_maps


def kernel(L0_attn_x, L0_attn_a_fwd, L0_attn_Wf1, L0_attn_bf1, L0_attn_Wf2,
           L0_attn_bf2, L0_attn_Wbx, L0_attn_bbx, L0_attn_Wgx, L0_attn_bgx,
           L0_attn_Wga, L0_attn_bga, L0_attn_a_param, L0_attn_Wb2, L0_attn_bb2,
           L0_ffn_gate_w, L0_ffn_up_w, L0_ffn_down_w, L0_ffn_s_in,
           L0_ffn_s_in_down):
    x = np.asarray(L0_attn_x, np.float32)
    wd = {
        "a_fwd": L0_attn_a_fwd, "a_param": L0_attn_a_param,
        "bf1": L0_attn_bf1, "bf2": L0_attn_bf2, "bbx": L0_attn_bbx,
        "bgx": L0_attn_bgx, "bga": L0_attn_bga, "bb2": L0_attn_bb2,
        "Wf1": L0_attn_Wf1, "Wf2": L0_attn_Wf2, "Wbx": L0_attn_Wbx,
        "Wgx": L0_attn_Wgx, "Wga": L0_attn_Wga, "Wb2": L0_attn_Wb2,
        "gate_w": L0_ffn_gate_w, "up_w": L0_ffn_up_w, "down_w": L0_ffn_down_w,
        "s_in": L0_ffn_s_in, "s_in_down": L0_ffn_s_in_down,
    }
    t = 2048
    nc = _get_prog(t)
    in_maps = make_inmaps(x, wd, t=t, ncore=8)
    res = run_bass_kernel_spmd(nc, in_maps, core_ids=list(range(8)))
    y = np.empty((2, 8192, 1024), np.float32)
    for c in range(8):
        b, q = divmod(c, 4)
        y[b, q * t:(q + 1) * t] = np.asarray(res.results[c]["y"], np.float32)
    return y


# version 07
# speedup vs baseline: 1.6627x; 1.0011x over previous
import numpy as np
import concourse.bass as bass
import concourse.mybir as mybir
from concourse.bass_utils import run_bass_kernel_spmd

F32 = mybir.dt.float32
BF16 = mybir.dt.bfloat16
FP8 = mybir.dt.float8e4
AF = mybir.ActivationFunctionType
OP = mybir.AluOpType
AX = mybir.AxisListType

D = 1024
G = 8
I_H = 4096
MI = 32
C8 = 2.0 ** -8
EPS = 1e-5


class Eng:
    def __init__(self, sem):
        self.ops = []
        self.sem = sem
        self.count = 0

    def emit(self, fn, inc=0):
        sem, n = self.sem, inc

        def op(e):
            ins = fn(e)
            if n:
                ins.then_inc(sem, n)
        self.ops.append(op)
        if inc:
            self.count += inc
        return self.count

    def wait(self, other, value=None):
        sem = other.sem
        v = other.count if value is None else value
        if v > 0:
            self.ops.append(lambda e, s=sem, vv=v: e.wait_ge(s, vv))


def build_program(t=2048, debug_out=None):
    tt = min(512, t)
    ntt = t // tt
    nlt = tt // 128
    ch = min(512, t)
    nch = t // ch
    hw_ = min(512, t)
    nh = t // hw_

    nc = bass.Bass("TRN2", target_bir_lowering=False, detect_race_conditions=False)

    x_in = nc.dram_tensor("x", [t, D], F32, kind="ExternalInput")
    y_out = nc.dram_tensor("y", [t, D], F32, kind="ExternalOutput")
    W = {}
    for nm in ("Wf1", "Wf2", "Wbx", "Wgx", "Wga", "Wb2"):
        W[nm] = nc.dram_tensor(nm, [D, D], F32, kind="ExternalInput")
    gate_w = nc.dram_tensor("gate_w", [I_H, D], F32, kind="ExternalInput")
    up_w = nc.dram_tensor("up_w", [I_H, D], F32, kind="ExternalInput")
    down_w = nc.dram_tensor("down_w", [D, I_H], F32, kind="ExternalInput")
    vecs8 = nc.dram_tensor("vecs8", [64, 128], F32, kind="ExternalInput")
    scal2 = nc.dram_tensor("scal2", [2, 1], F32, kind="ExternalInput")
    onehot = nc.dram_tensor("onehot", [128, 4], F32, kind="ExternalInput")

    qg_s = nc.dram_tensor("qg_s", [MI, G, 128, 128], BF16)
    qu_s = nc.dram_tensor("qu_s", [MI, G, 128, 128], BF16)
    qd_s = nc.dram_tensor("qd_s", [MI, G, 128, 128], BF16)
    si_s = nc.dram_tensor("si_s", [G, 128, t], F32)
    a_s = nc.dram_tensor("a_s", [G, 128, t], F32)
    cc_i = nc.dram_tensor("cc_i", [128, 24], F32)
    gate_sh = nc.dram_tensor("gate_sh", [4, 128, D], F32, kind="ExternalInput")
    up_sh = nc.dram_tensor("up_sh", [4, 128, D], F32, kind="ExternalInput")
    down_sh = nc.dram_tensor("down_sh", [128, I_H], F32, kind="ExternalInput")
    wsh_i = nc.dram_tensor("wsh_i", [96, 128, 128], BF16)
    wsh_o = nc.dram_tensor("wsh_o", [8, 96, 128, 128], BF16)
    cc_o = nc.dram_tensor("cc_o", [4, 128, 24], F32)

    from contextlib import ExitStack
    stack = ExitStack()
    with stack:
        sbuf = lambda n_, s, d=F32: stack.enter_context(nc.sbuf_tensor(n_, s, d))

        BUFA = sbuf("BUFA", [128, t * G])
        X = sbuf("X", [128, t * G])
        BUFH = sbuf("BUFH", [128, 8192])
        WS = sbuf("WS", [128, 2048])
        WQ = sbuf("WQ", [128, 4096], BF16)
        QXK = sbuf("QXK", [128, G * 512], BF16)
        QHD = sbuf("QHD", [128, 1536], BF16)
        QTOK = sbuf("QTOK", [128, 1024], BF16)
        VQ8 = sbuf("VQ8", [128, 1024], FP8)
        AMB = sbuf("AMB", [128, 256])
        SCH = sbuf("SCH", [128, 512])
        AMH = sbuf("AMH", [128, 512])
        HSQ = sbuf("HSQ", [128, 512])
        V8T = sbuf("V8T", [128, 512], FP8)
        RRB = sbuf("RRB", [128, 512])
        S8B = sbuf("S8B", [128, 512])
        SP4 = sbuf("SP4", [128, 1024])
        IDF = sbuf("IDF", [128, 128])
        IDB = sbuf("IDB", [128, 128], BF16)
        ZRO = sbuf("ZRO", [128, 16])
        GAPB = sbuf("GAPB", [128, 8])
        STATS = sbuf("STATS", [128, 64])
        CONS = sbuf("CONS", [128, 64])
        SUM = sbuf("SUM", [128, 64])
        SCL = sbuf("SCL", [128, 32])
        EXC = sbuf("EXC", [128, 96])
        GWS = sbuf("GWS", [128, 16])

        PB = [stack.enter_context(nc.psum_tensor(f"PB{i}", [128, 512], F32))
              for i in range(8)]
        PTR = PB[6]
        PTB = PB[7].ap().bitcast(BF16)

        sems = {k: stack.enter_context(nc.semaphore("s_" + k))
                for k in ("sy", "gp", "gpc", "v", "s", "pe", "cc")}
        SY, GP, V, S, PE = (Eng(sems[k]) for k in ("sy", "gp", "v", "s", "pe"))
        GPC = Eng(sems["gpc"])
        GPC.ops = GP.ops
        CCSEM = sems["cc"]

        FQ0 = BUFH[:, 4096:5120]
        FQ1 = BUFH[:, 5120:6144]
        FQ2 = BUFH[:, 6144:7168]
        SQS = BUFH[:, 7168:8192]
        QXF = QXK.ap().bitcast(F32)
        HQ0, HQ1, HQ2 = SP4[:, 0:512], SP4[:, 512:1024], HSQ

        def dma(eng, out, in_):
            return eng.emit(lambda e, o=out, i=in_: e.dma_start(out=o, in_=i),
                            inc=16)

        _vw = []

        def _rngs(aps):
            out = []
            for ap in aps:
                if ap is None or isinstance(ap, (int, float)):
                    continue
                try:
                    nm = ap.tensor.name
                    off = int(ap.offset)
                    sh = list(ap.shape)
                    try:
                        stv = list(ap.stride)
                    except Exception:
                        stv = None
                    if stv and len(stv) == len(sh) and len(sh) > 1:
                        span = 1 + sum((int(d) - 1) * abs(int(s))
                                       for d, s in zip(sh[1:], stv[1:]))
                    else:
                        span = 1
                        for d in sh[1:]:
                            span *= int(d)
                    out.append((nm, off, off + max(span, 1)))
                except Exception:
                    out.append(("?", 0, 1 << 62))
            return out

        def _fsz(ap):
            try:
                return int(ap.free_size)
            except Exception:
                pass
            try:
                n = 1
                for d in list(ap.shape)[1:]:
                    n *= int(d)
                return n
            except Exception:
                return 1

        def _vsmall(aps):
            return all(_fsz(ap) <= 128 for ap in aps)

        def _vraw(outs, ins):
            rd = _rngs(ins)
            need = 0
            for age, (small, wr) in enumerate(_vw):
                if not small:
                    continue
                for nm, lo, hi in wr:
                    for nm2, lo2, hi2 in rd:
                        if ((nm == nm2 or nm == "?" or nm2 == "?")
                                and lo < hi2 and lo2 < hi):
                            need = max(need, 2 - age)
            for _ in range(need):
                V.emit(lambda e: e.tensor_copy(GAPB[:, 0:8], ZRO[:, 0:8]),
                       inc=0)
                _vw.insert(0, (False, []))
            _vw.insert(0, (_vsmall(outs), _rngs(outs)))
            del _vw[2:]

        def tsc(eng, out, in0, s1, s2=None, op0=OP.mult, op1=None):
            if eng is V:
                _vraw([out], [in0, s1, s2])
            if op1 is None:
                return eng.emit(lambda e, o=out, i=in0, a=s1, p0=op0:
                                e.tensor_scalar(o, i, a, None, p0), inc=1)
            return eng.emit(lambda e, o=out, i=in0, a=s1, b=s2, p0=op0, p1=op1:
                            e.tensor_scalar(o, i, a, b, p0, p1), inc=1)

        def tt_(eng, out, a, b, op):
            if eng is V:
                _vraw([out], [a, b])
            return eng.emit(lambda e, o=out, x=a, y=b, p=op:
                            e.tensor_tensor(o, x, y, p), inc=1)

        def stt(eng, out, in0, sc, in1, op0, op1):
            if eng is V:
                _vraw([out], [in0, sc, in1])
            return eng.emit(lambda e, o=out, i0=in0, s=sc, i1=in1, p0=op0, p1=op1:
                            e.scalar_tensor_tensor(o, i0, s, i1, p0, p1), inc=1)

        def act(eng, out, in_, fn, bias=0.0, scale=1.0, accum=None):
            return eng.emit(lambda e, o=out, i=in_, f=fn, b=bias, s=scale, a=accum:
                            e.activation(o, i, f, bias=b, scale=s, accum_out=a),
                            inc=1)

        def cp(eng, out, in_):
            if eng is V:
                _vraw([out], [in_])
            return eng.emit(lambda e, o=out, i=in_: e.tensor_copy(o, i), inc=1)

        def recip(eng, out, in_):
            if eng is V:
                _vraw([out], [in_])
                _vw[0] = (True, _vw[0][1])
            return eng.emit(lambda e, o=out, i=in_: e.reciprocal(o, i), inc=1)

        def red(eng, out, in_, op, absval=False):
            if eng is V:
                _vraw([out], [in_])
            return eng.emit(lambda e, o=out, i=in_, p=op, a=absval:
                            e.tensor_reduce(o, i, AX.X, p, apply_absolute_value=a),
                            inc=1)

        def mm(out, lhsT, rhs, start, stop, inc=0):
            return PE.emit(lambda e, o=out, l=lhsT, r=rhs, a=start, b=stop:
                           e.matmul(o, l, r, start=a, stop=b), inc=inc)

        def tr(out, in_, ident):
            return PE.emit(lambda e, o=out, i=in_, d=ident:
                           e.transpose(o, i, d), inc=1)

        def scan(eng, out, d0, d1, init):
            if eng is V:
                _vraw([out], [d0, d1, init])
            return eng.emit(lambda e, o=out, a=d0, b=d1, i=init:
                            e.tensor_tensor_scan(o, a, b, i, OP.mult, OP.add),
                            inc=1)

        dma(SY, SP4[:64, :128], vecs8.ap())
        dma(SY, SCL[:, 0:1], scal2[0:1, :].broadcast_to([128, 1]))
        dma(SY, SCL[:, 1:2], scal2[1:2, :].broadcast_to([128, 1]))
        dma(SY, SCL[:, 2:6], onehot.ap())
        GPC.emit(lambda e: e.iota(IDF[:, :], [[1, 128]], channel_multiplier=-1,
                                  allow_small_or_imprecise_dtypes=True), inc=1)
        GPC.emit(lambda e: e.memset(ZRO[:, :], 0.0), inc=1)
        GPC.emit(lambda e: e.memset(SP4[:, 512:640], 1.0), inc=1)
        V.wait(GPC)
        tsc(V, SCL[:, 31:32], ZRO[:, 0:1], 1.0, None, OP.add)
        ONECOL = SCL[:, 31:32]
        tsc(V, IDB[:, :], IDF[:, :], 0.0, None, OP.is_equal)
        tsc(V, IDF[:, :], IDF[:, :], 0.0, None, OP.is_equal)
        PE.wait(SY)
        PE.wait(V)
        tr(PTR[:, :64], SP4[:64, :128], IDF[:64, :64])
        S.wait(PE)
        act(S, CONS[:, :], PTR[:128, :64], AF.Identity)
        V.wait(S)
        AFWD = CONS[:, 0:8]
        S.wait(V)
        act(S, SUM[:, 0:8], CONS[:, 8:16], AF.Exp)
        V.wait(S)
        tsc(V, SUM[:, 0:8], SUM[:, 0:8], 1.0, None, OP.add)
        S.wait(V)
        act(S, SUM[:, 0:8], SUM[:, 0:8], AF.Ln)
        act(S, SUM[:, 16:24], CONS[:, 0:8], AF.Ln)
        V.wait(S)
        tsc(V, SUM[:, 8:16], SUM[:, 0:8], -8.0)
        tsc(V, SCL[:, 30:31], ZRO[:, 0:1], EPS, None, OP.add)
        M8SP = SUM[:, 8:16]
        LNAF = SUM[:, 16:24]
        cp(V, SUM[:, 24:32], CONS[:, 0:8])
        for _ in range(int(round(np.log2(t)))):
            tt_(V, SUM[:, 24:32], SUM[:, 24:32], SUM[:, 24:32], OP.mult)
        ACORE = SUM[:, 24:32]

        def gate_rows_ap(src, it):
            b0 = 2048 * (it // 16) + (it % 16)
            return bass.AP(src, b0 * D, [[16 * D, 128], [1, D]])

        def global_absmax(src_ap_fn, ntile, npiece, out_col):
            tt_(V, GWS[:, out_col:out_col + 1], ZRO[:, 0:1], ZRO[:, 0:1], OP.add)
            for it in range(ntile):
                for piece in range(npiece):
                    j = (it * npiece + piece) % 2
                    dst = WS[:, j * 1024:(j + 1) * 1024]
                    SY.wait(V)
                    dma(SY, dst, src_ap_fn(it, piece))
                    V.wait(SY)
                    red(V, GWS[:, out_col + 1:out_col + 2], dst, OP.max,
                        absval=True)
                    tt_(V, GWS[:, out_col:out_col + 1],
                        GWS[:, out_col:out_col + 1],
                        GWS[:, out_col + 1:out_col + 2], OP.max)
            PE.wait(V)
            PE.wait(S)
            tr(PTR[:1, 0:128], GWS[:, out_col:out_col + 1], IDF[:, :])
            S.wait(PE)
            act(S, AMB[:1, 128:256], PTR[:1, 0:128], AF.Identity)
            V.wait(S)
            red(V, GWS[:1, out_col:out_col + 1], AMB[:1, 128:256], OP.max)
            PE.wait(V)
            mm(PTR[:, 128:129], SP4[:1, 512:640], GWS[:1, out_col:out_col + 1],
               True, True, inc=1)
            V.wait(PE)
            cp(V, GWS[:, out_col:out_col + 1], PTR[:, 128:129])

        global_absmax(lambda it, p: gate_rows_ap(gate_w, it), MI, 1, 0)
        global_absmax(lambda it, p: gate_rows_ap(up_w, it), MI, 1, 4)
        global_absmax(lambda it, p: down_w.ap()[it * 128:(it + 1) * 128,
                                                p * 1024:(p + 1) * 1024],
                      G, 4, 8)
        for col in (0, 4, 8):
            recip(V, GWS[:, col + 2:col + 3], GWS[:, col:col + 1])
        cp(V, SP4[:, 0:128], IDF[:, :])
        cp(V, SP4[:, 128:256], IDF[:, :])
        for j, col in enumerate((0, 4, 8)):
            tsc(V, SCL[:, 6 + j:7 + j], GWS[:, col + 2:col + 3], 2688.0)
        tt_(V, SCL[:, 16:17], SCL[:, 0:1], SCL[:, 6:7], OP.mult)
        tt_(V, SCL[:, 17:18], SCL[:, 0:1], SCL[:, 7:8], OP.mult)
        tt_(V, SCL[:, 18:19], SCL[:, 1:2], SCL[:, 8:9], OP.mult)
        for j in range(3):
            recip(V, SCL[:, 19 + j:20 + j], SCL[:, 16 + j:17 + j])
        cp(V, SP4[:, 0:128], IDF[:, :])
        cp(V, SP4[:, 128:256], IDF[:, :])
        cp(V, SCL[:, 9:10], SCL[:, 19:20])
        cp(V, SCL[:, 10:11], SCL[:, 20:21])
        tsc(V, SCL[:, 11:12], SCL[:, 21:22], 0.5)
        tsc(V, SCL[:, 12:13], SCL[:, 1:2], 1.0 / 12.0)
        tsc(V, SCL[:, 13:14], SCL[:, 1:2], 0.5)
        tsc(V, SCL[:, 24:25], SCL[:, 6:7], 1.0 / 12.0)
        tsc(V, SCL[:, 26:27], SCL[:, 6:7], 0.5)
        tsc(V, SCL[:, 25:26], SCL[:, 7:8], 1.0 / 12.0)
        tsc(V, SCL[:, 27:28], SCL[:, 7:8], 0.5)
        tsc(V, SCL[:, 28:29], SCL[:, 8:9], 1.0 / 12.0)
        tsc(V, SCL[:, 29:30], SCL[:, 8:9], 0.5)
        GX = SCL[:, 0:1]
        INVG1, INVG2, HINVD = SCL[:, 9:10], SCL[:, 10:11], SCL[:, 11:12]
        GD12, GD2 = SCL[:, 12:13], SCL[:, 13:14]

        def fp4(src, width, rr, s8, t0, t1, t2, outq, blk16, vq):
            nb = width // 16
            if blk16:
                tt_(V, t0[:, :width].rearrange("p (b k) -> p b k", k=16),
                    src.rearrange("p (b k) -> p b k", k=16),
                    rr.unsqueeze(2).broadcast_to([128, nb, 16]), OP.mult)
            else:
                tt_(V, t0[:, :width], src, rr, OP.mult)
            tsc(V, t0[:, :width], t0[:, :width], 6.5, -6.5, OP.min, OP.max)
            tsc(V, t1[:, :width], t0[:, :width], 2.0, 0.0, OP.subtract, OP.max)
            tsc(V, t2[:, :width], t0[:, :width], -2.0, 0.0, OP.subtract, OP.min)
            tt_(V, t1[:, :width], t1[:, :width], t2[:, :width], OP.add)
            tsc(V, t2[:, :width], t0[:, :width], 4.0, 0.0, OP.subtract, OP.max)
            stt(V, t2[:, :width], t2[:, :width], -0.25, t0[:, :width],
                OP.mult, OP.add)
            tsc(V, t0[:, :width], t0[:, :width], -4.0, 0.0, OP.subtract, OP.min)
            stt(V, t0[:, :width], t0[:, :width], -0.25, t2[:, :width],
                OP.mult, OP.add)
            stt(V, t0[:, :width], t1[:, :width], -0.5, t0[:, :width],
                OP.mult, OP.add)
            tsc(V, vq[:, :width], t0[:, :width], C8)
            tsc(V, t1[:, :width], vq[:, :width], 2 * C8, 0.0, OP.subtract, OP.max)
            tsc(V, t2[:, :width], vq[:, :width], -2 * C8, 0.0, OP.subtract, OP.min)
            tt_(V, t1[:, :width], t1[:, :width], t2[:, :width], OP.add)
            tsc(V, t2[:, :width], vq[:, :width], 3 * C8, 0.0, OP.subtract, OP.max)
            stt(V, t2[:, :width], t2[:, :width], 2.0, t1[:, :width],
                OP.mult, OP.add)
            tsc(V, t1[:, :width], vq[:, :width], -3 * C8, 0.0, OP.subtract,
                OP.min)
            stt(V, t1[:, :width], t1[:, :width], 2.0, t2[:, :width],
                OP.mult, OP.add)
            tt_(V, t1[:, :width], vq[:, :width], t1[:, :width], OP.add)
            if blk16:
                tt_(V, outq.rearrange("p (b k) -> p b k", k=16),
                    t1[:, :width].rearrange("p (b k) -> p b k", k=16),
                    s8.unsqueeze(2).broadcast_to([128, nb, 16]), OP.mult)
            else:
                tt_(V, outq, t1[:, :width], s8, OP.mult)

        def sc_stage(amax_ap, n, mul_ap, half_ap, rr_out, s8_out, v8t):
            tt_(V, rr_out[:, :n], amax_ap, mul_ap.broadcast_to([128, n]), OP.mult)
            tsc(V, rr_out[:, :n], rr_out[:, :n], 1.0, 224.0, OP.mult, OP.min)
            cp(V, v8t[:, :n], rr_out[:, :n])
            recip(V, rr_out[:, :n], v8t[:, :n])
            tsc(V, s8_out[:, :n], v8t[:, :n], 512.0)
            cp(V, STATS[:, 32:64], SUM[:, 0:32])
            tsc(V, rr_out[:, :n], rr_out[:, :n], 1e30, None, OP.min)
            tt_(V, rr_out[:, :n], rr_out[:, :n],
                half_ap.broadcast_to([128, n]), OP.mult)

        def wprep_gateup(src, base, which):
            mul_ap = SCL[:, 24 + which:25 + which]
            half_ap = SCL[:, 26 + which:27 + which]
            for pair in range(2):
                its = (2 * pair, 2 * pair + 1)
                for j, it in enumerate(its):
                    SY.wait(V)
                    SY.wait(PE)
                    dma(SY, WS[:, j * 1024:(j + 1) * 1024], src.ap()[it])
                V.wait(SY)
                for j in range(2):
                    red(V, AMB[:, j * 64:(j + 1) * 64],
                        WS[:, j * 1024:(j + 1) * 1024]
                        .rearrange("p (b k) -> p b k", k=16), OP.max, absval=True)
                sc_stage(AMB[:, :128], 128, mul_ap, half_ap, RRB, S8B, V8T)
                for j, it in enumerate(its):
                    fp4(WS[:, j * 1024:(j + 1) * 1024], 1024,
                        RRB[:, j * 64:(j + 1) * 64], S8B[:, j * 64:(j + 1) * 64],
                        FQ0, FQ1, FQ2, QTOK[:, :1024], True, VQ8)
                    for g in range(G):
                        slot = g % 8
                        PE.wait(V)
                        PE.wait(S)
                        tr(PTB[:, slot * 128:slot * 128 + 128],
                           QTOK[:, g * 128:(g + 1) * 128], IDB[:, :])
                        S.wait(PE)
                        act(S, WQ[:, slot * 128:slot * 128 + 128],
                            PTB[:, slot * 128:slot * 128 + 128], AF.Identity)
                        GP.wait(S)
                        dma(GP, wsh_i.ap()[base + it * 8 + g, :, :],
                            WQ[:, slot * 128:slot * 128 + 128])

        wprep_gateup(gate_sh, 0, 0)
        wprep_gateup(up_sh, 32, 1)

        def wprep_down():
            for it in range(1):
                for half in range(2):
                    SY.wait(V)
                    SY.wait(PE)
                    dma(SY, WS[:, :2048],
                        down_sh.ap()[:, half * 2048:(half + 1) * 2048])
                    V.wait(SY)
                    red(V, AMB[:, 0:128],
                        WS[:, :2048].rearrange("p (b k) -> p b k", k=16),
                        OP.max, absval=True)
                    sc_stage(AMB[:, :128], 128, SCL[:, 28:29], SCL[:, 29:30],
                             RRB, S8B, V8T)
                    for j in range(2):
                        fp4(WS[:, j * 1024:(j + 1) * 1024], 1024,
                            RRB[:, j * 64:(j + 1) * 64],
                            S8B[:, j * 64:(j + 1) * 64],
                            FQ0, FQ1, FQ2, QTOK[:, :1024], True, VQ8)
                        cp(V, WQ[:, (2 + j) * 1024:(3 + j) * 1024],
                           QTOK[:, :1024])
                    for kk in range(16):
                        tau = half * 16 + kk
                        slot = kk % 8
                        PE.wait(V)
                        PE.wait(S)
                        tr(PTB[:, slot * 128:slot * 128 + 128],
                           WQ[:, 2048 + kk:4096:16], IDB[:, :])
                        S.wait(PE)
                        act(S, WQ[:, slot * 128:slot * 128 + 128],
                            PTB[:, slot * 128:slot * 128 + 128], AF.Identity)
                        GP.wait(S)
                        dma(GP, wsh_i.ap()[64 + tau, :, :],
                            WQ[:, slot * 128:slot * 128 + 128])

        wprep_down()
        GP.ops.append(lambda e, s=GP.sem, v=GP.count: e.wait_ge(s, v))

        def _wcc(e):
            e.collective_compute(
                "AllGather", OP.bypass,
                replica_groups=[[0, 1, 2, 3, 4, 5, 6, 7]],
                ins=[wsh_i.ap().opt()], outs=[wsh_o.ap().opt()]
            ).then_inc(CCSEM, 1)
        GP.ops.append(_wcc)

        def ln_tile(xtok, c, gmul=None):
            red(V, STATS[:, c:c + 1], xtok, OP.add)
            S.wait(V)
            act(S, SQS, xtok, AF.Square, accum=STATS[:, 8 + c:9 + c])
            V.wait(S)
            tsc(V, STATS[:, 16 + c:17 + c], STATS[:, c:c + 1], 1.0 / D)
            tsc(V, STATS[:, 20 + c:21 + c], STATS[:, 8 + c:9 + c], 1.0 / D)
            tt_(V, STATS[:, 24 + c:25 + c], STATS[:, 16 + c:17 + c],
                STATS[:, 16 + c:17 + c], OP.mult)
            tt_(V, STATS[:, 20 + c:21 + c], STATS[:, 20 + c:21 + c],
                STATS[:, 24 + c:25 + c], OP.subtract)
            S.wait(V)
            act(S, STATS[:, 24 + c:25 + c], STATS[:, 20 + c:21 + c], AF.Sqrt,
                bias=SCL[:, 30:31])
            V.wait(S)
            recip(V, STATS[:, 28 + c:29 + c], STATS[:, 24 + c:25 + c])
            cp(V, STATS[:, 32:48], SUM[:, 0:16])
            cp(V, STATS[:, 48:64], SUM[:, 16:32])
            if gmul is not None:
                tt_(V, STATS[:, 28 + c:29 + c], STATS[:, 28 + c:29 + c],
                    gmul, OP.mult)
            tt_(V, STATS[:, 24 + c:25 + c], STATS[:, 16 + c:17 + c],
                STATS[:, 28 + c:29 + c], OP.mult)
            tsc(V, STATS[:, 24 + c:25 + c], STATS[:, 24 + c:25 + c], -1.0)
            return STATS[:, 28 + c:29 + c], STATS[:, 24 + c:25 + c]

        def transpose_fm_to_tok(tok0, dst):
            for g in range(G):
                sl = g % 4
                if sl == 0:
                    PE.wait(S)
                    PE.wait(V)
                tr(PTR[:, sl * 128:sl * 128 + 128],
                   X[:, g * t + tok0:g * t + tok0 + 128], IDF[:, :])
                if sl == 3:
                    S.wait(PE)
                    act(S, dst[:, (g - 3) * 128:(g + 1) * 128], PTR[:, :512],
                        AF.Identity)

        def build_x():
            for lt in range(t // 128):
                tok0 = lt * 128
                SY.wait(V)
                SY.wait(PE)
                dma(SY, SP4[:, :], x_in.ap()[tok0:tok0 + 128, :])
                PE.wait(SY)
                for g in range(G):
                    sl = g % 4
                    if sl == 0:
                        PE.wait(S)
                    tr(PTR[:, sl * 128:sl * 128 + 128],
                       SP4[:, g * 128:(g + 1) * 128], IDF[:, :])
                    if sl == 3:
                        S.wait(PE)
                        for gg in range(g - 3, g + 1):
                            act(S, X[:, gg * t + tok0:gg * t + tok0 + 128],
                                PTR[:, (gg % 4) * 128:(gg % 4) * 128 + 128],
                                AF.Identity)

        def ffn(sub):
            for it in range(ntt):
                for lt in range(nlt):
                    tok0 = it * tt + lt * 128
                    if sub == 1:
                        SY.wait(V)
                        SY.wait(S)
                        dma(SY, SP4[:, :], x_in.ap()[tok0:tok0 + 128, :])
                        V.wait(SY)
                        S.wait(SY)
                    else:
                        transpose_fm_to_tok(tok0, SP4)
                        V.wait(S)
                    scale, bias = ln_tile(SP4[:, :], lt, gmul=GX)
                    S.wait(V)
                    act(S, BUFH[:, lt * 1024:(lt + 1) * 1024], SP4[:, :],
                        AF.Identity, bias=bias, scale=scale)
                    V.wait(S)
                    red(V, AMB[:, lt * 64:(lt + 1) * 64],
                        BUFH[:, lt * 1024:(lt + 1) * 1024]
                        .rearrange("p (b k) -> p b k", k=16), OP.max, absval=True)
                nbt = nlt * 64
                tsc(V, SCH[:, :nbt], AMB[:, :nbt], 1.0 / 12.0, 224.0,
                    OP.mult, OP.min)
                cp(V, V8T[:, :nbt], SCH[:, :nbt])
                recip(V, SCH[:, :nbt], V8T[:, :nbt])
                tsc(V, S8B[:, :nbt], V8T[:, :nbt], 512.0)
                cp(V, STATS[:, 32:64], SUM[:, 0:32])
                tsc(V, RRB[:, :nbt], SCH[:, :nbt], 1e30, 0.5, OP.min, OP.mult)
                for lt in range(nlt):
                    fp4(BUFH[:, lt * 1024:(lt + 1) * 1024], 1024,
                        RRB[:, lt * 64:(lt + 1) * 64],
                        S8B[:, lt * 64:(lt + 1) * 64],
                        FQ0, FQ1, FQ2, QTOK[:, :1024], True, VQ8)
                    for g in range(G):
                        slot = g % 8
                        PE.wait(V)
                        PE.wait(S)
                        tr(PTB[:, slot * 128:slot * 128 + 128],
                           QTOK[:, g * 128:(g + 1) * 128], IDB[:, :])
                        S.wait(PE)
                        act(S, QXK[:, g * tt + lt * 128:g * tt + lt * 128 + 128],
                            PTB[:, slot * 128:slot * 128 + 128], AF.Identity)
                for hh in range(2):
                    gu_pe, gu_v, gu_s = [], [], []
                    for m16 in range(16):
                        m = hh * 16 + m16
                        bg, bu = PB[2 * (m % 2)], PB[2 * (m % 2) + 1]
                        wql = WQ[:, (m % 2) * 2048:(m % 2) * 2048 + 1024]
                        wqu = WQ[:, (m % 2) * 2048 + 1024:(m % 2) * 2048 + 2048]
                        if sub == 1 and it == 0 and hh == 0 and m16 == 0:
                            SY.ops.append(lambda e: e.wait_ge(CCSEM, 1))
                            GP.ops.append(lambda e: e.wait_ge(CCSEM, 1))
                        if m16 < 2:
                            SY.wait(PE)
                            GP.wait(PE)
                        else:
                            SY.wait(PE, gu_pe[m16 - 2])
                            GP.wait(PE, gu_pe[m16 - 2])
                        dma(SY, wql.rearrange("p (g c) -> p g c", g=G),
                            wsh_o.ap()[m // 4, (m % 4) * 8:(m % 4) * 8 + 8,
                                       :, :].rearrange("g p c -> p g c"))
                        dma(GP, wqu.rearrange("p (g c) -> p g c", g=G),
                            wsh_o.ap()[m // 4, 32 + (m % 4) * 8:
                                       32 + (m % 4) * 8 + 8,
                                       :, :].rearrange("g p c -> p g c"))
                        PE.wait(SY)
                        PE.wait(GP)
                        if m16 < 2:
                            PE.wait(V)
                            PE.wait(S)
                        else:
                            PE.wait(V, gu_v[m16 - 2])
                            PE.wait(S, gu_s[m16 - 2])
                        for k in range(G):
                            mm(bg[:, :tt], wql[:, k * 128:(k + 1) * 128],
                               QXK[:, k * tt:k * tt + tt], k == 0, k == G - 1,
                               inc=(1 if k == G - 1 else 0))
                        for k in range(G):
                            mm(bu[:, :tt], wqu[:, k * 128:(k + 1) * 128],
                               QXK[:, k * tt:k * tt + tt], k == 0, k == G - 1,
                               inc=(1 if k == G - 1 else 0))
                        gu_pe.append(PE.count)
                        S.wait(PE)
                        sgt = SP4[:, :512] if m % 2 == 0 else SP4[:, 512:1024]
                        act(S, sgt[:, :tt], bg[:, :tt], AF.Silu, scale=INVG1)
                        V.wait(S)
                        hdst = BUFH[:, m16 * 512:m16 * 512 + tt]
                        stt(V, hdst, bu[:, :tt], INVG2, sgt[:, :tt],
                            OP.mult, OP.mult)
                        gu_v.append(V.count)
                        S.wait(V)
                        act(S, HSQ[:, :tt], hdst, AF.Square)
                        gu_s.append(S.count)
                        V.wait(S)
                        if m16 == 0:
                            cp(V, AMH[:, :tt], HSQ[:, :tt])
                        else:
                            tt_(V, AMH[:, :tt], AMH[:, :tt], HSQ[:, :tt], OP.max)
                    S.wait(V)
                    act(S, AMH[:, :tt], AMH[:, :tt], AF.Sqrt)
                    V.wait(S)
                    sc_stage(AMH[:, :tt], tt, GD12, GD2, SCH, S8B, V8T)
                    dn_pe = []
                    for m16 in range(16):
                        k = hh * 16 + m16
                        qslot = QHD[:, (m16 % 3) * 512:(m16 % 3) * 512 + tt]
                        if m16 >= 2:
                            V.wait(PE)
                        fp4(BUFH[:, m16 * 512:m16 * 512 + tt], tt,
                            SCH[:, :tt], S8B[:, :tt],
                            HQ0, HQ1, HQ2, qslot, False, VQ8)
                        wqd = WQ[:, (m16 % 2) * 1024:(m16 % 2) * 1024 + 1024]
                        wq_q = SY if m16 % 2 == 0 else GP
                        if m16 < 2:
                            wq_q.wait(PE)
                        else:
                            wq_q.wait(PE, dn_pe[m16 - 2])
                        dma(wq_q, wqd.rearrange("p (g c) -> p g c", g=G),
                            wsh_o.ap()[:, 64 + k, :, :]
                            .rearrange("g p c -> p g c"))
                        PE.wait(wq_q)
                        PE.wait(V)
                        for m in range(G):
                            mm(PB[m][:, :tt], wqd[:, m * 128:(m + 1) * 128],
                               qslot, m16 == 0, m16 == 15,
                               inc=(1 if m == G - 1 else 0))
                        dn_pe.append(PE.count)
                    V.wait(PE)
                    for m in range(G):
                        xsl = X[:, m * t + it * tt:m * t + it * tt + tt]
                        stt(V, xsl, PB[m][:, :tt], HINVD, xsl, OP.mult, OP.add)

        def attn():
            LGA = QXF[:, 0:512]
            PEX = QXF[:, 512:1024]
            WEX = QXF[:, 1024:1536]
            TM0 = QXF[:, 1536:2048]
            SIT = QTOK.ap().bitcast(F32)
            AT = AMH
            XG = BUFH[:, 0:G * 512]
            VC = BUFH[:, G * 512:2 * G * 512]
            U = BUFA

            GPC.wait(V)
            GPC.wait(S)
            GPC.emit(lambda e: e.iota(RRB[:, :], [[1, 512]],
                                      channel_multiplier=0,
                                      allow_small_or_imprecise_dtypes=True),
                     inc=1)
            S.wait(GPC)

            cp(V, SUM[:, 32:48], ZRO[:, 0:16])
            for c in range(nch):
                ct0 = c * ch
                for lt in range(ch // 128):
                    tok0 = ct0 + lt * 128
                    transpose_fm_to_tok(tok0, SP4)
                    V.wait(S)
                    scale, bias = ln_tile(SP4[:, :], lt % 4)
                    S.wait(V)
                    act(S, SQS, SP4[:, :], AF.Identity, bias=bias, scale=scale)
                    PE.wait(S)
                    for g in range(G):
                        sl = g % 4
                        if sl == 0:
                            PE.wait(S)
                            PE.wait(V)
                        tr(PTR[:, sl * 128:sl * 128 + 128],
                           SQS[:, g * 128:(g + 1) * 128], IDF[:, :])
                        if sl == 3:
                            q0 = (g // 4) * 512
                            S.wait(PE)
                            act(S, SP4[:, q0:q0 + 512], PTR[:, :512],
                                AF.Identity)
                            V.wait(S)
                            for gg in range(g - 3, g + 1):
                                pseg = SP4[:, q0 + (gg % 4) * 128:
                                            q0 + (gg % 4) * 128 + 128]
                                act(S, XG[:, gg * ch + lt * 128:
                                          gg * ch + lt * 128 + 128],
                                    pseg, AF.Gelu_apprx_tanh)
                                xsl = X[:, gg * t + tok0:gg * t + tok0 + 128]
                                tt_(V, xsl, xsl, pseg, OP.add)
                for wname, bcol, isU in (("Wf1", 2, True), ("Wbx", 4, False)):
                    wf_pe = []
                    for k in range(G):
                        wq_q = SY if k % 2 == 0 else GP
                        if k < 2:
                            wq_q.wait(PE)
                        else:
                            wq_q.wait(PE, wf_pe[k - 2])
                        dma(wq_q, WS[:, (k % 2) * 1024:(k % 2) * 1024 + 1024],
                            W[wname].ap()[k * 128:(k + 1) * 128, :])
                        PE.wait(wq_q)
                        if k == 0:
                            PE.wait(S)
                            PE.wait(V)
                        for m in range(G):
                            mm(PB[m][:, :ch],
                               WS[:, (k % 2) * 1024 + m * 128:
                                  (k % 2) * 1024 + m * 128 + 128],
                               XG[:, k * ch:(k + 1) * ch], k == 0, k == G - 1,
                               inc=(1 if m == G - 1 else 0))
                        wf_pe.append(PE.count)
                    S.wait(PE)
                    for m in range(G):
                        if isU:
                            dst = U[:, m * t + ct0:m * t + ct0 + ch]
                        else:
                            dst = VC[:, m * ch:(m + 1) * ch]
                        act(S, dst, PB[m][:, :ch], AF.Identity,
                            bias=CONS[:, bcol * 8 + m:bcol * 8 + m + 1])
                for wname, bcol in (("Wgx", 5), ("Wga", 6)):
                    wg_pe = []
                    for k in range(G):
                        wq_q = SY if k % 2 == 0 else GP
                        if k < 2:
                            wq_q.wait(PE)
                        else:
                            wq_q.wait(PE, wg_pe[k - 2])
                        dma(wq_q, WS[:, (k % 2) * 1024:(k % 2) * 1024 + 1024],
                            W[wname].ap()[k * 128:(k + 1) * 128, :])
                        PE.wait(wq_q)
                        if k == 0:
                            PE.wait(S)
                            PE.wait(V)
                        for gg in range(G):
                            mm(PB[gg][:, :ch],
                               WS[:, (k % 2) * 1024 + gg * 128:
                                  (k % 2) * 1024 + gg * 128 + 128],
                               VC[:, k * ch:(k + 1) * ch], k == 0, k == G - 1,
                               inc=(1 if gg == G - 1 else 0))
                        wg_pe.append(PE.count)
                    S.wait(PE)
                    if wname == "Wgx":
                        for gg in range(G):
                            act(S, XG[:, gg * ch:gg * ch + ch], PB[gg][:, :ch],
                                AF.Sigmoid,
                                bias=CONS[:, bcol * 8 + gg:bcol * 8 + gg + 1])
                for g in range(G):
                    act(S, SP4[:, 512:512 + ch], PB[g][:, :ch], AF.Sigmoid,
                        bias=CONS[:, 6 * 8 + g:6 * 8 + g + 1])
                    V.wait(S)
                    tsc(V, LGA[:, :ch], SP4[:, 512:512 + ch], M8SP[:, g:g + 1])
                    S.wait(V)
                    act(S, AT[:, :ch], LGA[:, :ch], AF.Exp)
                    act(S, WEX[:, :ch], LGA[:, :ch], AF.Exp, scale=2.0)
                    GP.wait(S)
                    dma(GP, a_s.ap()[g, :, ct0:ct0 + ch], AT[:, :ch])
                    V.wait(S)
                    tsc(V, WEX[:, :ch], WEX[:, :ch], -1.0, 1.0, OP.mult, OP.add)
                    tsc(V, WEX[:, :ch], WEX[:, :ch], 0.0, None, OP.max)
                    S.wait(V)
                    act(S, TM0[:, :ch], WEX[:, :ch], AF.Sqrt)
                    V.wait(S)
                    tt_(V, SIT[:, :ch], XG[:, g * ch:g * ch + ch],
                        VC[:, g * ch:g * ch + ch], OP.mult)
                    tt_(V, SIT[:, :ch], SIT[:, :ch], TM0[:, :ch], OP.mult)
                    GP.wait(V)
                    dma(GP, si_s.ap()[g, :, ct0:ct0 + ch], SIT[:, :ch])
                    scan(V, PEX[:, :ch], ONECOL.broadcast_to([128, ch]),
                         LGA[:, :ch], 0.0)
                    tt_(V, PEX[:, :ch], PEX[:, :ch], LGA[:, :ch], OP.subtract)
                    S.wait(V)
                    act(S, WEX[:, :ch], PEX[:, :ch], AF.Exp,
                        bias=SUM[:, 32 + g:33 + g])
                    V.wait(S)
                    tt_(V, TM0[:, :ch], WEX[:, :ch], SIT[:, :ch], OP.mult)
                    red(V, STATS[:, 0:1], TM0[:, :ch], OP.add)
                    tt_(V, SUM[:, 40 + g:41 + g], SUM[:, 40 + g:41 + g],
                        STATS[:, 0:1], OP.add)
                    red(V, STATS[:, 1:2], LGA[:, :ch], OP.add)
                    tt_(V, SUM[:, 32 + g:33 + g], SUM[:, 32 + g:33 + g],
                        STATS[:, 1:2], OP.add)
            for g in range(G):
                for h2 in range(nh):
                    tsc(V, STATS[:, 2:3], LNAF[:, g:g + 1],
                        float(t - 1 - hw_ * h2))
                    tsc(V, STATS[:, 3:4], LNAF[:, g:g + 1], -1.0)
                    S.wait(V)
                    act(S, WEX[:, :hw_], RRB[:, :hw_], AF.Exp,
                        bias=STATS[:, 2:3], scale=STATS[:, 3:4])
                    V.wait(S)
                    tt_(V, TM0[:, :hw_],
                        U[:, g * t + h2 * hw_:g * t + h2 * hw_ + hw_],
                        WEX[:, :hw_], OP.mult)
                    red(V, STATS[:, 4 + h2:5 + h2], TM0[:, :hw_], OP.add)
                for h2 in range(1, nh):
                    tt_(V, STATS[:, 4:5], STATS[:, 4:5],
                        STATS[:, 4 + h2:5 + h2], OP.add)
                cp(V, SUM[:, 48 + g:49 + g], STATS[:, 4:5])
            S.wait(V)
            act(S, SUM[:, 56:64], SUM[:, 32:40], AF.Exp)
            V.wait(S)
            cp(V, EXC[:, 0:8], SUM[:, 48:56])
            cp(V, EXC[:, 8:16], SUM[:, 40:48])
            cp(V, EXC[:, 16:24], SUM[:, 56:64])
            GP.wait(V)
            dma(GP, cc_i.ap(), EXC[:, 0:24])
            GP.ops.append(lambda e, s=GP.sem, v=GP.count: e.wait_ge(s, v))

            def _cc(e):
                e.collective_compute(
                    "AllGather", OP.bypass,
                    replica_groups=[[0, 1, 2, 3], [4, 5, 6, 7]],
                    ins=[cc_i.ap().opt()], outs=[cc_o.ap().opt()]
                ).then_inc(CCSEM, 1)
            GP.ops.append(_cc)
            GP.ops.append(lambda e: e.wait_ge(CCSEM, 2))
            dma(GP, EXC[:, 0:96].rearrange("p (q c) -> p q c", q=4),
                cc_o.ap().rearrange("q p c -> p q c"))
            V.wait(GP)
            cp(V, STATS[:, 0:8], ZRO[:, 0:8])
            cp(V, STATS[:, 8:16], ZRO[:, 0:8])
            cp(V, STATS[:, 16:24], ZRO[:, 0:8])
            for qq in range(4):
                stt(V, STATS[:, 8:16], STATS[:, 0:8], SCL[:, 2 + qq:3 + qq],
                    STATS[:, 8:16], OP.mult, OP.add)
                tt_(V, STATS[:, 0:8], STATS[:, 0:8], ACORE, OP.mult)
                tt_(V, STATS[:, 0:8], STATS[:, 0:8],
                    EXC[:, qq * 24:qq * 24 + 8], OP.add)
            cp(V, STATS[:, 0:8], ZRO[:, 0:8])
            for qq in range(3, -1, -1):
                stt(V, STATS[:, 16:24], STATS[:, 0:8], SCL[:, 2 + qq:3 + qq],
                    STATS[:, 16:24], OP.mult, OP.add)
                tt_(V, STATS[:, 0:8], STATS[:, 0:8],
                    EXC[:, qq * 24 + 16:qq * 24 + 24], OP.mult)
                tt_(V, STATS[:, 0:8], STATS[:, 0:8],
                    EXC[:, qq * 24 + 8:qq * 24 + 16], OP.add)
            for g in range(G):
                scan(V, U[:, g * t:(g + 1) * t],
                     AFWD[:, g:g + 1].broadcast_to([128, t]),
                     U[:, g * t:(g + 1) * t], STATS[:, 8 + g:9 + g])
            for it in range(ntt):
                w2_pe = []
                for k in range(G):
                    wq_q = SY if k % 2 == 0 else GP
                    if k < 2:
                        wq_q.wait(PE)
                    else:
                        wq_q.wait(PE, w2_pe[k - 2])
                    dma(wq_q, WS[:, (k % 2) * 1024:(k % 2) * 1024 + 1024],
                        W["Wf2"].ap()[k * 128:(k + 1) * 128, :])
                    PE.wait(wq_q)
                    if k == 0:
                        PE.wait(V)
                    for m in range(G):
                        mm(PB[m][:, :tt],
                           WS[:, (k % 2) * 1024 + m * 128:
                              (k % 2) * 1024 + m * 128 + 128],
                           U[:, k * t + it * tt:k * t + it * tt + tt],
                           k == 0, k == G - 1,
                           inc=(1 if m == G - 1 else 0))
                    w2_pe.append(PE.count)
                V.wait(PE)
                for m in range(G):
                    xsl = X[:, m * t + it * tt:m * t + it * tt + tt]
                    stt(V, xsl, PB[m][:, :tt], CONS[:, 3 * 8 + m:3 * 8 + m + 1],
                        xsl, OP.add, OP.add)
            sc_v = []
            SY.wait(GP)
            for g in range(G):
                b0 = (g % 2) * 2 * t
                if g < 2:
                    SY.wait(V)
                else:
                    SY.wait(V, sc_v[g - 2])
                dma(SY, BUFH[:, b0:b0 + t], si_s.ap()[g, :, :])
                dma(SY, BUFH[:, b0 + t:b0 + 2 * t], a_s.ap()[g, :, :])
                V.wait(SY)
                scan(V, U[:, g * t:(g + 1) * t],
                     BUFH[:, b0 + 2 * t - 1:b0 + t - 1:-1],
                     (BUFH[:, t - 1::-1] if b0 == 0 else
                      BUFH[:, b0 + t - 1:b0 - 1:-1]),
                     STATS[:, 16 + g:17 + g])
                sc_v.append(V.count)
            for it in range(ntt):
                wb_pe = []
                for k in range(G):
                    wq_q = SY if k % 2 == 0 else GP
                    if k < 2:
                        wq_q.wait(PE)
                    else:
                        wq_q.wait(PE, wb_pe[k - 2])
                    dma(wq_q, WS[:, (k % 2) * 1024:(k % 2) * 1024 + 1024],
                        W["Wb2"].ap()[k * 128:(k + 1) * 128, :])
                    PE.wait(wq_q)
                    if k == 0:
                        PE.wait(V)
                    for m in range(G):
                        mm(PB[m][:, :tt],
                           WS[:, (k % 2) * 1024 + m * 128:
                              (k % 2) * 1024 + m * 128 + 128],
                           U[:, k * t + (t - (it + 1) * tt):
                              k * t + (t - it * tt)],
                           k == 0, k == G - 1,
                           inc=(1 if m == G - 1 else 0))
                    wb_pe.append(PE.count)
                V.wait(PE)
                for m in range(G):
                    xsl = X[:, m * t + it * tt:m * t + it * tt + tt]
                    stt(V, xsl, PB[m][:, tt - 1::-1],
                        CONS[:, 7 * 8 + m:7 * 8 + m + 1], xsl, OP.add, OP.add)

        def output():
            for lt in range(t // 128):
                tok0 = lt * 128
                transpose_fm_to_tok(tok0, SP4)
                V.wait(S)
                cp(V, BUFA[:, lt * 1024:(lt + 1) * 1024], SP4[:, :])
            SY.wait(V)
            dma(SY, y_out.ap().rearrange("(n p) c -> p n c", p=128),
                BUFA[:, 0:(t // 128) * 1024].rearrange("p (n c) -> p n c", c=D))
            SY.ops.append(lambda e, s=SY.sem, v=SY.count: e.wait_ge(s, v))

        if debug_out is not None:
            debug_out.update(locals())

        build_x()
        ffn(1)
        attn()
        ffn(3)
        output()

        with nc.Block() as block:
            block.sync(lambda e: [op(e) for op in SY.ops])
            block.gpsimd(lambda e: [op(e) for op in GP.ops])
            block.vector(lambda e: [op(e) for op in V.ops])
            block.scalar(lambda e: [op(e) for op in S.ops])
            block.tensor(lambda e: [op(e) for op in PE.ops])

    return nc


_PROG_CACHE = {}


def _get_prog(t):
    if t not in _PROG_CACHE:
        _PROG_CACHE[t] = build_program(t)
    return _PROG_CACHE[t]


def make_inmaps(x_full, wd, t=2048, ncore=8):
    vec_order = ("a_fwd", "a_param", "bf1", "bf2", "bbx", "bgx", "bga", "bb2")
    vecs8 = np.concatenate([np.asarray(wd[k], np.float32).reshape(8, 128)
                            for k in vec_order], axis=0)
    scal2 = np.stack([np.asarray(wd["s_in"], np.float32).reshape(1),
                      np.asarray(wd["s_in_down"], np.float32).reshape(1)])
    base = {
        "Wf1": np.asarray(wd["Wf1"], np.float32),
        "Wf2": np.asarray(wd["Wf2"], np.float32),
        "Wbx": np.asarray(wd["Wbx"], np.float32),
        "Wgx": np.asarray(wd["Wgx"], np.float32),
        "Wga": np.asarray(wd["Wga"], np.float32),
        "Wb2": np.asarray(wd["Wb2"], np.float32),
        "gate_w": np.asarray(wd["gate_w"], np.float32),
        "up_w": np.asarray(wd["up_w"], np.float32),
        "down_w": np.asarray(wd["down_w"], np.float32),
        "vecs8": vecs8, "scal2": scal2,
    }
    gw = np.asarray(wd["gate_w"], np.float32)
    uw = np.asarray(wd["up_w"], np.float32)
    dw = np.asarray(wd["down_w"], np.float32)
    p128 = 16 * np.arange(128)

    def tile_rows(w, it):
        return w[2048 * (it // 16) + (it % 16) + p128]

    nq = ncore // x_full.shape[0]
    in_maps = []
    for c in range(ncore):
        b, q = divmod(c, nq)
        oh = np.zeros((128, 4), np.float32)
        oh[:, q % 4] = 1.0
        m = dict(base)
        m["x"] = np.ascontiguousarray(x_full[b, q * t:(q + 1) * t], dtype=np.float32)
        m["onehot"] = oh
        m["gate_sh"] = np.stack([tile_rows(gw, 4 * c + j) for j in range(4)])
        m["up_sh"] = np.stack([tile_rows(uw, 4 * c + j) for j in range(4)])
        m["down_sh"] = np.ascontiguousarray(dw[c * 128:(c + 1) * 128, :])
        in_maps.append(m)
    return in_maps


def kernel(L0_attn_x, L0_attn_a_fwd, L0_attn_Wf1, L0_attn_bf1, L0_attn_Wf2,
           L0_attn_bf2, L0_attn_Wbx, L0_attn_bbx, L0_attn_Wgx, L0_attn_bgx,
           L0_attn_Wga, L0_attn_bga, L0_attn_a_param, L0_attn_Wb2, L0_attn_bb2,
           L0_ffn_gate_w, L0_ffn_up_w, L0_ffn_down_w, L0_ffn_s_in,
           L0_ffn_s_in_down):
    x = np.asarray(L0_attn_x, np.float32)
    wd = {
        "a_fwd": L0_attn_a_fwd, "a_param": L0_attn_a_param,
        "bf1": L0_attn_bf1, "bf2": L0_attn_bf2, "bbx": L0_attn_bbx,
        "bgx": L0_attn_bgx, "bga": L0_attn_bga, "bb2": L0_attn_bb2,
        "Wf1": L0_attn_Wf1, "Wf2": L0_attn_Wf2, "Wbx": L0_attn_Wbx,
        "Wgx": L0_attn_Wgx, "Wga": L0_attn_Wga, "Wb2": L0_attn_Wb2,
        "gate_w": L0_ffn_gate_w, "up_w": L0_ffn_up_w, "down_w": L0_ffn_down_w,
        "s_in": L0_ffn_s_in, "s_in_down": L0_ffn_s_in_down,
    }
    t = 2048
    nc = _get_prog(t)
    in_maps = make_inmaps(x, wd, t=t, ncore=8)
    res = run_bass_kernel_spmd(nc, in_maps, core_ids=list(range(8)))
    y = np.empty((2, 8192, 1024), np.float32)
    for c in range(8):
        b, q = divmod(c, 4)
        y[b, q * t:(q + 1) * t] = np.asarray(res.results[c]["y"], np.float32)
    return y


# version 08 (final)
# speedup vs baseline: 1.6675x; 1.0029x over previous
import numpy as np
import concourse.bass as bass
import concourse.mybir as mybir
from concourse.bass_utils import run_bass_kernel_spmd

F32 = mybir.dt.float32
BF16 = mybir.dt.bfloat16
FP8 = mybir.dt.float8e4
AF = mybir.ActivationFunctionType
OP = mybir.AluOpType
AX = mybir.AxisListType

D = 1024
G = 8
I_H = 4096
MI = 32
C8 = 2.0 ** -8
EPS = 1e-5


class Eng:
    def __init__(self, sem):
        self.ops = []
        self.sem = sem
        self.count = 0

    def emit(self, fn, inc=0):
        sem, n = self.sem, inc

        def op(e):
            ins = fn(e)
            if n:
                ins.then_inc(sem, n)
        self.ops.append(op)
        if inc:
            self.count += inc
        return self.count

    def wait(self, other, value=None):
        sem = other.sem
        v = other.count if value is None else value
        if v > 0:
            self.ops.append(lambda e, s=sem, vv=v: e.wait_ge(s, vv))


def build_program(t=2048, debug_out=None):
    tt = min(512, t)
    ntt = t // tt
    nlt = tt // 128
    ch = min(512, t)
    nch = t // ch
    hw_ = min(512, t)
    nh = t // hw_

    nc = bass.Bass("TRN2", target_bir_lowering=False, detect_race_conditions=False)

    x_in = nc.dram_tensor("x", [t, D], F32, kind="ExternalInput")
    y_out = nc.dram_tensor("y", [t, D], F32, kind="ExternalOutput")
    W = {}
    for nm in ("Wf1", "Wf2", "Wbx", "Wgx", "Wga", "Wb2"):
        W[nm] = nc.dram_tensor(nm, [D, D], F32, kind="ExternalInput")
    gate_w = nc.dram_tensor("gate_w", [I_H, D], F32, kind="ExternalInput")
    up_w = nc.dram_tensor("up_w", [I_H, D], F32, kind="ExternalInput")
    down_w = nc.dram_tensor("down_w", [D, I_H], F32, kind="ExternalInput")
    vecs8 = nc.dram_tensor("vecs8", [64, 128], F32, kind="ExternalInput")
    scal2 = nc.dram_tensor("scal2", [2, 1], F32, kind="ExternalInput")
    onehot = nc.dram_tensor("onehot", [128, 4], F32, kind="ExternalInput")

    qg_s = nc.dram_tensor("qg_s", [MI, G, 128, 128], BF16)
    qu_s = nc.dram_tensor("qu_s", [MI, G, 128, 128], BF16)
    qd_s = nc.dram_tensor("qd_s", [MI, G, 128, 128], BF16)
    si_s = nc.dram_tensor("si_s", [G, 128, t], F32)
    a_s = nc.dram_tensor("a_s", [G, 128, t], F32)
    cc_i = nc.dram_tensor("cc_i", [128, 24], F32)
    gate_sh = nc.dram_tensor("gate_sh", [4, 128, D], F32, kind="ExternalInput")
    up_sh = nc.dram_tensor("up_sh", [4, 128, D], F32, kind="ExternalInput")
    down_sh = nc.dram_tensor("down_sh", [128, I_H], F32, kind="ExternalInput")
    wsh_i = nc.dram_tensor("wsh_i", [96, 128, 128], BF16)
    wsh_o = nc.dram_tensor("wsh_o", [8, 96, 128, 128], BF16,
                           addr_space="Shared")
    cc_o = nc.dram_tensor("cc_o", [4, 128, 24], F32)

    from contextlib import ExitStack
    stack = ExitStack()
    with stack:
        sbuf = lambda n_, s, d=F32: stack.enter_context(nc.sbuf_tensor(n_, s, d))

        BUFA = sbuf("BUFA", [128, t * G])
        X = sbuf("X", [128, t * G])
        BUFH = sbuf("BUFH", [128, 8192])
        WS = sbuf("WS", [128, 2048])
        WQ = sbuf("WQ", [128, 4096], BF16)
        QXK = sbuf("QXK", [128, G * 512], BF16)
        QHD = sbuf("QHD", [128, 1536], BF16)
        QTOK = sbuf("QTOK", [128, 1024], BF16)
        VQ8 = sbuf("VQ8", [128, 1024], FP8)
        AMB = sbuf("AMB", [128, 256])
        SCH = sbuf("SCH", [128, 512])
        AMH = sbuf("AMH", [128, 512])
        HSQ = sbuf("HSQ", [128, 512])
        V8T = sbuf("V8T", [128, 512], FP8)
        RRB = sbuf("RRB", [128, 512])
        S8B = sbuf("S8B", [128, 512])
        SP4 = sbuf("SP4", [128, 1024])
        IDF = sbuf("IDF", [128, 128])
        IDB = sbuf("IDB", [128, 128], BF16)
        ZRO = sbuf("ZRO", [128, 16])
        GAPB = sbuf("GAPB", [128, 8])
        STATS = sbuf("STATS", [128, 64])
        CONS = sbuf("CONS", [128, 64])
        SUM = sbuf("SUM", [128, 64])
        SCL = sbuf("SCL", [128, 32])
        EXC = sbuf("EXC", [128, 96])
        GWS = sbuf("GWS", [128, 16])

        PB = [stack.enter_context(nc.psum_tensor(f"PB{i}", [128, 512], F32))
              for i in range(8)]
        PTR = PB[6]
        PTB = PB[7].ap().bitcast(BF16)

        sems = {k: stack.enter_context(nc.semaphore("s_" + k))
                for k in ("sy", "gp", "gpc", "v", "s", "pe", "cc")}
        SY, GP, V, S, PE = (Eng(sems[k]) for k in ("sy", "gp", "v", "s", "pe"))
        GPC = Eng(sems["gpc"])
        GPC.ops = GP.ops
        CCSEM = sems["cc"]

        FQ0 = BUFH[:, 4096:5120]
        FQ1 = BUFH[:, 5120:6144]
        FQ2 = BUFH[:, 6144:7168]
        SQS = BUFH[:, 7168:8192]
        QXF = QXK.ap().bitcast(F32)
        HQ0, HQ1, HQ2 = SP4[:, 0:512], SP4[:, 512:1024], HSQ

        def dma(eng, out, in_):
            return eng.emit(lambda e, o=out, i=in_: e.dma_start(out=o, in_=i),
                            inc=16)

        _vw = []

        def _rngs(aps):
            out = []
            for ap in aps:
                if ap is None or isinstance(ap, (int, float)):
                    continue
                try:
                    nm = ap.tensor.name
                    off = int(ap.offset)
                    sh = list(ap.shape)
                    try:
                        stv = list(ap.stride)
                    except Exception:
                        stv = None
                    if stv and len(stv) == len(sh) and len(sh) > 1:
                        span = 1 + sum((int(d) - 1) * abs(int(s))
                                       for d, s in zip(sh[1:], stv[1:]))
                    else:
                        span = 1
                        for d in sh[1:]:
                            span *= int(d)
                    out.append((nm, off, off + max(span, 1)))
                except Exception:
                    out.append(("?", 0, 1 << 62))
            return out

        def _fsz(ap):
            try:
                return int(ap.free_size)
            except Exception:
                pass
            try:
                n = 1
                for d in list(ap.shape)[1:]:
                    n *= int(d)
                return n
            except Exception:
                return 1

        def _vsmall(aps):
            return all(_fsz(ap) <= 128 for ap in aps)

        def _vraw(outs, ins):
            rd = _rngs(ins)
            need = 0
            for age, (small, wr) in enumerate(_vw):
                if not small:
                    continue
                for nm, lo, hi in wr:
                    for nm2, lo2, hi2 in rd:
                        if ((nm == nm2 or nm == "?" or nm2 == "?")
                                and lo < hi2 and lo2 < hi):
                            need = max(need, 2 - age)
            for _ in range(need):
                V.emit(lambda e: e.tensor_copy(GAPB[:, 0:8], ZRO[:, 0:8]),
                       inc=0)
                _vw.insert(0, (False, []))
            _vw.insert(0, (_vsmall(outs), _rngs(outs)))
            del _vw[2:]

        def tsc(eng, out, in0, s1, s2=None, op0=OP.mult, op1=None):
            if eng is V:
                _vraw([out], [in0, s1, s2])
            if op1 is None:
                return eng.emit(lambda e, o=out, i=in0, a=s1, p0=op0:
                                e.tensor_scalar(o, i, a, None, p0), inc=1)
            return eng.emit(lambda e, o=out, i=in0, a=s1, b=s2, p0=op0, p1=op1:
                            e.tensor_scalar(o, i, a, b, p0, p1), inc=1)

        def tt_(eng, out, a, b, op):
            if eng is V:
                _vraw([out], [a, b])
            return eng.emit(lambda e, o=out, x=a, y=b, p=op:
                            e.tensor_tensor(o, x, y, p), inc=1)

        def stt(eng, out, in0, sc, in1, op0, op1):
            if eng is V:
                _vraw([out], [in0, sc, in1])
            return eng.emit(lambda e, o=out, i0=in0, s=sc, i1=in1, p0=op0, p1=op1:
                            e.scalar_tensor_tensor(o, i0, s, i1, p0, p1), inc=1)

        def act(eng, out, in_, fn, bias=0.0, scale=1.0, accum=None):
            return eng.emit(lambda e, o=out, i=in_, f=fn, b=bias, s=scale, a=accum:
                            e.activation(o, i, f, bias=b, scale=s, accum_out=a),
                            inc=1)

        def cp(eng, out, in_):
            if eng is V:
                _vraw([out], [in_])
            return eng.emit(lambda e, o=out, i=in_: e.tensor_copy(o, i), inc=1)

        def recip(eng, out, in_):
            if eng is V:
                _vraw([out], [in_])
                _vw[0] = (True, _vw[0][1])
            return eng.emit(lambda e, o=out, i=in_: e.reciprocal(o, i), inc=1)

        def red(eng, out, in_, op, absval=False):
            if eng is V:
                _vraw([out], [in_])
            return eng.emit(lambda e, o=out, i=in_, p=op, a=absval:
                            e.tensor_reduce(o, i, AX.X, p, apply_absolute_value=a),
                            inc=1)

        def mm(out, lhsT, rhs, start, stop, inc=0):
            return PE.emit(lambda e, o=out, l=lhsT, r=rhs, a=start, b=stop:
                           e.matmul(o, l, r, start=a, stop=b), inc=inc)

        def tr(out, in_, ident):
            return PE.emit(lambda e, o=out, i=in_, d=ident:
                           e.transpose(o, i, d), inc=1)

        def scan(eng, out, d0, d1, init):
            if eng is V:
                _vraw([out], [d0, d1, init])
            return eng.emit(lambda e, o=out, a=d0, b=d1, i=init:
                            e.tensor_tensor_scan(o, a, b, i, OP.mult, OP.add),
                            inc=1)

        dma(SY, SP4[:64, :128], vecs8.ap())
        dma(SY, SCL[:, 0:1], scal2[0:1, :].broadcast_to([128, 1]))
        dma(SY, SCL[:, 1:2], scal2[1:2, :].broadcast_to([128, 1]))
        dma(SY, SCL[:, 2:6], onehot.ap())
        GPC.emit(lambda e: e.iota(IDF[:, :], [[1, 128]], channel_multiplier=-1,
                                  allow_small_or_imprecise_dtypes=True), inc=1)
        GPC.emit(lambda e: e.memset(ZRO[:, :], 0.0), inc=1)
        GPC.emit(lambda e: e.memset(SP4[:, 512:640], 1.0), inc=1)
        V.wait(GPC)
        tsc(V, SCL[:, 31:32], ZRO[:, 0:1], 1.0, None, OP.add)
        ONECOL = SCL[:, 31:32]
        tsc(V, IDB[:, :], IDF[:, :], 0.0, None, OP.is_equal)
        tsc(V, IDF[:, :], IDF[:, :], 0.0, None, OP.is_equal)
        PE.wait(SY)
        PE.wait(V)
        tr(PTR[:, :64], SP4[:64, :128], IDF[:64, :64])
        S.wait(PE)
        act(S, CONS[:, :], PTR[:128, :64], AF.Identity)
        V.wait(S)
        AFWD = CONS[:, 0:8]
        S.wait(V)
        act(S, SUM[:, 0:8], CONS[:, 8:16], AF.Exp)
        V.wait(S)
        tsc(V, SUM[:, 0:8], SUM[:, 0:8], 1.0, None, OP.add)
        S.wait(V)
        act(S, SUM[:, 0:8], SUM[:, 0:8], AF.Ln)
        act(S, SUM[:, 16:24], CONS[:, 0:8], AF.Ln)
        V.wait(S)
        tsc(V, SUM[:, 8:16], SUM[:, 0:8], -8.0)
        tsc(V, SCL[:, 30:31], ZRO[:, 0:1], EPS, None, OP.add)
        M8SP = SUM[:, 8:16]
        LNAF = SUM[:, 16:24]
        cp(V, SUM[:, 24:32], CONS[:, 0:8])
        for _ in range(int(round(np.log2(t)))):
            tt_(V, SUM[:, 24:32], SUM[:, 24:32], SUM[:, 24:32], OP.mult)
        ACORE = SUM[:, 24:32]

        def gate_rows_ap(src, it):
            b0 = 2048 * (it // 16) + (it % 16)
            return bass.AP(src, b0 * D, [[16 * D, 128], [1, D]])

        def global_absmax(src_ap_fn, ntile, npiece, out_col):
            tt_(V, GWS[:, out_col:out_col + 1], ZRO[:, 0:1], ZRO[:, 0:1], OP.add)
            for it in range(ntile):
                for piece in range(npiece):
                    j = (it * npiece + piece) % 2
                    dst = WS[:, j * 1024:(j + 1) * 1024]
                    SY.wait(V)
                    dma(SY, dst, src_ap_fn(it, piece))
                    V.wait(SY)
                    red(V, GWS[:, out_col + 1:out_col + 2], dst, OP.max,
                        absval=True)
                    tt_(V, GWS[:, out_col:out_col + 1],
                        GWS[:, out_col:out_col + 1],
                        GWS[:, out_col + 1:out_col + 2], OP.max)
            PE.wait(V)
            PE.wait(S)
            tr(PTR[:1, 0:128], GWS[:, out_col:out_col + 1], IDF[:, :])
            S.wait(PE)
            act(S, AMB[:1, 128:256], PTR[:1, 0:128], AF.Identity)
            V.wait(S)
            red(V, GWS[:1, out_col:out_col + 1], AMB[:1, 128:256], OP.max)
            PE.wait(V)
            mm(PTR[:, 128:129], SP4[:1, 512:640], GWS[:1, out_col:out_col + 1],
               True, True, inc=1)
            V.wait(PE)
            cp(V, GWS[:, out_col:out_col + 1], PTR[:, 128:129])

        global_absmax(lambda it, p: gate_rows_ap(gate_w, it), MI, 1, 0)
        global_absmax(lambda it, p: gate_rows_ap(up_w, it), MI, 1, 4)
        global_absmax(lambda it, p: down_w.ap()[it * 128:(it + 1) * 128,
                                                p * 1024:(p + 1) * 1024],
                      G, 4, 8)
        for col in (0, 4, 8):
            recip(V, GWS[:, col + 2:col + 3], GWS[:, col:col + 1])
        cp(V, SP4[:, 0:128], IDF[:, :])
        cp(V, SP4[:, 128:256], IDF[:, :])
        for j, col in enumerate((0, 4, 8)):
            tsc(V, SCL[:, 6 + j:7 + j], GWS[:, col + 2:col + 3], 2688.0)
        tt_(V, SCL[:, 16:17], SCL[:, 0:1], SCL[:, 6:7], OP.mult)
        tt_(V, SCL[:, 17:18], SCL[:, 0:1], SCL[:, 7:8], OP.mult)
        tt_(V, SCL[:, 18:19], SCL[:, 1:2], SCL[:, 8:9], OP.mult)
        for j in range(3):
            recip(V, SCL[:, 19 + j:20 + j], SCL[:, 16 + j:17 + j])
        cp(V, SP4[:, 0:128], IDF[:, :])
        cp(V, SP4[:, 128:256], IDF[:, :])
        cp(V, SCL[:, 9:10], SCL[:, 19:20])
        cp(V, SCL[:, 10:11], SCL[:, 20:21])
        tsc(V, SCL[:, 11:12], SCL[:, 21:22], 0.5)
        tsc(V, SCL[:, 12:13], SCL[:, 1:2], 1.0 / 12.0)
        tsc(V, SCL[:, 13:14], SCL[:, 1:2], 0.5)
        tsc(V, SCL[:, 24:25], SCL[:, 6:7], 1.0 / 12.0)
        tsc(V, SCL[:, 26:27], SCL[:, 6:7], 0.5)
        tsc(V, SCL[:, 25:26], SCL[:, 7:8], 1.0 / 12.0)
        tsc(V, SCL[:, 27:28], SCL[:, 7:8], 0.5)
        tsc(V, SCL[:, 28:29], SCL[:, 8:9], 1.0 / 12.0)
        tsc(V, SCL[:, 29:30], SCL[:, 8:9], 0.5)
        GX = SCL[:, 0:1]
        INVG1, INVG2, HINVD = SCL[:, 9:10], SCL[:, 10:11], SCL[:, 11:12]
        GD12, GD2 = SCL[:, 12:13], SCL[:, 13:14]

        def fp4(src, width, rr, s8, t0, t1, t2, outq, blk16, vq):
            nb = width // 16
            if blk16:
                tt_(V, t0[:, :width].rearrange("p (b k) -> p b k", k=16),
                    src.rearrange("p (b k) -> p b k", k=16),
                    rr.unsqueeze(2).broadcast_to([128, nb, 16]), OP.mult)
            else:
                tt_(V, t0[:, :width], src, rr, OP.mult)
            tsc(V, t0[:, :width], t0[:, :width], 6.5, -6.5, OP.min, OP.max)
            tsc(V, t1[:, :width], t0[:, :width], 2.0, 0.0, OP.subtract, OP.max)
            tsc(V, t2[:, :width], t0[:, :width], -2.0, 0.0, OP.subtract, OP.min)
            tt_(V, t1[:, :width], t1[:, :width], t2[:, :width], OP.add)
            tsc(V, t2[:, :width], t0[:, :width], 4.0, 0.0, OP.subtract, OP.max)
            stt(V, t2[:, :width], t2[:, :width], -0.25, t0[:, :width],
                OP.mult, OP.add)
            tsc(V, t0[:, :width], t0[:, :width], -4.0, 0.0, OP.subtract, OP.min)
            stt(V, t0[:, :width], t0[:, :width], -0.25, t2[:, :width],
                OP.mult, OP.add)
            stt(V, t0[:, :width], t1[:, :width], -0.5, t0[:, :width],
                OP.mult, OP.add)
            tsc(V, vq[:, :width], t0[:, :width], C8)
            tsc(V, t1[:, :width], vq[:, :width], 2 * C8, 0.0, OP.subtract, OP.max)
            tsc(V, t2[:, :width], vq[:, :width], -2 * C8, 0.0, OP.subtract, OP.min)
            tt_(V, t1[:, :width], t1[:, :width], t2[:, :width], OP.add)
            tsc(V, t2[:, :width], vq[:, :width], 3 * C8, 0.0, OP.subtract, OP.max)
            stt(V, t2[:, :width], t2[:, :width], 2.0, t1[:, :width],
                OP.mult, OP.add)
            tsc(V, t1[:, :width], vq[:, :width], -3 * C8, 0.0, OP.subtract,
                OP.min)
            stt(V, t1[:, :width], t1[:, :width], 2.0, t2[:, :width],
                OP.mult, OP.add)
            tt_(V, t1[:, :width], vq[:, :width], t1[:, :width], OP.add)
            if blk16:
                tt_(V, outq.rearrange("p (b k) -> p b k", k=16),
                    t1[:, :width].rearrange("p (b k) -> p b k", k=16),
                    s8.unsqueeze(2).broadcast_to([128, nb, 16]), OP.mult)
            else:
                tt_(V, outq, t1[:, :width], s8, OP.mult)

        def sc_stage(amax_ap, n, mul_ap, half_ap, rr_out, s8_out, v8t):
            tt_(V, rr_out[:, :n], amax_ap, mul_ap.broadcast_to([128, n]), OP.mult)
            tsc(V, rr_out[:, :n], rr_out[:, :n], 1.0, 224.0, OP.mult, OP.min)
            cp(V, v8t[:, :n], rr_out[:, :n])
            recip(V, rr_out[:, :n], v8t[:, :n])
            tsc(V, s8_out[:, :n], v8t[:, :n], 512.0)
            cp(V, STATS[:, 32:64], SUM[:, 0:32])
            tsc(V, rr_out[:, :n], rr_out[:, :n], 1e30, None, OP.min)
            tt_(V, rr_out[:, :n], rr_out[:, :n],
                half_ap.broadcast_to([128, n]), OP.mult)

        def wprep_gateup(src, base, which):
            mul_ap = SCL[:, 24 + which:25 + which]
            half_ap = SCL[:, 26 + which:27 + which]
            for pair in range(2):
                its = (2 * pair, 2 * pair + 1)
                for j, it in enumerate(its):
                    SY.wait(V)
                    SY.wait(PE)
                    dma(SY, WS[:, j * 1024:(j + 1) * 1024], src.ap()[it])
                V.wait(SY)
                for j in range(2):
                    red(V, AMB[:, j * 64:(j + 1) * 64],
                        WS[:, j * 1024:(j + 1) * 1024]
                        .rearrange("p (b k) -> p b k", k=16), OP.max, absval=True)
                sc_stage(AMB[:, :128], 128, mul_ap, half_ap, RRB, S8B, V8T)
                for j, it in enumerate(its):
                    fp4(WS[:, j * 1024:(j + 1) * 1024], 1024,
                        RRB[:, j * 64:(j + 1) * 64], S8B[:, j * 64:(j + 1) * 64],
                        FQ0, FQ1, FQ2, QTOK[:, :1024], True, VQ8)
                    for g in range(G):
                        slot = g % 8
                        PE.wait(V)
                        PE.wait(S)
                        tr(PTB[:, slot * 128:slot * 128 + 128],
                           QTOK[:, g * 128:(g + 1) * 128], IDB[:, :])
                        S.wait(PE)
                        act(S, WQ[:, slot * 128:slot * 128 + 128],
                            PTB[:, slot * 128:slot * 128 + 128], AF.Identity)
                        GP.wait(S)
                        dma(GP, wsh_i.ap()[base + it * 8 + g, :, :],
                            WQ[:, slot * 128:slot * 128 + 128])

        wprep_gateup(gate_sh, 0, 0)
        wprep_gateup(up_sh, 32, 1)

        def wprep_down():
            for it in range(1):
                for half in range(2):
                    SY.wait(V)
                    SY.wait(PE)
                    dma(SY, WS[:, :2048],
                        down_sh.ap()[:, half * 2048:(half + 1) * 2048])
                    V.wait(SY)
                    red(V, AMB[:, 0:128],
                        WS[:, :2048].rearrange("p (b k) -> p b k", k=16),
                        OP.max, absval=True)
                    sc_stage(AMB[:, :128], 128, SCL[:, 28:29], SCL[:, 29:30],
                             RRB, S8B, V8T)
                    for j in range(2):
                        fp4(WS[:, j * 1024:(j + 1) * 1024], 1024,
                            RRB[:, j * 64:(j + 1) * 64],
                            S8B[:, j * 64:(j + 1) * 64],
                            FQ0, FQ1, FQ2, QTOK[:, :1024], True, VQ8)
                        cp(V, WQ[:, (2 + j) * 1024:(3 + j) * 1024],
                           QTOK[:, :1024])
                    for kk in range(16):
                        tau = half * 16 + kk
                        slot = kk % 8
                        PE.wait(V)
                        PE.wait(S)
                        tr(PTB[:, slot * 128:slot * 128 + 128],
                           WQ[:, 2048 + kk:4096:16], IDB[:, :])
                        S.wait(PE)
                        act(S, WQ[:, slot * 128:slot * 128 + 128],
                            PTB[:, slot * 128:slot * 128 + 128], AF.Identity)
                        GP.wait(S)
                        dma(GP, wsh_i.ap()[64 + tau, :, :],
                            WQ[:, slot * 128:slot * 128 + 128])

        wprep_down()
        GP.ops.append(lambda e, s=GP.sem, v=GP.count: e.wait_ge(s, v))

        def _wcc(e):
            e.collective_compute(
                "AllGather", OP.bypass,
                replica_groups=[[0, 1, 2, 3, 4, 5, 6, 7]],
                ins=[wsh_i.ap().opt()], outs=[wsh_o.ap().opt()]
            ).then_inc(CCSEM, 1)
        GP.ops.append(_wcc)

        def ln_tile(xtok, c, gmul=None):
            red(V, STATS[:, c:c + 1], xtok, OP.add)
            S.wait(V)
            act(S, SQS, xtok, AF.Square, accum=STATS[:, 8 + c:9 + c])
            V.wait(S)
            tsc(V, STATS[:, 16 + c:17 + c], STATS[:, c:c + 1], 1.0 / D)
            tsc(V, STATS[:, 20 + c:21 + c], STATS[:, 8 + c:9 + c], 1.0 / D)
            tt_(V, STATS[:, 24 + c:25 + c], STATS[:, 16 + c:17 + c],
                STATS[:, 16 + c:17 + c], OP.mult)
            tt_(V, STATS[:, 20 + c:21 + c], STATS[:, 20 + c:21 + c],
                STATS[:, 24 + c:25 + c], OP.subtract)
            S.wait(V)
            act(S, STATS[:, 24 + c:25 + c], STATS[:, 20 + c:21 + c], AF.Sqrt,
                bias=SCL[:, 30:31])
            V.wait(S)
            recip(V, STATS[:, 28 + c:29 + c], STATS[:, 24 + c:25 + c])
            cp(V, STATS[:, 32:48], SUM[:, 0:16])
            cp(V, STATS[:, 48:64], SUM[:, 16:32])
            if gmul is not None:
                tt_(V, STATS[:, 28 + c:29 + c], STATS[:, 28 + c:29 + c],
                    gmul, OP.mult)
            tt_(V, STATS[:, 24 + c:25 + c], STATS[:, 16 + c:17 + c],
                STATS[:, 28 + c:29 + c], OP.mult)
            tsc(V, STATS[:, 24 + c:25 + c], STATS[:, 24 + c:25 + c], -1.0)
            return STATS[:, 28 + c:29 + c], STATS[:, 24 + c:25 + c]

        def transpose_fm_to_tok(tok0, dst):
            for g in range(G):
                sl = g % 4
                if sl == 0:
                    PE.wait(S)
                    PE.wait(V)
                tr(PTR[:, sl * 128:sl * 128 + 128],
                   X[:, g * t + tok0:g * t + tok0 + 128], IDF[:, :])
                if sl == 3:
                    S.wait(PE)
                    act(S, dst[:, (g - 3) * 128:(g + 1) * 128], PTR[:, :512],
                        AF.Identity)

        def build_x():
            for lt in range(t // 128):
                tok0 = lt * 128
                SY.wait(V)
                SY.wait(PE)
                dma(SY, SP4[:, :], x_in.ap()[tok0:tok0 + 128, :])
                PE.wait(SY)
                for g in range(G):
                    sl = g % 4
                    if sl == 0:
                        PE.wait(S)
                    tr(PTR[:, sl * 128:sl * 128 + 128],
                       SP4[:, g * 128:(g + 1) * 128], IDF[:, :])
                    if sl == 3:
                        S.wait(PE)
                        for gg in range(g - 3, g + 1):
                            act(S, X[:, gg * t + tok0:gg * t + tok0 + 128],
                                PTR[:, (gg % 4) * 128:(gg % 4) * 128 + 128],
                                AF.Identity)

        def ffn(sub):
            for it in range(ntt):
                for lt in range(nlt):
                    tok0 = it * tt + lt * 128
                    if sub == 1:
                        SY.wait(V)
                        SY.wait(S)
                        dma(SY, SP4[:, :], x_in.ap()[tok0:tok0 + 128, :])
                        V.wait(SY)
                        S.wait(SY)
                    else:
                        transpose_fm_to_tok(tok0, SP4)
                        V.wait(S)
                    scale, bias = ln_tile(SP4[:, :], lt, gmul=GX)
                    S.wait(V)
                    act(S, BUFH[:, lt * 1024:(lt + 1) * 1024], SP4[:, :],
                        AF.Identity, bias=bias, scale=scale)
                    V.wait(S)
                    red(V, AMB[:, lt * 64:(lt + 1) * 64],
                        BUFH[:, lt * 1024:(lt + 1) * 1024]
                        .rearrange("p (b k) -> p b k", k=16), OP.max, absval=True)
                nbt = nlt * 64
                tsc(V, SCH[:, :nbt], AMB[:, :nbt], 1.0 / 12.0, 224.0,
                    OP.mult, OP.min)
                cp(V, V8T[:, :nbt], SCH[:, :nbt])
                recip(V, SCH[:, :nbt], V8T[:, :nbt])
                tsc(V, S8B[:, :nbt], V8T[:, :nbt], 512.0)
                cp(V, STATS[:, 32:64], SUM[:, 0:32])
                tsc(V, RRB[:, :nbt], SCH[:, :nbt], 1e30, 0.5, OP.min, OP.mult)
                for lt in range(nlt):
                    fp4(BUFH[:, lt * 1024:(lt + 1) * 1024], 1024,
                        RRB[:, lt * 64:(lt + 1) * 64],
                        S8B[:, lt * 64:(lt + 1) * 64],
                        FQ0, FQ1, FQ2, QTOK[:, :1024], True, VQ8)
                    for g in range(G):
                        slot = g % 8
                        PE.wait(V)
                        PE.wait(S)
                        tr(PTB[:, slot * 128:slot * 128 + 128],
                           QTOK[:, g * 128:(g + 1) * 128], IDB[:, :])
                        S.wait(PE)
                        act(S, QXK[:, g * tt + lt * 128:g * tt + lt * 128 + 128],
                            PTB[:, slot * 128:slot * 128 + 128], AF.Identity)
                for hh in range(2):
                    gu_pe, gu_v, gu_s = [], [], []
                    for m16 in range(16):
                        m = hh * 16 + m16
                        bg, bu = PB[2 * (m % 2)], PB[2 * (m % 2) + 1]
                        wql = WQ[:, (m % 2) * 2048:(m % 2) * 2048 + 1024]
                        wqu = WQ[:, (m % 2) * 2048 + 1024:(m % 2) * 2048 + 2048]
                        if sub == 1 and it == 0 and hh == 0 and m16 == 0:
                            SY.ops.append(lambda e: e.wait_ge(CCSEM, 1))
                            GP.ops.append(lambda e: e.wait_ge(CCSEM, 1))
                        if m16 < 2:
                            SY.wait(PE)
                            GP.wait(PE)
                        else:
                            SY.wait(PE, gu_pe[m16 - 2])
                            GP.wait(PE, gu_pe[m16 - 2])
                        dma(SY, wql.rearrange("p (g c) -> p g c", g=G),
                            wsh_o.ap()[m // 4, (m % 4) * 8:(m % 4) * 8 + 8,
                                       :, :].rearrange("g p c -> p g c"))
                        dma(GP, wqu.rearrange("p (g c) -> p g c", g=G),
                            wsh_o.ap()[m // 4, 32 + (m % 4) * 8:
                                       32 + (m % 4) * 8 + 8,
                                       :, :].rearrange("g p c -> p g c"))
                        PE.wait(SY)
                        PE.wait(GP)
                        if m16 < 2:
                            PE.wait(V)
                            PE.wait(S)
                        else:
                            PE.wait(V, gu_v[m16 - 2])
                            PE.wait(S, gu_s[m16 - 2])
                        for k in range(G):
                            mm(bg[:, :tt], wql[:, k * 128:(k + 1) * 128],
                               QXK[:, k * tt:k * tt + tt], k == 0, k == G - 1,
                               inc=(1 if k == G - 1 else 0))
                        for k in range(G):
                            mm(bu[:, :tt], wqu[:, k * 128:(k + 1) * 128],
                               QXK[:, k * tt:k * tt + tt], k == 0, k == G - 1,
                               inc=(1 if k == G - 1 else 0))
                        gu_pe.append(PE.count)
                        S.wait(PE)
                        sgt = SP4[:, :512] if m % 2 == 0 else SP4[:, 512:1024]
                        act(S, sgt[:, :tt], bg[:, :tt], AF.Silu, scale=INVG1)
                        V.wait(S)
                        hdst = BUFH[:, m16 * 512:m16 * 512 + tt]
                        stt(V, hdst, bu[:, :tt], INVG2, sgt[:, :tt],
                            OP.mult, OP.mult)
                        gu_v.append(V.count)
                        S.wait(V)
                        act(S, HSQ[:, :tt], hdst, AF.Square)
                        gu_s.append(S.count)
                        V.wait(S)
                        if m16 == 0:
                            cp(V, AMH[:, :tt], HSQ[:, :tt])
                        else:
                            tt_(V, AMH[:, :tt], AMH[:, :tt], HSQ[:, :tt], OP.max)
                    S.wait(V)
                    act(S, AMH[:, :tt], AMH[:, :tt], AF.Sqrt)
                    V.wait(S)
                    sc_stage(AMH[:, :tt], tt, GD12, GD2, SCH, S8B, V8T)
                    dn_pe = []
                    for m16 in range(16):
                        k = hh * 16 + m16
                        qslot = QHD[:, (m16 % 3) * 512:(m16 % 3) * 512 + tt]
                        if m16 >= 2:
                            V.wait(PE)
                        fp4(BUFH[:, m16 * 512:m16 * 512 + tt], tt,
                            SCH[:, :tt], S8B[:, :tt],
                            HQ0, HQ1, HQ2, qslot, False, VQ8)
                        wqd = WQ[:, (m16 % 2) * 1024:(m16 % 2) * 1024 + 1024]
                        wq_q = SY if m16 % 2 == 0 else GP
                        if m16 < 2:
                            wq_q.wait(PE)
                        else:
                            wq_q.wait(PE, dn_pe[m16 - 2])
                        dma(wq_q, wqd.rearrange("p (g c) -> p g c", g=G),
                            wsh_o.ap()[:, 64 + k, :, :]
                            .rearrange("g p c -> p g c"))
                        PE.wait(wq_q)
                        PE.wait(V)
                        for m in range(G):
                            mm(PB[m][:, :tt], wqd[:, m * 128:(m + 1) * 128],
                               qslot, m16 == 0, m16 == 15,
                               inc=(1 if m == G - 1 else 0))
                        dn_pe.append(PE.count)
                    V.wait(PE)
                    for m in range(G):
                        xsl = X[:, m * t + it * tt:m * t + it * tt + tt]
                        stt(V, xsl, PB[m][:, :tt], HINVD, xsl, OP.mult, OP.add)

        def attn():
            LGA = QXF[:, 0:512]
            PEX = QXF[:, 512:1024]
            WEX = QXF[:, 1024:1536]
            TM0 = QXF[:, 1536:2048]
            SIT = QTOK.ap().bitcast(F32)
            AT = AMH
            XG = BUFH[:, 0:G * 512]
            VC = BUFH[:, G * 512:2 * G * 512]
            U = BUFA

            GPC.wait(V)
            GPC.wait(S)
            GPC.emit(lambda e: e.iota(RRB[:, :], [[1, 512]],
                                      channel_multiplier=0,
                                      allow_small_or_imprecise_dtypes=True),
                     inc=1)
            S.wait(GPC)

            cp(V, SUM[:, 32:48], ZRO[:, 0:16])
            for c in range(nch):
                ct0 = c * ch
                for lt in range(ch // 128):
                    tok0 = ct0 + lt * 128
                    transpose_fm_to_tok(tok0, SP4)
                    V.wait(S)
                    scale, bias = ln_tile(SP4[:, :], lt % 4)
                    S.wait(V)
                    act(S, SQS, SP4[:, :], AF.Identity, bias=bias, scale=scale)
                    PE.wait(S)
                    for g in range(G):
                        sl = g % 4
                        if sl == 0:
                            PE.wait(S)
                            PE.wait(V)
                        tr(PTR[:, sl * 128:sl * 128 + 128],
                           SQS[:, g * 128:(g + 1) * 128], IDF[:, :])
                        if sl == 3:
                            q0 = (g // 4) * 512
                            S.wait(PE)
                            act(S, SP4[:, q0:q0 + 512], PTR[:, :512],
                                AF.Identity)
                            V.wait(S)
                            for gg in range(g - 3, g + 1):
                                pseg = SP4[:, q0 + (gg % 4) * 128:
                                            q0 + (gg % 4) * 128 + 128]
                                act(S, XG[:, gg * ch + lt * 128:
                                          gg * ch + lt * 128 + 128],
                                    pseg, AF.Gelu_apprx_tanh)
                                xsl = X[:, gg * t + tok0:gg * t + tok0 + 128]
                                tt_(V, xsl, xsl, pseg, OP.add)
                for wname, bcol, isU in (("Wf1", 2, True), ("Wbx", 4, False)):
                    wf_pe = []
                    for k in range(G):
                        wq_q = SY if k % 2 == 0 else GP
                        if k < 2:
                            wq_q.wait(PE)
                        else:
                            wq_q.wait(PE, wf_pe[k - 2])
                        dma(wq_q, WS[:, (k % 2) * 1024:(k % 2) * 1024 + 1024],
                            W[wname].ap()[k * 128:(k + 1) * 128, :])
                        PE.wait(wq_q)
                        if k == 0:
                            PE.wait(S)
                            PE.wait(V)
                        for m in range(G):
                            mm(PB[m][:, :ch],
                               WS[:, (k % 2) * 1024 + m * 128:
                                  (k % 2) * 1024 + m * 128 + 128],
                               XG[:, k * ch:(k + 1) * ch], k == 0, k == G - 1,
                               inc=(1 if m == G - 1 else 0))
                        wf_pe.append(PE.count)
                    S.wait(PE)
                    for m in range(G):
                        if isU:
                            dst = U[:, m * t + ct0:m * t + ct0 + ch]
                        else:
                            dst = VC[:, m * ch:(m + 1) * ch]
                        act(S, dst, PB[m][:, :ch], AF.Identity,
                            bias=CONS[:, bcol * 8 + m:bcol * 8 + m + 1])
                for wname, bcol in (("Wgx", 5), ("Wga", 6)):
                    wg_pe = []
                    for k in range(G):
                        wq_q = SY if k % 2 == 0 else GP
                        if k < 2:
                            wq_q.wait(PE)
                        else:
                            wq_q.wait(PE, wg_pe[k - 2])
                        dma(wq_q, WS[:, (k % 2) * 1024:(k % 2) * 1024 + 1024],
                            W[wname].ap()[k * 128:(k + 1) * 128, :])
                        PE.wait(wq_q)
                        if k == 0:
                            PE.wait(S)
                            PE.wait(V)
                        for gg in range(G):
                            mm(PB[gg][:, :ch],
                               WS[:, (k % 2) * 1024 + gg * 128:
                                  (k % 2) * 1024 + gg * 128 + 128],
                               VC[:, k * ch:(k + 1) * ch], k == 0, k == G - 1,
                               inc=(1 if gg == G - 1 else 0))
                        wg_pe.append(PE.count)
                    S.wait(PE)
                    if wname == "Wgx":
                        for gg in range(G):
                            act(S, XG[:, gg * ch:gg * ch + ch], PB[gg][:, :ch],
                                AF.Sigmoid,
                                bias=CONS[:, bcol * 8 + gg:bcol * 8 + gg + 1])
                for g in range(G):
                    act(S, SP4[:, 512:512 + ch], PB[g][:, :ch], AF.Sigmoid,
                        bias=CONS[:, 6 * 8 + g:6 * 8 + g + 1])
                    V.wait(S)
                    tsc(V, LGA[:, :ch], SP4[:, 512:512 + ch], M8SP[:, g:g + 1])
                    S.wait(V)
                    act(S, AT[:, :ch], LGA[:, :ch], AF.Exp)
                    act(S, WEX[:, :ch], LGA[:, :ch], AF.Exp, scale=2.0)
                    GP.wait(S)
                    dma(GP, a_s.ap()[g, :, ct0:ct0 + ch], AT[:, :ch])
                    V.wait(S)
                    tsc(V, WEX[:, :ch], WEX[:, :ch], -1.0, 1.0, OP.mult, OP.add)
                    tsc(V, WEX[:, :ch], WEX[:, :ch], 0.0, None, OP.max)
                    S.wait(V)
                    act(S, TM0[:, :ch], WEX[:, :ch], AF.Sqrt)
                    V.wait(S)
                    tt_(V, SIT[:, :ch], XG[:, g * ch:g * ch + ch],
                        VC[:, g * ch:g * ch + ch], OP.mult)
                    tt_(V, SIT[:, :ch], SIT[:, :ch], TM0[:, :ch], OP.mult)
                    GP.wait(V)
                    dma(GP, si_s.ap()[g, :, ct0:ct0 + ch], SIT[:, :ch])
                    scan(V, PEX[:, :ch], ONECOL.broadcast_to([128, ch]),
                         LGA[:, :ch], 0.0)
                    tt_(V, PEX[:, :ch], PEX[:, :ch], LGA[:, :ch], OP.subtract)
                    S.wait(V)
                    act(S, WEX[:, :ch], PEX[:, :ch], AF.Exp,
                        bias=SUM[:, 32 + g:33 + g])
                    V.wait(S)
                    tt_(V, TM0[:, :ch], WEX[:, :ch], SIT[:, :ch], OP.mult)
                    red(V, STATS[:, 0:1], TM0[:, :ch], OP.add)
                    tt_(V, SUM[:, 40 + g:41 + g], SUM[:, 40 + g:41 + g],
                        STATS[:, 0:1], OP.add)
                    red(V, STATS[:, 1:2], LGA[:, :ch], OP.add)
                    tt_(V, SUM[:, 32 + g:33 + g], SUM[:, 32 + g:33 + g],
                        STATS[:, 1:2], OP.add)
            for g in range(G):
                for h2 in range(nh):
                    tsc(V, STATS[:, 2:3], LNAF[:, g:g + 1],
                        float(t - 1 - hw_ * h2))
                    tsc(V, STATS[:, 3:4], LNAF[:, g:g + 1], -1.0)
                    S.wait(V)
                    act(S, WEX[:, :hw_], RRB[:, :hw_], AF.Exp,
                        bias=STATS[:, 2:3], scale=STATS[:, 3:4])
                    V.wait(S)
                    tt_(V, TM0[:, :hw_],
                        U[:, g * t + h2 * hw_:g * t + h2 * hw_ + hw_],
                        WEX[:, :hw_], OP.mult)
                    red(V, STATS[:, 4 + h2:5 + h2], TM0[:, :hw_], OP.add)
                for h2 in range(1, nh):
                    tt_(V, STATS[:, 4:5], STATS[:, 4:5],
                        STATS[:, 4 + h2:5 + h2], OP.add)
                cp(V, SUM[:, 48 + g:49 + g], STATS[:, 4:5])
            S.wait(V)
            act(S, SUM[:, 56:64], SUM[:, 32:40], AF.Exp)
            V.wait(S)
            cp(V, EXC[:, 0:8], SUM[:, 48:56])
            cp(V, EXC[:, 8:16], SUM[:, 40:48])
            cp(V, EXC[:, 16:24], SUM[:, 56:64])
            GP.wait(V)
            dma(GP, cc_i.ap(), EXC[:, 0:24])
            GP.ops.append(lambda e, s=GP.sem, v=GP.count: e.wait_ge(s, v))

            def _cc(e):
                e.collective_compute(
                    "AllGather", OP.bypass,
                    replica_groups=[[0, 1, 2, 3], [4, 5, 6, 7]],
                    ins=[cc_i.ap().opt()], outs=[cc_o.ap().opt()]
                ).then_inc(CCSEM, 1)
            GP.ops.append(_cc)
            GP.ops.append(lambda e: e.wait_ge(CCSEM, 2))
            dma(GP, EXC[:, 0:96].rearrange("p (q c) -> p q c", q=4),
                cc_o.ap().rearrange("q p c -> p q c"))
            V.wait(GP)
            cp(V, STATS[:, 0:8], ZRO[:, 0:8])
            cp(V, STATS[:, 8:16], ZRO[:, 0:8])
            cp(V, STATS[:, 16:24], ZRO[:, 0:8])
            for qq in range(4):
                stt(V, STATS[:, 8:16], STATS[:, 0:8], SCL[:, 2 + qq:3 + qq],
                    STATS[:, 8:16], OP.mult, OP.add)
                tt_(V, STATS[:, 0:8], STATS[:, 0:8], ACORE, OP.mult)
                tt_(V, STATS[:, 0:8], STATS[:, 0:8],
                    EXC[:, qq * 24:qq * 24 + 8], OP.add)
            cp(V, STATS[:, 0:8], ZRO[:, 0:8])
            for qq in range(3, -1, -1):
                stt(V, STATS[:, 16:24], STATS[:, 0:8], SCL[:, 2 + qq:3 + qq],
                    STATS[:, 16:24], OP.mult, OP.add)
                tt_(V, STATS[:, 0:8], STATS[:, 0:8],
                    EXC[:, qq * 24 + 16:qq * 24 + 24], OP.mult)
                tt_(V, STATS[:, 0:8], STATS[:, 0:8],
                    EXC[:, qq * 24 + 8:qq * 24 + 16], OP.add)
            for g in range(G):
                scan(V, U[:, g * t:(g + 1) * t],
                     AFWD[:, g:g + 1].broadcast_to([128, t]),
                     U[:, g * t:(g + 1) * t], STATS[:, 8 + g:9 + g])
            for it in range(ntt):
                w2_pe = []
                for k in range(G):
                    wq_q = SY if k % 2 == 0 else GP
                    if k < 2:
                        wq_q.wait(PE)
                    else:
                        wq_q.wait(PE, w2_pe[k - 2])
                    dma(wq_q, WS[:, (k % 2) * 1024:(k % 2) * 1024 + 1024],
                        W["Wf2"].ap()[k * 128:(k + 1) * 128, :])
                    PE.wait(wq_q)
                    if k == 0:
                        PE.wait(V)
                    for m in range(G):
                        mm(PB[m][:, :tt],
                           WS[:, (k % 2) * 1024 + m * 128:
                              (k % 2) * 1024 + m * 128 + 128],
                           U[:, k * t + it * tt:k * t + it * tt + tt],
                           k == 0, k == G - 1,
                           inc=(1 if m == G - 1 else 0))
                    w2_pe.append(PE.count)
                V.wait(PE)
                for m in range(G):
                    xsl = X[:, m * t + it * tt:m * t + it * tt + tt]
                    stt(V, xsl, PB[m][:, :tt], CONS[:, 3 * 8 + m:3 * 8 + m + 1],
                        xsl, OP.add, OP.add)
            sc_v = []
            SY.wait(GP)
            for g in range(G):
                b0 = (g % 2) * 2 * t
                if g < 2:
                    SY.wait(V)
                else:
                    SY.wait(V, sc_v[g - 2])
                dma(SY, BUFH[:, b0:b0 + t], si_s.ap()[g, :, :])
                dma(SY, BUFH[:, b0 + t:b0 + 2 * t], a_s.ap()[g, :, :])
                V.wait(SY)
                scan(V, U[:, g * t:(g + 1) * t],
                     BUFH[:, b0 + 2 * t - 1:b0 + t - 1:-1],
                     (BUFH[:, t - 1::-1] if b0 == 0 else
                      BUFH[:, b0 + t - 1:b0 - 1:-1]),
                     STATS[:, 16 + g:17 + g])
                sc_v.append(V.count)
            for it in range(ntt):
                wb_pe = []
                for k in range(G):
                    wq_q = SY if k % 2 == 0 else GP
                    if k < 2:
                        wq_q.wait(PE)
                    else:
                        wq_q.wait(PE, wb_pe[k - 2])
                    dma(wq_q, WS[:, (k % 2) * 1024:(k % 2) * 1024 + 1024],
                        W["Wb2"].ap()[k * 128:(k + 1) * 128, :])
                    PE.wait(wq_q)
                    if k == 0:
                        PE.wait(V)
                    for m in range(G):
                        mm(PB[m][:, :tt],
                           WS[:, (k % 2) * 1024 + m * 128:
                              (k % 2) * 1024 + m * 128 + 128],
                           U[:, k * t + (t - (it + 1) * tt):
                              k * t + (t - it * tt)],
                           k == 0, k == G - 1,
                           inc=(1 if m == G - 1 else 0))
                    wb_pe.append(PE.count)
                V.wait(PE)
                for m in range(G):
                    xsl = X[:, m * t + it * tt:m * t + it * tt + tt]
                    stt(V, xsl, PB[m][:, tt - 1::-1],
                        CONS[:, 7 * 8 + m:7 * 8 + m + 1], xsl, OP.add, OP.add)

        def output():
            for lt in range(t // 128):
                tok0 = lt * 128
                transpose_fm_to_tok(tok0, SP4)
                V.wait(S)
                cp(V, BUFA[:, lt * 1024:(lt + 1) * 1024], SP4[:, :])
            SY.wait(V)
            dma(SY, y_out.ap().rearrange("(n p) c -> p n c", p=128),
                BUFA[:, 0:(t // 128) * 1024].rearrange("p (n c) -> p n c", c=D))
            SY.ops.append(lambda e, s=SY.sem, v=SY.count: e.wait_ge(s, v))

        if debug_out is not None:
            debug_out.update(locals())

        build_x()
        ffn(1)
        attn()
        ffn(3)
        output()

        with nc.Block() as block:
            block.sync(lambda e: [op(e) for op in SY.ops])
            block.gpsimd(lambda e: [op(e) for op in GP.ops])
            block.vector(lambda e: [op(e) for op in V.ops])
            block.scalar(lambda e: [op(e) for op in S.ops])
            block.tensor(lambda e: [op(e) for op in PE.ops])

    return nc


_PROG_CACHE = {}


def _get_prog(t):
    if t not in _PROG_CACHE:
        _PROG_CACHE[t] = build_program(t)
    return _PROG_CACHE[t]


def make_inmaps(x_full, wd, t=2048, ncore=8):
    vec_order = ("a_fwd", "a_param", "bf1", "bf2", "bbx", "bgx", "bga", "bb2")
    vecs8 = np.concatenate([np.asarray(wd[k], np.float32).reshape(8, 128)
                            for k in vec_order], axis=0)
    scal2 = np.stack([np.asarray(wd["s_in"], np.float32).reshape(1),
                      np.asarray(wd["s_in_down"], np.float32).reshape(1)])
    base = {
        "Wf1": np.asarray(wd["Wf1"], np.float32),
        "Wf2": np.asarray(wd["Wf2"], np.float32),
        "Wbx": np.asarray(wd["Wbx"], np.float32),
        "Wgx": np.asarray(wd["Wgx"], np.float32),
        "Wga": np.asarray(wd["Wga"], np.float32),
        "Wb2": np.asarray(wd["Wb2"], np.float32),
        "gate_w": np.asarray(wd["gate_w"], np.float32),
        "up_w": np.asarray(wd["up_w"], np.float32),
        "down_w": np.asarray(wd["down_w"], np.float32),
        "vecs8": vecs8, "scal2": scal2,
    }
    gw = np.asarray(wd["gate_w"], np.float32)
    uw = np.asarray(wd["up_w"], np.float32)
    dw = np.asarray(wd["down_w"], np.float32)
    p128 = 16 * np.arange(128)

    def tile_rows(w, it):
        return w[2048 * (it // 16) + (it % 16) + p128]

    nq = ncore // x_full.shape[0]
    in_maps = []
    for c in range(ncore):
        b, q = divmod(c, nq)
        oh = np.zeros((128, 4), np.float32)
        oh[:, q % 4] = 1.0
        m = dict(base)
        m["x"] = np.ascontiguousarray(x_full[b, q * t:(q + 1) * t], dtype=np.float32)
        m["onehot"] = oh
        m["gate_sh"] = np.stack([tile_rows(gw, 4 * c + j) for j in range(4)])
        m["up_sh"] = np.stack([tile_rows(uw, 4 * c + j) for j in range(4)])
        m["down_sh"] = np.ascontiguousarray(dw[c * 128:(c + 1) * 128, :])
        in_maps.append(m)
    return in_maps


def kernel(L0_attn_x, L0_attn_a_fwd, L0_attn_Wf1, L0_attn_bf1, L0_attn_Wf2,
           L0_attn_bf2, L0_attn_Wbx, L0_attn_bbx, L0_attn_Wgx, L0_attn_bgx,
           L0_attn_Wga, L0_attn_bga, L0_attn_a_param, L0_attn_Wb2, L0_attn_bb2,
           L0_ffn_gate_w, L0_ffn_up_w, L0_ffn_down_w, L0_ffn_s_in,
           L0_ffn_s_in_down):
    x = np.asarray(L0_attn_x, np.float32)
    wd = {
        "a_fwd": L0_attn_a_fwd, "a_param": L0_attn_a_param,
        "bf1": L0_attn_bf1, "bf2": L0_attn_bf2, "bbx": L0_attn_bbx,
        "bgx": L0_attn_bgx, "bga": L0_attn_bga, "bb2": L0_attn_bb2,
        "Wf1": L0_attn_Wf1, "Wf2": L0_attn_Wf2, "Wbx": L0_attn_Wbx,
        "Wgx": L0_attn_Wgx, "Wga": L0_attn_Wga, "Wb2": L0_attn_Wb2,
        "gate_w": L0_ffn_gate_w, "up_w": L0_ffn_up_w, "down_w": L0_ffn_down_w,
        "s_in": L0_ffn_s_in, "s_in_down": L0_ffn_s_in_down,
    }
    t = 2048
    nc = _get_prog(t)
    in_maps = make_inmaps(x, wd, t=t, ncore=8)
    res = run_bass_kernel_spmd(nc, in_maps, core_ids=list(range(8)))
    y = np.empty((2, 8192, 1024), np.float32)
    for c in range(8):
        b, q = divmod(c, 4)
        y[b, q * t:(q + 1) * t] = np.asarray(res.results[c]["y"], np.float32)
    return y
